# version 01
# baseline (speedup 1.0000x reference)
import sys
if "/opt/trn_rl_repo" not in sys.path:
    sys.path.insert(0, "/opt/trn_rl_repo")
import numpy as np
import ml_dtypes
import concourse.bass as bass
import concourse.mybir as mybir
from concourse.bass_utils import run_bass_kernel_spmd
from concourse.aot_env import allow_ici_all_to_all

F32 = mybir.dt.float32
BF16 = mybir.dt.bfloat16
I8 = mybir.dt.int8
AF = mybir.ActivationFunctionType
OP = mybir.AluOpType

NCORE = 8
C = 2048
TOK = 1024
T = 8192
INTER = 8192
H_PER = 2
DH = 128
NFC = C // 128
EPS_LN = 1e-5
EPS_NRM = 1e-8

PHASES = 6
DEBUG_TAP = None
SERIAL_B = False
SERIAL_C = 0


class Plan:

    ENGS = ("tensor", "vector", "scalar", "sync", "gpsimd")

    def __init__(self):
        self.ops = {e: [] for e in self.ENGS}
        self.cnt = {}
        self.sem_names = []

    def sem(self, name):
        if name not in self.cnt:
            self.cnt[name] = 0
            self.sem_names.append(name)
        return name

    def c(self, sem):
        return self.cnt.get(sem, 0)

    def add(self, eng, fn, waits=(), inc=None, inc_n=1):
        for s, v in waits:
            assert v <= self.cnt.get(s, 0), (s, v, self.cnt.get(s, 0))
        if inc is not None:
            self.sem(inc)
            self.cnt[inc] += inc_n
        self.ops[eng].append((fn, tuple(waits), inc, inc_n))
        return self.cnt.get(inc, 0) if inc is not None else None

    def barrier(self):
        snap = dict(self.cnt)
        eng_sem = {"tensor": "T", "vector": "V", "scalar": "S",
                   "sync": "Dq", "gpsimd": "Dg"}
        for e in self.ENGS:
            waits = tuple((s, v) for s, v in snap.items()
                          if v > 0 and s != eng_sem.get(e))
            self.ops[e].append((None, waits, None, 0))

    def emit(self, block, sems):
        last_wait = {e: {} for e in self.ENGS}
        stats = {e: [0, 0] for e in self.ENGS}

        def emit_for(eng_name):
            def go(eng):
                lw = last_wait[eng_name]
                for fn, waits, inc, inc_n in self.ops[eng_name]:
                    for s, v in waits:
                        if lw.get(s, -1) >= v:
                            continue
                        eng.wait_ge(sems[s], v)
                        lw[s] = v
                        stats[eng_name][1] += 1
                    if fn is None:
                        continue
                    instr = fn(eng)
                    stats[eng_name][0] += 1
                    if inc is not None:
                        instr.then_inc(sems[inc], inc_n)
            return go

        for e in self.ENGS:
            getattr(block, e)(emit_for(e))
        return stats


def build_nc():
    nc = bass.Bass("TRN2", target_bir_lowering=False, debug=False)
    p = Plan()
    T_, V_, S_, Dq, Dg, CC = (p.sem(s) for s in ("T", "V", "S", "Dq", "Dg", "CC"))

    x_in = nc.dram_tensor("x", [128, NFC * TOK], F32, kind="ExternalInput").ap()
    wq_in = nc.dram_tensor("wq", [128, NFC, H_PER * DH], F32, kind="ExternalInput").ap()
    wk_in = nc.dram_tensor("wk", [128, NFC, H_PER * DH], F32, kind="ExternalInput").ap()
    wv_in = nc.dram_tensor("wv", [128, NFC, H_PER * DH], F32, kind="ExternalInput").ap()
    tau_in = nc.dram_tensor("tau", [1, 1], F32, kind="ExternalInput").ap()
    nu_in = nc.dram_tensor("nu", [1, 1], F32, kind="ExternalInput").ap()
    wg_in = nc.dram_tensor("wg", [16, 128, NFC * 512], I8, kind="ExternalInput").ap()
    wu_in = nc.dram_tensor("wu", [16, 128, NFC * 512], I8, kind="ExternalInput").ap()
    wd_in = nc.dram_tensor("wd", [8, 16, 128, 8 * 128], I8, kind="ExternalInput").ap()
    sg_in = nc.dram_tensor("sg", [128, 64], F32, kind="ExternalInput").ap()
    su_in = nc.dram_tensor("su", [128, 64], F32, kind="ExternalInput").ap()
    sd_in = nc.dram_tensor("sd", [128, 16], F32, kind="ExternalInput").ap()
    mask_in = nc.dram_tensor("mask", [128, 4 * 512], BF16, kind="ExternalInput").ap()
    eye_in = nc.dram_tensor("eye", [128, 128], BF16, kind="ExternalInput").ap()
    y_out = nc.dram_tensor("y", [128, NFC * TOK], F32, kind="ExternalOutput").ap()

    xres = nc.dram_tensor("xres", [128, NFC * TOK], F32).ap()
    ag_in = nc.dram_tensor("ag_in", [2 * 128, NFC * TOK], BF16)
    ag_out = nc.dram_tensor("ag_out", [NCORE * 2 * 128, NFC * TOK], BF16)
    a2a_in = nc.dram_tensor("a2a_in", [C, TOK], F32)
    a2a_out = nc.dram_tensor("a2a_out", [C, TOK], F32)
    dbg_dram = nc.dram_tensor("dbg_dram", [C, TOK], F32)

    with (
        nc.sbuf_tensor("AR1", [128, 16384], F32) as AR1,
        nc.sbuf_tensor("AR2", [128, 16384], F32) as AR2,
        nc.sbuf_tensor("AR3", [128, 16384], F32) as AR3,
        nc.sbuf_tensor("AR4", [128, 2816], F32) as AR4,
        nc.psum_tensor("PS", [128, 4096], F32) as PS,
        nc.semaphore("sT") as sT, nc.semaphore("sV") as sV,
        nc.semaphore("sS") as sS, nc.semaphore("sDq") as sDq,
        nc.semaphore("sDg") as sDg, nc.semaphore("sCC") as sCC,
        nc.Block() as block,
    ):
        sems = {"T": sT, "V": sV, "S": sS, "Dq": sDq, "Dg": sDg, "CC": sCC}

        def bank(i, width=512, off=0):
            return PS[:, i * 512 + off: i * 512 + off + width]

        def bank_row(i, width=512):
            return PS[0:1, i * 512: i * 512 + width]

        xa = AR1
        wgb = [AR1[:, 0:4096].bitcast(BF16), AR1[:, 4096:8192].bitcast(BF16)]
        wub = [AR1[:, 8192:12288].bitcast(BF16), AR1[:, 12288:16384].bitcast(BF16)]
        qh_fm = AR1[:, 0:4096].bitcast(BF16)
        ql_fm = AR1[:, 4096:8192].bitcast(BF16)
        kh_fm = AR1[:, 8192:12288].bitcast(BF16)
        kl_fm = AR1[:, 12288:16384].bitcast(BF16)
        yh = AR2[:, 0:8192].bitcast(BF16)
        yl = AR2[:, 8192:16384].bitcast(BF16)
        vh_tm = AR2[:, 0:4096].bitcast(BF16)
        vl_tm = AR2[:, 4096:8192].bitcast(BF16)
        ysb = [(AR2[:, 8192:10240].bitcast(BF16), AR2[:, 10240:12288].bitcast(BF16)),
               (AR2[:, 12288:14336].bitcast(BF16), AR2[:, 14336:16384].bitcast(BF16))]
        x2t = [AR3[:, 0:1024], AR3[:, 1024:2048]]
        bca = AR3[:, 2048:3072]
        bcb = AR3[:, 3072:4096]
        lst = [AR3[:, 4096:4608], AR3[:, 4608:5120]]
        lsy = [AR3[:, 5120:5632], AR3[:, 5632:6144]]
        fh = AR3[:, 0:4096].bitcast(BF16)
        fl = AR3[:, 4096:8192].bitcast(BF16)
        wdb = [AR3[:, 8192:8704].bitcast(BF16), AR3[:, 8704:9216].bitcast(BF16)]
        gb = [AR3[:, 9216:9728], AR3[:, 9728:10240]]
        ub = [AR3[:, 10240:10752], AR3[:, 10752:11264]]
        fb = [AR3[:, 11264:11776], AR3[:, 11776:12288]]
        dsb = [AR3[:, 12288:12800], AR3[:, 12800:13312]]
        w32 = AR3[:, 0:2048]
        wsh = AR3[:, 2048:5120].bitcast(BF16)
        wsl = AR3[:, 5120:8192].bitcast(BF16)
        psq = AR3[:, 8192:8576]
        nrmt = AR3[:, 8576:8608]
        p32 = [AR3[:, 8608:9120], AR3[:, 9120:9632]]
        phb = [AR3[:, 9632:9888].bitcast(BF16), AR3[:, 9888:10144].bitcast(BF16)]
        plb = [AR3[:, 10144:10400].bitcast(BF16), AR3[:, 10400:10656].bitcast(BF16)]
        osb = [AR3[:, 10656:11168], AR3[:, 11168:11680]]
        bc_sb = AR3[:, 11680:12192]
        ttt = [AR3[:, 12192 + 64 * i: 12256 + 64 * i].bitcast(BF16) for i in range(4)]
        masks = AR4[:, 0:1024].bitcast(BF16)
        eye = AR4[:, 1024:1088].bitcast(BF16)
        ones_bf = AR4[:, 1088:1089].bitcast(BF16)[:, 0:1]
        ones_f = AR4[:, 1090:1091]
        ones_row = AR4[0:1, 1092:1220]
        sg_sb = AR4[:, 1220:1284]
        su_sb = AR4[:, 1284:1348]
        sdh_sb = AR4[:, 1348:1364]
        sd_sb = AR4[:, 1364:1380]
        tau_bc = AR4[:, 1380:1381]
        nu_sb = AR4[0:1, 1382:1383]
        tau_sb = AR4[0:1, 1384:1385]
        r_s1 = AR3[0:1, 6144:7168]
        r_s2 = AR3[0:1, 7168:8192]
        r_t0 = AR3[0:1, 8192:9216]
        r_a = AR3[0:1, 9216:10240]
        r_b = AR3[0:1, 10240:11264]
        r_m = r_s1
        r_dn = AR3[0:1, 12448:12960]
        r_rc = AR3[0:1, 12960:13472]

        nmm = [0]

        def mm(out, lhsT, rhs, start, stop, waits=(), inc=None):
            def f(e, out=out, lhsT=lhsT, rhs=rhs, start=start, stop=stop):
                return e.matmul(out, lhsT, rhs, start=start, stop=stop)
            nmm[0] += 1
            return p.add("tensor", f, waits=waits, inc=inc)

        def init():
            p.add("sync", lambda e: e.dma_start(out=xa[:, :], in_=x_in[:, :]), inc=Dq, inc_n=16)
            p.add("sync", lambda e: e.dma_start(out=masks[:, :], in_=mask_in[:, :]), inc=Dq, inc_n=16)
            p.add("sync", lambda e: e.dma_start(out=eye[:, :], in_=eye_in[:, :]), inc=Dq, inc_n=16)
            p.add("sync", lambda e: e.dma_start(out=sg_sb[:, :], in_=sg_in[:, :]), inc=Dq, inc_n=16)
            p.add("sync", lambda e: e.dma_start(out=su_sb[:, :], in_=su_in[:, :]), inc=Dq, inc_n=16)
            p.add("sync", lambda e: e.dma_start(out=sd_sb[:, :], in_=sd_in[:, :]), inc=Dq, inc_n=16)
            p.add("sync", lambda e: e.dma_start(out=nu_sb[:, :], in_=nu_in[:, :]), inc=Dq, inc_n=16)
            p.add("sync", lambda e: e.dma_start(out=tau_sb[:, :], in_=tau_in[:, :]), inc=Dq, inc_n=16)
            dq0 = p.c(Dq)
            p.add("vector", lambda e: e.memset(ones_bf[:, :], 1.0), inc=V_)
            p.add("vector", lambda e: e.memset(ones_f[:, :], 1.0), inc=V_)
            p.add("vector", lambda e: e.memset(ones_row[:, :], 1.0), inc=V_)
            p.add("vector", lambda e: e.tensor_scalar_mul(sdh_sb[:, :], sd_sb[:, :], 0.5),
                  waits=[(Dq, dq0)], inc=V_)
            mm(bank(7, 1), ones_row[:, :], tau_sb[:, :], True, True,
               waits=[(V_, p.c(V_)), (Dq, dq0)], inc=T_)
            p.add("scalar", lambda e: e.copy(tau_bc[:, :], bank(7, 1)),
                  waits=[(T_, p.c(T_))], inc=S_)
            p.add("gpsimd", lambda e: e.dma_start(out=xres[:, :], in_=xa[:, :]),
                  waits=[(Dq, dq0)], inc=Dg, inc_n=16)

        def layer_norm(load_first=False):
            if load_first:
                p.add("sync", lambda e: e.dma_start(out=xa[:, :], in_=xres[:, :]),
                      inc=Dq, inc_n=16)
                dq_x = p.c(Dq)
                p.ops["tensor"].append((None, ((Dq, dq_x),), None, 0))
                p.ops["scalar"].append((None, ((Dq, dq_x),), None, 0))
                p.ops["vector"].append((None, ((Dq, dq_x),), None, 0))
            t_sq = {}
            for fc in range(NFC):
                xr = xa[:, fc * 1024:(fc + 1) * 1024]
                b = fc % 2
                w = [(T_, t_sq[fc - 2])] if fc >= 2 else []
                t_sq[fc] = None
                s_done = p.add("scalar", (lambda e, o=x2t[b], i=xr: e.square(o, i)),
                               waits=w, inc=S_)
                for tt in range(2):
                    mm(bank_row(0 + tt), ones_f[:, :], xr[:, tt * 512:(tt + 1) * 512],
                       fc == 0, fc == NFC - 1)
                    t_sq[fc] = mm(bank_row(2 + tt), ones_f[:, :],
                                  x2t[b][:, tt * 512:(tt + 1) * 512],
                                  fc == 0, fc == NFC - 1,
                                  waits=[(S_, s_done)], inc=T_ if tt == 1 else None)
            t_stats = p.c(T_)
            p.add("vector", lambda e: e.tensor_copy(r_s1[:, 0:512], bank_row(0)), waits=[(T_, t_stats)], inc=V_)
            p.add("vector", lambda e: e.tensor_copy(r_s1[:, 512:1024], bank_row(1)), inc=V_)
            p.add("vector", lambda e: e.tensor_copy(r_s2[:, 0:512], bank_row(2)), inc=V_)
            p.add("vector", lambda e: e.tensor_copy(r_s2[:, 512:1024], bank_row(3)), inc=V_)
            p.add("vector", lambda e: e.tensor_scalar_mul(r_m[:, :], r_s1[:, :], 1.0 / C), inc=V_)
            p.add("vector", lambda e: e.tensor_tensor(r_t0[:, :], r_m[:, :], r_m[:, :], op=OP.mult), inc=V_)
            p.add("vector", lambda e: e.tensor_scalar(r_s2[:, :], r_s2[:, :], 1.0 / C, EPS_LN,
                                                      op0=OP.mult, op1=OP.add), inc=V_)
            v_pre = p.add("vector", lambda e: e.tensor_tensor(r_t0[:, :], r_s2[:, :], r_t0[:, :],
                                                              op=OP.subtract), inc=V_)
            s_sqrt = p.add("scalar", lambda e: e.sqrt(r_t0[:, :], r_t0[:, :]),
                           waits=[(V_, v_pre)], inc=S_)
            p.add("vector", lambda e: e.reciprocal(r_a[:, :], r_t0[:, :]),
                  waits=[(S_, s_sqrt)], inc=V_)
            p.add("vector", lambda e: e.tensor_tensor(r_b[:, :], r_m[:, :], r_a[:, :], op=OP.mult), inc=V_)
            v_rows = p.add("vector", lambda e: e.tensor_scalar_mul(r_b[:, :], r_b[:, :], -1.0), inc=V_)
            for tt in range(2):
                mm(bank(4 + 2 * tt), ones_row[:, :], r_a[:, tt * 512:(tt + 1) * 512], True, True,
                   waits=[(V_, v_rows)])
                t_bc = mm(bank(5 + 2 * tt), ones_row[:, :], r_b[:, tt * 512:(tt + 1) * 512],
                          True, True, inc=T_)
                p.add("scalar", (lambda e, tt=tt: e.copy(bca[:, tt * 512:(tt + 1) * 512], bank(4 + 2 * tt))),
                      waits=[(T_, t_bc)], inc=S_)
                p.add("scalar", (lambda e, tt=tt: e.copy(bcb[:, tt * 512:(tt + 1) * 512], bank(5 + 2 * tt))),
                      inc=S_)
            s_bc = p.c(S_)
            for fc in range(NFC):
                for tt in range(2):
                    xrt = xa[:, fc * 1024 + tt * 512: fc * 1024 + tt * 512 + 512]
                    st = tt
                    yh_t = yh[:, fc * 1024 + tt * 512: fc * 1024 + tt * 512 + 512]
                    yl_t = yl[:, fc * 1024 + tt * 512: fc * 1024 + tt * 512 + 512]
                    p.add("vector", (lambda e, o=lst[st], i=xrt, b2=bca[:, tt * 512:(tt + 1) * 512]:
                                     e.tensor_tensor(o, i, b2, op=OP.mult)),
                          waits=[(S_, s_bc)], inc=V_)
                    v_y = p.add("vector", (lambda e, o=lsy[st], i=lst[st], b2=bcb[:, tt * 512:(tt + 1) * 512]:
                                           e.tensor_tensor(o, i, b2, op=OP.add)), inc=V_)
                    s_h = p.add("scalar", (lambda e, o=yh_t, i=lsy[st]: e.copy(o, i)),
                                waits=[(V_, v_y)], inc=S_)
                    p.add("vector", (lambda e, o=yl_t, i=lsy[st], h=yh_t:
                                     e.tensor_tensor(o, i, h, op=OP.subtract)),
                          waits=[(S_, s_h)], inc=V_)

        def ffn():
            t_gate_last = {}
            t_up_last = {}
            t_down_last = {}
            s_evac = {}
            v_evac = {}
            s_fuse = 0
            v_fuse = 0
            dg_w = {}
            dg_wd = {}
            s_dsb = {}
            dg_dsb = {}
            y_ready = (V_, p.c(V_))

            def load_w(cidx):
                buf = cidx % 2
                w = []
                if cidx >= 2:
                    w = [(T_, t_gate_last.get(buf, 0))]
                p.add("gpsimd", (lambda e, b=buf, ci=cidx:
                                 e.dma_start(out=wgb[b][:, :], in_=wg_in[ci, :, :])),
                      waits=w, inc=Dg, inc_n=16)
                if cidx >= 2:
                    w = [(T_, t_up_last.get(buf, 0))]
                dg_w[cidx] = p.add("gpsimd", (lambda e, b=buf, ci=cidx:
                                              e.dma_start(out=wub[b][:, :], in_=wu_in[ci, :, :])),
                                   waits=w, inc=Dg, inc_n=16)

            load_w(0)
            load_w(1)
            for blk in range(8):
                for ic in range(2):
                    cidx = blk * 2 + ic
                    buf = cidx % 2
                    wgv = wgb[buf].rearrange("p (f i) -> p f i", f=NFC)
                    wuv = wub[buf].rearrange("p (f i) -> p f i", f=NFC)
                    for sub in range(4):
                        ig = cidx * 4 + sub
                        sl = ic * 4 + sub
                        for tt in range(2):
                            rot = (sub * 2 + tt) % 2
                            gbank, ubank = 0 + rot, 2 + rot
                            w = [(Dg, dg_w[cidx]), y_ready,
                                 (S_, s_evac.get(gbank, 0))]
                            for pi, ysl in enumerate((yh, yl)):
                                for fc in range(NFC):
                                    t_g = mm(bank(gbank),
                                             wgv[:, fc, sub * 128:(sub + 1) * 128],
                                             ysl[:, fc * 1024 + tt * 512:][:, 0:512],
                                             pi == 0 and fc == 0, pi == 1 and fc == NFC - 1,
                                             waits=w if (pi == 0 and fc == 0) else (),
                                             inc=T_ if (pi == 1 and fc == NFC - 1) else None)
                            t_gate_last[buf] = t_g
                            w = [(V_, v_evac.get(ubank, 0))]
                            for pi, ysl in enumerate((yh, yl)):
                                for fc in range(NFC):
                                    t_u = mm(bank(ubank),
                                             wuv[:, fc, sub * 128:(sub + 1) * 128],
                                             ysl[:, fc * 1024 + tt * 512:][:, 0:512],
                                             pi == 0 and fc == 0, pi == 1 and fc == NFC - 1,
                                             waits=w if (pi == 0 and fc == 0) else (),
                                             inc=T_ if (pi == 1 and fc == NFC - 1) else None)
                            t_up_last[buf] = t_u
                            st = rot
                            s_g = p.add("scalar",
                                        (lambda e, o=gb[st], b2=gbank, ig=ig:
                                         e.activation(o, bank(b2), AF.Silu, scale=sg_sb[:, ig:ig + 1])),
                                        waits=[(T_, t_g)], inc=S_)
                            s_evac[gbank] = s_g
                            v_u = p.add("vector",
                                        (lambda e, o=ub[st], b2=ubank, ig=ig:
                                         e.tensor_scalar_mul(o, bank(b2), su_sb[:, ig:ig + 1])),
                                        waits=[(T_, t_u)], inc=V_)
                            v_evac[ubank] = v_u
                            v_f = p.add("vector",
                                        (lambda e, o=fb[st], a=gb[st], b2=ub[st]:
                                         e.tensor_tensor(o, a, b2, op=OP.mult)),
                                        waits=[(S_, s_g)], inc=V_)
                            fh_t = fh[:, sl * 1024 + tt * 512:][:, 0:512]
                            fl_t = fl[:, sl * 1024 + tt * 512:][:, 0:512]
                            w_fz = [(V_, v_f)]
                            if blk > 0:
                                w_fz.append((T_, t_blk_down))
                            s_fuse = p.add("scalar",
                                           (lambda e, o=fh_t, i=fb[st]: e.copy(o, i)),
                                           waits=w_fz, inc=S_)
                            v_fuse = p.add("vector",
                                           (lambda e, o=fl_t, i=fb[st], h=fh_t:
                                            e.tensor_tensor(o, i, h, op=OP.subtract)),
                                           waits=[(S_, s_fuse)], inc=V_)
                    if cidx + 2 < 16:
                        load_w(cidx + 2)
                fhv = fh.rearrange("p (s t) -> p s t", s=8)
                flv = fl.rearrange("p (s t) -> p s t", s=8)
                for dmc in range(16):
                    wbuf = dmc % 2
                    w = [(T_, t_down_last.get(wbuf, 0))]
                    dg_wd[dmc] = p.add("gpsimd",
                                       (lambda e, b=wbuf, blk=blk, dmc=dmc:
                                        e.dma_start(out=wdb[b][:, :], in_=wd_in[blk, dmc, :, :])),
                                       waits=w, inc=Dg, inc_n=16)
                    wdv = wdb[wbuf].rearrange("p (s i) -> p s i", s=8)
                    for tt in range(2):
                        rot = 4 + (dmc * 2 + tt) % 2
                        w = [(Dg, dg_wd[dmc]), (S_, s_fuse), (V_, v_fuse),
                             (S_, s_dsb.get(rot, 0))]
                        first = True
                        for sub in range(8):
                            for pv, fv in ((0, fhv), (1, flv)):
                                t_d = mm(bank(rot),
                                         wdv[:, sub, :],
                                         fv[:, sub, tt * 512:(tt + 1) * 512],
                                         first, sub == 7 and pv == 1,
                                         waits=w if first else (),
                                         inc=T_ if (sub == 7 and pv == 1) else None)
                                first = False
                        t_down_last[wbuf] = t_d
                        t_blk_down = t_d
                        dst = (dmc * 2 + tt) % 2
                        s_d = p.add("scalar",
                                    (lambda e, o=dsb[dst], b2=rot, dmc=dmc:
                                     e.activation(o, bank(b2), AF.Copy, scale=sdh_sb[:, dmc:dmc + 1])),
                                    waits=[(T_, t_d), (Dg, dg_dsb.get(dst, 0))], inc=S_)
                        s_dsb[rot] = s_d
                        dg_dsb[dst] = p.add("gpsimd",
                                            (lambda e, i=dsb[dst], dmc=dmc, tt=tt:
                                             e.dma_start(out=xres[:, dmc * 1024 + tt * 512:][:, 0:512],
                                                         in_=i, accum_op=OP.add)),
                                            waits=[(S_, s_d)], inc=Dg, inc_n=16)
            return p.c(Dg)

        agi = ag_in.ap().rearrange("(s p) t -> s p t", p=128)
        ago = ag_out.ap().rearrange("(r s p) t -> r s p t", s=2, p=128)

        def phase2_ag():
            v_last, s_last = p.c(V_), p.c(S_)
            p.add("sync", lambda e: e.dma_start(out=agi[0], in_=yh[:, :]),
                  waits=[(V_, v_last), (S_, s_last)], inc=Dq, inc_n=16)
            p.add("sync", lambda e: e.dma_start(out=agi[1], in_=yl[:, :]), inc=Dq, inc_n=16)
            dq_ag = p.c(Dq)
            p.add("gpsimd", (lambda e: e.collective_compute(
                "AllGather", OP.bypass, replica_groups=[list(range(NCORE))],
                ins=[ag_in.ap().opt()], outs=[ag_out.ap().opt()])),
                waits=[(Dq, dq_ag)], inc=CC)

        def attn_head(hl, cc_need):
            prev_use = [(T_, p.c(T_))]
            for ai, w_arr in enumerate((wq_in, wk_in, wv_in)):
                p.add("sync", (lambda e, w=w_arr: e.dma_start(
                    out=w32[:, :], in_=w[:, :, hl * 128:(hl + 1) * 128])),
                    waits=prev_use + ([(V_, p.c(V_))] if ai else []), inc=Dq, inc_n=16)
                dq_w = p.c(Dq)
                wsh_sl = wsh.rearrange("p (f j) -> p f j", f=NFC)[:, :, ai * 128:(ai + 1) * 128]
                wsl_sl = wsl.rearrange("p (f j) -> p f j", f=NFC)[:, :, ai * 128:(ai + 1) * 128]
                s_w = p.add("scalar", (lambda e, o=wsh_sl: e.copy(o, w32.rearrange(
                    "p (f j) -> p f j", f=NFC))), waits=[(Dq, dq_w)], inc=S_)
                v_w = p.add("vector", (lambda e, o=wsl_sl, h=wsh_sl: e.tensor_tensor(
                    o, w32.rearrange("p (f j) -> p f j", f=NFC), h, op=OP.subtract)),
                    waits=[(S_, s_w)], inc=V_)
                prev_use = [(V_, v_w)]
            s_wsplit, v_wsplit = p.c(S_), p.c(V_)

            wsh3 = wsh.rearrange("p (f j) -> p f j", f=NFC)
            wsl3 = wsl.rearrange("p (f j) -> p f j", f=NFC)
            t_y_use = {}
            v_split_bank = {}
            v_rd_prev = 0
            t_tp_use = {}
            s_tp_evac = {}
            for sh in range(NCORE):
                for tq in range(4):
                    gq = sh * 4 + tq
                    yb = gq % 2
                    w = [(CC, cc_need)]
                    if gq >= 2:
                        w.append((T_, t_y_use[gq - 2]))
                    for s2 in range(2):
                        p.add("sync", (lambda e, sh=sh, tq=tq, s2=s2, yb=yb: e.dma_start(
                            out=ysb[yb][s2].rearrange("p (f t) -> p f t", f=NFC),
                            in_=ago[sh, s2].rearrange("p (f t) -> p f t", f=NFC)
                                [:, :, tq * 256:(tq + 1) * 256])),
                            waits=w if s2 == 0 else (), inc=Dq, inc_n=16)
                    dq_y = p.c(Dq)
                    for t8 in range(2):
                        gt = gq * 2 + t8
                        pr = gt % 2
                        w0 = [(Dq, dq_y), (S_, s_wsplit), (V_, v_wsplit),
                              (V_, v_split_bank.get(pr, 0))]
                        ops48 = [(yb_i, wv3, fc)
                                 for yb_i, wv3 in ((0, wsh3), (1, wsh3), (0, wsl3))
                                 for fc in range(NFC)]
                        for i48, (yb_i, wv3, fc) in enumerate(ops48):
                            last = i48 == len(ops48) - 1
                            t_pj = mm(bank(pr, 384),
                                      ysb[yb][yb_i].rearrange("p (f t) -> p f t", f=NFC)
                                      [:, fc, t8 * 128:(t8 + 1) * 128],
                                      wv3[:, fc, :],
                                      i48 == 0, last,
                                      waits=w0 if i48 == 0 else (),
                                      inc=T_ if last else None)
                        t_y_use[gq] = t_pj
                        s_sq = p.add("scalar", (lambda e, pr=pr: e.square(psq[:, :], bank(pr, 384))),
                                     waits=[(T_, t_pj), (V_, v_rd_prev)], inc=S_)
                        v_rd = p.add("vector", (lambda e: e.tensor_reduce(
                            nrmt[:, 0:3], psq.rearrange("p (a j) -> p a j", a=3),
                            axis=mybir.AxisListType.X, op=OP.add)),
                            waits=[(S_, s_sq)], inc=V_)
                        v_rd_prev = v_rd
                        s_sr = p.add("scalar", lambda e: e.sqrt(nrmt[:, 4:7], nrmt[:, 0:3]),
                                     waits=[(V_, v_rd)], inc=S_)
                        v_ep = p.add("vector", (lambda e: e.tensor_scalar(
                            nrmt[:, 8:11], nrmt[:, 4:7], EPS_NRM, None, op0=OP.add, op1=OP.bypass)),
                            waits=[(S_, s_sr)], inc=V_)
                        p.add("vector", lambda e: e.reciprocal(nrmt[:, 12:15], nrmt[:, 8:11]),
                              waits=[(V_, v_ep)], inc=V_)
                        v_tau = p.add("vector", (lambda e: e.tensor_scalar_mul(
                            nrmt[:, 16:17], nrmt[:, 12:13], tau_bc[:, :])), inc=V_)
                        sc_q, sc_k, sc_v = nrmt[:, 16:17], nrmt[:, 13:14], nrmt[:, 14:15]
                        if DEBUG_TAP == "vraw":
                            one_r = AR4[:, 1090:1091]
                            sc_q = sc_k = sc_v = one_r
                        if DEBUG_TAP == "scl":
                            p.add("vector", (lambda e, gt=gt: e.tensor_copy(
                                AR3[:, 13568 + gt * 20: 13568 + gt * 20 + 17], nrmt[:, 0:17])),
                                inc=V_)
                        p.add("vector", (lambda e, pr=pr, gt=gt: e.tensor_scalar(
                            vh_tm[:, gt * 128:(gt + 1) * 128], bank(pr, 128, 256),
                            sc_v, None, op0=OP.mult, op1=OP.bypass)),
                            waits=[(V_, v_tau)], inc=V_)
                        v_vl = p.add("vector", (lambda e, pr=pr, gt=gt: e.scalar_tensor_tensor(
                            vl_tm[:, gt * 128:(gt + 1) * 128], bank(pr, 128, 256),
                            sc_v, vh_tm[:, gt * 128:(gt + 1) * 128],
                            op0=OP.mult, op1=OP.subtract)), inc=V_)
                        v_last_split = v_vl
                        for ai, (dh, dl, scl) in enumerate(((qh_fm, ql_fm, sc_q),
                                                           (kh_fm, kl_fm, sc_k))):
                            th, tl = ttt[2 * ai], ttt[2 * ai + 1]
                            v_h = p.add("vector", (lambda e, pr=pr, ai=ai, th=th, scl=scl:
                                e.tensor_scalar(th[:, :], bank(pr, 128, ai * 128), scl, None,
                                                op0=OP.mult, op1=OP.bypass)),
                                waits=[(T_, t_tp_use.get(2 * ai, 0))], inc=V_)
                            v_l = p.add("vector", (lambda e, pr=pr, ai=ai, th=th, tl=tl, scl=scl:
                                e.scalar_tensor_tensor(tl[:, :], bank(pr, 128, ai * 128),
                                                       scl, th[:, :],
                                                       op0=OP.mult, op1=OP.subtract)),
                                waits=[(T_, t_tp_use.get(2 * ai + 1, 0))], inc=V_)
                            v_last_split = v_l
                            for j, (tsrc, dst, vdep) in enumerate(((th, dh, v_h), (tl, dl, v_l))):
                                tpb = 2 + (2 * ai + j)
                                t_tp = p.add("tensor", (lambda e, tpb=tpb, tsrc=tsrc:
                                    e.transpose(bank(tpb, 64).bitcast(BF16),
                                                tsrc[:, :], eye[:, :])),
                                    waits=[(V_, vdep),
                                           (S_, s_tp_evac.get(tpb, 0))], inc=T_)
                                t_tp_use[2 * ai + j] = t_tp
                                s_tp_evac[tpb] = p.add("scalar", (lambda e, tpb=tpb, dst=dst, gt=gt:
                                    e.copy(dst[:, gt * 128:(gt + 1) * 128],
                                           bank(tpb, 64).bitcast(BF16))),
                                    waits=[(T_, t_tp)], inc=S_)
                        v_split_bank[pr] = v_last_split
                        if SERIAL_B:
                            p.barrier()
            s_prep, v_prep, t_prep = p.c(S_), p.c(V_), p.c(T_)

            if DEBUG_TAP == "scl":
                v_dbg = p.c(V_)
                p.add("sync", lambda e: e.dma_start(out=y_out[:, 0:1280],
                      in_=AR3[:, 13568:14848]), waits=[(V_, v_dbg)], inc=Dq, inc_n=16)
                return
            if DEBUG_TAP == "v2" and hl == 0:
                stage3 = AR3[:, 13568:16384]
                for hh in range(4):
                    v_d = p.add("vector", (lambda e, hh=hh: e.tensor_tensor(
                        stage3[:, 0:2048], vh_tm[:, hh * 2048:(hh + 1) * 2048],
                        vl_tm[:, hh * 2048:(hh + 1) * 2048], op=OP.add)),
                        waits=[(S_, s_prep), (V_, v_prep), (Dq, p.c(Dq))], inc=V_)
                    p.add("sync", (lambda e, hh=hh: e.dma_start(
                        out=y_out[:, hh * 2048:(hh + 1) * 2048],
                        in_=stage3[:, 0:2048])), waits=[(V_, v_d)], inc=Dq, inc_n=16)
            if DEBUG_TAP == "qk2" and hl == 0:
                stage2 = AR3[:, 13568:16384]
                for ii, (ah, al, off) in enumerate(((qh_fm, ql_fm, 0), (kh_fm, kl_fm, 8192))):
                    for hh in range(4):
                        v_d = p.add("vector", (lambda e, ah=ah, al=al, hh=hh: e.tensor_tensor(
                            stage2[:, 0:2048], ah[:, hh * 2048:(hh + 1) * 2048],
                            al[:, hh * 2048:(hh + 1) * 2048], op=OP.add)),
                            waits=[(S_, s_prep), (V_, v_prep), (Dq, p.c(Dq))], inc=V_)
                        p.add("sync", (lambda e, off=off, hh=hh: e.dma_start(
                            out=y_out[:, off + hh * 2048: off + (hh + 1) * 2048],
                            in_=stage2[:, 0:2048])), waits=[(V_, v_d)], inc=Dq, inc_n=16)
            if DEBUG_TAP in ("qk", "v", "vraw"):
                stage = AR2[:, 8192:16384]
                if DEBUG_TAP == "qk":
                    a_h, a_l, b_h, b_l = qh_fm, ql_fm, kh_fm, kl_fm
                else:
                    a_h, a_l, b_h, b_l = vh_tm, vl_tm, vh_tm, vl_tm
                p.add("vector", lambda e: e.tensor_tensor(stage[:, 0:4096], a_h[:, 0:4096],
                      a_l[:, 0:4096], op=OP.add), waits=[(S_, s_prep), (V_, v_prep)], inc=V_)
                p.add("vector", lambda e: e.tensor_tensor(stage[:, 4096:8192], b_h[:, 0:4096],
                      b_l[:, 0:4096], op=OP.add), inc=V_)
                v_dbg = p.c(V_)
                p.add("sync", lambda e: e.dma_start(out=y_out[:, 0:8192], in_=stage[:, :]),
                      waits=[(V_, v_dbg)], inc=Dq, inc_n=16)
                p.add("vector", lambda e: e.tensor_tensor(stage[:, 0:4096], a_h[:, 4096:8192],
                      a_l[:, 4096:8192], op=OP.add), waits=[(Dq, p.c(Dq))], inc=V_)
                p.add("vector", lambda e: e.tensor_tensor(stage[:, 4096:8192], b_h[:, 4096:8192],
                      b_l[:, 4096:8192], op=OP.add), inc=V_)
                v_dbg = p.c(V_)
                p.add("sync", lambda e: e.dma_start(out=y_out[:, 8192:16384], in_=stage[:, :]),
                      waits=[(V_, v_dbg)], inc=Dq, inc_n=16)
                return

            s_exp_bank = {}
            v_pl_buf = {}
            t_p_use = {}
            v_pv_evac = 0
            v_dn_evac = 0
            s_bc_evac = 0
            dq_osb = {}
            for qt in range(16):
                nkt = 4 * qt + 4
                for kt in range(nkt):
                    bp = kt % 2
                    w0 = [(S_, s_exp_bank.get(bp, 0)), (S_, s_prep), (V_, v_prep)]
                    if kt == 0:
                        w0.append((V_, v_pv_evac))
                        w0.append((V_, v_dn_evac))
                    mm(bank(bp), kh_fm[:, kt * 128:(kt + 1) * 128],
                       qh_fm[:, qt * 512:(qt + 1) * 512], True, False, waits=w0)
                    mm(bank(bp), kh_fm[:, kt * 128:(kt + 1) * 128],
                       ql_fm[:, qt * 512:(qt + 1) * 512], False, False)
                    t_sc = mm(bank(bp), kl_fm[:, kt * 128:(kt + 1) * 128],
                              qh_fm[:, qt * 512:(qt + 1) * 512], False, True, inc=T_)
                    s_exp = p.add("scalar", (lambda e, bp=bp: e.activation(
                        p32[bp][:, :], bank(bp), AF.Exp)),
                        waits=[(T_, t_sc), (V_, v_pl_buf.get(bp, 0))], inc=S_)
                    s_exp_bank[bp] = s_exp
                    diag = kt >= 4 * qt
                    v_mask = None
                    if diag:
                        dj = kt - 4 * qt
                        v_mask = p.add("vector", (lambda e, bp=bp, dj=dj: e.tensor_tensor(
                            p32[bp][:, :], p32[bp][:, :],
                            masks[:, dj * 512:(dj + 1) * 512], op=OP.mult)),
                            waits=[(S_, s_exp)], inc=V_)
                    s_ph = p.add("scalar", (lambda e, bp=bp: e.copy(phb[bp][:, :], p32[bp][:, :])),
                                 waits=([(V_, v_mask)] if diag else []) +
                                       [(T_, t_p_use.get(bp, 0))], inc=S_)
                    v_pl = p.add("vector", (lambda e, bp=bp: e.tensor_tensor(
                        plb[bp][:, :], p32[bp][:, :], phb[bp][:, :], op=OP.subtract)),
                        waits=[(S_, s_ph)], inc=V_)
                    v_pl_buf[bp] = v_pl
                    if DEBUG_TAP == "p0" and hl == 0 and qt == 0:
                        v_d = p.add("vector", (lambda e, bp=bp, kt=kt: e.tensor_copy(
                            AR3[:, 13568 + kt * 512: 13568 + (kt + 1) * 512], p32[bp][:, :])),
                            inc=V_)
                        p.add("sync", (lambda e, kt=kt: e.dma_start(
                            out=y_out[:, kt * 512:(kt + 1) * 512],
                            in_=AR3[:, 13568 + kt * 512: 13568 + (kt + 1) * 512])),
                            waits=[(V_, v_d)], inc=Dq, inc_n=16)
                    mm(bank_row(3), ones_bf[:, :], phb[bp][:, :], kt == 0, False,
                       waits=[(S_, s_ph)])
                    t_dn = mm(bank_row(3), ones_bf[:, :], plb[bp][:, :], False, kt == nkt - 1,
                              waits=[(V_, v_pl)], inc=T_ if kt == nkt - 1 else None)
                    mm(bank(6), vh_tm[:, kt * 128:(kt + 1) * 128], phb[bp][:, :], kt == 0, False)
                    mm(bank(6), vh_tm[:, kt * 128:(kt + 1) * 128], plb[bp][:, :], False, False)
                    t_pv = mm(bank(6), vl_tm[:, kt * 128:(kt + 1) * 128], phb[bp][:, :],
                              False, kt == nkt - 1, inc=T_)
                    t_p_use[bp] = t_pv
                    if SERIAL_C >= 2:
                        p.barrier()
                v_dn_evac = p.add("vector", lambda e: e.tensor_copy(r_dn[:, :], bank_row(3)),
                                  waits=[(T_, t_dn)], inc=V_)
                p.add("vector", lambda e: e.reciprocal(r_rc[:, :], r_dn[:, :]),
                      waits=[(V_, v_dn_evac)], inc=V_)
                v_rcn = p.add("vector", (lambda e: e.tensor_scalar_mul(
                    r_rc[:, :], r_rc[:, :], nu_sb[:, :])), inc=V_)
                t_bc = mm(bank(7), ones_row[:, :], r_rc[:, :], True, True,
                          waits=[(V_, v_rcn), (S_, s_bc_evac)], inc=T_)
                s_bc_evac = p.add("scalar", lambda e: e.copy(bc_sb[:, :], bank(7)),
                                  waits=[(T_, t_bc)], inc=S_)
                ob = qt % 2
                if DEBUG_TAP == "p0" and hl == 0 and qt == 0:
                    v_d = p.add("vector", lambda e: e.tensor_copy(
                        AR3[:, 15616:16128], bank(6)), waits=[(T_, t_pv)], inc=V_)
                    p.add("sync", lambda e: e.dma_start(
                        out=y_out[:, 2048:2560], in_=AR3[:, 15616:16128]),
                        waits=[(V_, v_d)], inc=Dq, inc_n=16)
                    v_d = p.add("vector", lambda e: e.tensor_copy(
                        AR3[0:1, 16128:16384].bitcast(F32)[:, 0:256],
                        bank_row(3, 256)), waits=[(T_, t_dn)], inc=V_)
                    v_d = p.add("vector", lambda e: e.tensor_copy(
                        AR3[0:1, 15360:15616], bank_row(3, 512)[:, 256:512]), inc=V_)
                    p.add("sync", lambda e: e.dma_start(
                        out=y_out[0:1, 2560:2816], in_=AR3[0:1, 16128:16384]),
                        waits=[(V_, v_d)], inc=Dq, inc_n=16)
                    p.add("sync", lambda e: e.dma_start(
                        out=y_out[0:1, 2816:3072], in_=AR3[0:1, 15360:15616]),
                        inc=Dq, inc_n=16)
                v_pv_evac = p.add("vector", (lambda e, ob=ob: e.scalar_tensor_tensor(
                    osb[ob][:, :], bank(6), 1.0, bc_sb[:, :], op0=OP.mult, op1=OP.mult)),
                    waits=[(S_, s_bc_evac), (T_, t_pv), (Dq, dq_osb.get(ob, 0))], inc=V_)
                if DEBUG_TAP == "p0" and hl == 0 and qt == 0:
                    v_d = p.add("vector", lambda e: e.tensor_copy(
                        AR3[0:1, 15616:16128], r_rc[:, :]), waits=[(V_, v_pv_evac)], inc=V_)
                    p.add("sync", lambda e: e.dma_start(
                        out=y_out[0:1, 3072:3584], in_=AR3[0:1, 15616:16128]),
                        waits=[(V_, v_d)], inc=Dq, inc_n=16)
                    v_d = p.add("vector", lambda e: e.tensor_copy(
                        AR3[:, 15616:16128], bc_sb[:, :]), inc=V_)
                    p.add("sync", lambda e: e.dma_start(
                        out=y_out[:, 3584:4096], in_=AR3[:, 15616:16128]),
                        waits=[(V_, v_d)], inc=Dq, inc_n=16)
                    v_d = p.add("vector", (lambda e, ob=ob: e.tensor_copy(
                        AR3[:, 15616:16128], osb[ob][:, :])), inc=V_)
                    p.add("sync", lambda e: e.dma_start(
                        out=y_out[:, 4096:4608], in_=AR3[:, 15616:16128]),
                        waits=[(V_, v_d)], inc=Dq, inc_n=16)
                p.add("sync", (lambda e, ob=ob, qt=qt, hl=hl: e.dma_start(
                    out=a2a_in.ap()[256 * (qt // 2) + 128 * hl: 256 * (qt // 2) + 128 * (hl + 1),
                                    (qt % 2) * 512:(qt % 2 + 1) * 512],
                    in_=osb[ob][:, :])), waits=[(V_, v_pv_evac)], inc=Dq, inc_n=16)
                if DEBUG_TAP == "dbg2":
                    p.add("sync", (lambda e, ob=ob, qt=qt, hl=hl: e.dma_start(
                        out=dbg_dram.ap()[256 * (qt // 2) + 128 * hl: 256 * (qt // 2) + 128 * (hl + 1),
                                          (qt % 2) * 512:(qt % 2 + 1) * 512],
                        in_=osb[ob][:, :])), inc=Dq, inc_n=16)
                dq_osb[ob] = p.c(Dq)
                if SERIAL_C >= 1:
                    p.barrier()

        def phase4_a2a():
            dq_all = p.c(Dq)
            if DEBUG_TAP == "dbg2":
                dbv = dbg_dram.ap().rearrange("(f p) t -> p f t", p=128)
                p.add("sync", (lambda e: e.dma_start(
                    out=y_out.rearrange("p (f t) -> p f t", f=NFC), in_=dbv)),
                    waits=[(Dq, dq_all)], inc=Dq, inc_n=16)
                dq_all = p.c(Dq)
            if DEBUG_TAP == "ag2":
                p.add("gpsimd", (lambda e: e.dma_start(
                    out=y_out.rearrange("p t -> p t"), in_=ago[2, 0])),
                    waits=[(Dq, dq_all)], inc=Dg, inc_n=16)
            if DEBUG_TAP == "a2ain3":
                aiv3 = a2a_in.ap().rearrange("(f p) t -> p f t", p=128)
                p.add("sync", (lambda e: e.dma_start(
                    out=y_out.rearrange("p (f t) -> p f t", f=NFC), in_=aiv3)),
                    waits=[(Dq, dq_all)], inc=Dq, inc_n=16)
                dq_all = p.c(Dq)
            if DEBUG_TAP == "p0":
                p.add("sync", (lambda e: e.dma_start(
                    out=AR3[:, 13568:14080], in_=a2a_in.ap()[0:128, 0:512])),
                    waits=[(Dq, dq_all)], inc=Dq, inc_n=16)
                dq_b = p.c(Dq)
                p.add("sync", (lambda e: e.dma_start(
                    out=y_out[:, 4608:5120], in_=AR3[:, 13568:14080])),
                    waits=[(Dq, dq_b)], inc=Dq, inc_n=16)
                dq_all = p.c(Dq)

            def _a2a(e):
                with allow_ici_all_to_all():
                    return e.collective_compute(
                        "AllToAll", OP.bypass, replica_groups=[list(range(NCORE))],
                        ins=[a2a_in.ap().opt()], outs=[a2a_out.ap().opt()])
            p.add("gpsimd", _a2a, waits=[(Dq, dq_all)], inc=CC)
            cc_n = p.c(CC)
            if DEBUG_TAP == "a2aout":
                p.add("gpsimd", (lambda e: e.dma_start(
                    out=y_out.rearrange("p (f t) -> p f t", f=NFC),
                    in_=a2a_out.ap().rearrange("(f p) t -> p f t", p=128))),
                    waits=[(CC, cc_n)], inc=Dg, inc_n=16)
                return
            a2v = a2a_out.ap().rearrange("(f p) t -> p f t", p=128)
            for fc in range(NFC):
                p.add("gpsimd", (lambda e, fc=fc: e.dma_start(
                    out=xres[:, fc * 1024:(fc + 1) * 1024], in_=a2v[:, fc, :],
                    accum_op=OP.add)),
                    waits=[(CC, cc_n)] if fc == 0 else (), inc=Dg, inc_n=16)

        init()
        p.barrier()
        layer_norm()
        p.barrier()
        if DEBUG_TAP == "ln1":
            for fc in range(NFC):
                v_add = p.add("vector", (lambda e, fc=fc:
                              e.tensor_tensor(xa[:, fc * 1024:(fc + 1) * 1024],
                                              yh[:, fc * 1024:(fc + 1) * 1024],
                                              yl[:, fc * 1024:(fc + 1) * 1024], op=OP.add)), inc=V_)
            p.add("gpsimd", lambda e: e.dma_start(out=xres[:, :], in_=xa[:, :]),
                  waits=[(V_, p.c(V_))], inc=Dg, inc_n=16)
            p.barrier()
        else:
            ffn()
            p.barrier()
            if PHASES >= 2:
                layer_norm(load_first=True)
                phase2_ag()
                p.barrier()
            if DEBUG_TAP == "ag":
                cc_need = p.c(CC)
                p.add("gpsimd", (lambda e: e.dma_start(
                    out=y_out.rearrange("p t -> p t"), in_=ago[2, 0])),
                    waits=[(CC, cc_need)], inc=Dg, inc_n=16)
            if PHASES >= 3 and DEBUG_TAP != "ag":
                cc_need = p.c(CC)
                attn_head(0, cc_need)
                p.barrier()
                if DEBUG_TAP not in ("qk", "v", "vraw", "scl"):
                    attn_head(1, cc_need)
                    p.barrier()
            if DEBUG_TAP == "a2ain2":
                aiv2 = a2a_in.ap().rearrange("(f p) t -> p f t", p=128)
                p.add("gpsimd", (lambda e: e.dma_start(
                    out=y_out.rearrange("p (f t) -> p f t", f=NFC), in_=aiv2)),
                    waits=[(CC, p.c(CC))], inc=Dg, inc_n=16)
            if DEBUG_TAP == "a2a":
                aiv = a2a_in.ap().rearrange("(f p) t -> p f t", p=128)
                p.add("sync", lambda e: e.dma_start(
                    out=y_out.rearrange("p (f t) -> p f t", f=NFC), in_=aiv),
                    inc=Dq, inc_n=16)
            if PHASES >= 4 and DEBUG_TAP not in ("a2a",):
                phase4_a2a()
                p.barrier()
            if PHASES >= 5:
                layer_norm(load_first=True)
                p.barrier()
                ffn()
                p.barrier()

        if DEBUG_TAP not in ("a2a", "qk", "v", "vraw", "scl", "ag", "a2aout", "a2ain2", "a2ain3", "ag2", "qk2", "dbg2", "v2", "p0"):
            p.add("sync", lambda e: e.dma_start(out=xa[:, :], in_=xres[:, :]), inc=Dq, inc_n=16)
            dq_ld = p.c(Dq)
            p.add("sync", lambda e: e.dma_start(out=y_out[:, :], in_=xa[:, :]),
                  waits=[(Dq, dq_ld)], inc=Dq, inc_n=16)
        dq_st = p.c(Dq)
        p.add("gpsimd", lambda e: e.nop(), waits=[(Dq, dq_st)])

        stats = p.emit(block, sems)
        print("emit stats:", {k: tuple(v) for k, v in stats.items()}, "matmuls:", nmm[0])
    return nc


def stage_inputs(inputs):
    x = np.asarray(inputs["L0_attn_x"])[0]
    wq = np.asarray(inputs["L0_attn_wq"])
    wk = np.asarray(inputs["L0_attn_wk"])
    wv = np.asarray(inputs["L0_attn_wv"])
    tau = np.asarray(inputs["L0_attn_tau"]).reshape(1, 1)
    nu = np.asarray(inputs["L0_attn_nu"]).reshape(1, 1)
    wg = np.asarray(inputs["L0_ffn_w_gate"])
    wu = np.asarray(inputs["L0_ffn_w_up"])
    wd = np.asarray(inputs["L0_ffn_w_down"])
    sg = np.asarray(inputs["L0_ffn_s_gate"])
    su = np.asarray(inputs["L0_ffn_s_up"])
    sd = np.asarray(inputs["L0_ffn_s_down"])

    bf = ml_dtypes.bfloat16
    wg_s = np.ascontiguousarray(
        wg.reshape(NFC, 128, 16, 512).transpose(2, 1, 0, 3).reshape(16, 128, NFC * 512))
    wu_s = np.ascontiguousarray(
        wu.reshape(NFC, 128, 16, 512).transpose(2, 1, 0, 3).reshape(16, 128, NFC * 512))
    wd_s = np.ascontiguousarray(
        wd.reshape(8, 8, 128, 16, 128).transpose(0, 3, 2, 1, 4).reshape(8, 16, 128, 8 * 128))
    sg_s = np.ascontiguousarray(sg.reshape(64, 128).T)
    su_s = np.ascontiguousarray(su.reshape(64, 128).T)
    sd_s = np.ascontiguousarray(sd.reshape(16, 128).T)
    mask = np.ascontiguousarray(
        np.triu(np.ones((512, 512), np.float32)).reshape(4, 128, 512)
        .transpose(1, 0, 2).reshape(128, 4 * 512)).astype(bf)
    eye = np.eye(128, dtype=np.float32).astype(bf)

    in_maps = []
    for c in range(NCORE):
        rows = slice(c * TOK, (c + 1) * TOK)
        x_slab = np.ascontiguousarray(
            x[rows].T.reshape(NFC, 128, TOK).transpose(1, 0, 2).reshape(128, NFC * TOK))
        hc = slice(c * H_PER * DH, (c + 1) * H_PER * DH)
        wq_s = np.ascontiguousarray(wq[hc, :].T.reshape(NFC, 128, H_PER * DH).transpose(1, 0, 2))
        wk_s = np.ascontiguousarray(wk[hc, :].T.reshape(NFC, 128, H_PER * DH).transpose(1, 0, 2))
        wv_s = np.ascontiguousarray(wv[hc, :].T.reshape(NFC, 128, H_PER * DH).transpose(1, 0, 2))
        in_maps.append({
            "x": x_slab, "wq": wq_s, "wk": wk_s, "wv": wv_s,
            "tau": tau, "nu": nu,
            "wg": wg_s, "wu": wu_s, "wd": wd_s,
            "sg": sg_s, "su": su_s, "sd": sd_s,
            "mask": mask, "eye": eye,
        })
    return in_maps


def unstage_output(results):
    outs = []
    for c in range(NCORE):
        slab = results[c]["y"]
        xt = slab.reshape(128, NFC, TOK).transpose(1, 0, 2).reshape(C, TOK)
        outs.append(np.ascontiguousarray(xt.T))
    return np.concatenate(outs, axis=0)[None]


def kernel(**inputs):
    in_maps = stage_inputs(inputs)
    nc = build_nc()
    res = run_bass_kernel_spmd(nc, in_maps, core_ids=list(range(NCORE)))
    return unstage_output(res.results)


# version 02
# speedup vs baseline: 1.0231x; 1.0231x over previous
import sys
if "/opt/trn_rl_repo" not in sys.path:
    sys.path.insert(0, "/opt/trn_rl_repo")
import numpy as np
import ml_dtypes
import concourse.bass as bass
import concourse.mybir as mybir
from concourse.bass_utils import run_bass_kernel_spmd
from concourse.aot_env import allow_ici_all_to_all

F32 = mybir.dt.float32
BF16 = mybir.dt.bfloat16
I8 = mybir.dt.int8
AF = mybir.ActivationFunctionType
OP = mybir.AluOpType

NCORE = 8
C = 2048
TOK = 1024
T = 8192
INTER = 8192
H_PER = 2
DH = 128
NFC = C // 128
EPS_LN = 1e-5
EPS_NRM = 1e-8

PHASES = 6
DEBUG_TAP = None
SERIAL_B = False
SERIAL_C = 0


class Plan:

    ENGS = ("tensor", "vector", "scalar", "sync", "gpsimd")

    def __init__(self):
        self.ops = {e: [] for e in self.ENGS}
        self.cnt = {}
        self.sem_names = []

    def sem(self, name):
        if name not in self.cnt:
            self.cnt[name] = 0
            self.sem_names.append(name)
        return name

    def c(self, sem):
        return self.cnt.get(sem, 0)

    def add(self, eng, fn, waits=(), inc=None, inc_n=1):
        for s, v in waits:
            assert v <= self.cnt.get(s, 0), (s, v, self.cnt.get(s, 0))
        if inc is not None:
            self.sem(inc)
            self.cnt[inc] += inc_n
        self.ops[eng].append((fn, tuple(waits), inc, inc_n))
        return self.cnt.get(inc, 0) if inc is not None else None

    def barrier(self):
        snap = dict(self.cnt)
        eng_sem = {"tensor": "T", "vector": "V", "scalar": "S",
                   "sync": "Dq", "gpsimd": "Dg"}
        for e in self.ENGS:
            waits = tuple((s, v) for s, v in snap.items()
                          if v > 0 and s != eng_sem.get(e))
            self.ops[e].append((None, waits, None, 0))

    def emit(self, block, sems):
        last_wait = {e: {} for e in self.ENGS}
        stats = {e: [0, 0] for e in self.ENGS}

        def emit_for(eng_name):
            def go(eng):
                lw = last_wait[eng_name]
                for fn, waits, inc, inc_n in self.ops[eng_name]:
                    for s, v in waits:
                        if lw.get(s, -1) >= v:
                            continue
                        eng.wait_ge(sems[s], v)
                        lw[s] = v
                        stats[eng_name][1] += 1
                    if fn is None:
                        continue
                    instr = fn(eng)
                    stats[eng_name][0] += 1
                    if inc is not None:
                        instr.then_inc(sems[inc], inc_n)
            return go

        for e in self.ENGS:
            getattr(block, e)(emit_for(e))
        return stats


def build_nc():
    nc = bass.Bass("TRN2", target_bir_lowering=False, debug=False)
    p = Plan()
    T_, V_, S_, Dq, Dg, CC = (p.sem(s) for s in ("T", "V", "S", "Dq", "Dg", "CC"))

    x_in = nc.dram_tensor("x", [128, NFC * TOK], F32, kind="ExternalInput").ap()
    wq_in = nc.dram_tensor("wq", [128, NFC, H_PER * DH], F32, kind="ExternalInput").ap()
    wk_in = nc.dram_tensor("wk", [128, NFC, H_PER * DH], F32, kind="ExternalInput").ap()
    wv_in = nc.dram_tensor("wv", [128, NFC, H_PER * DH], F32, kind="ExternalInput").ap()
    tau_in = nc.dram_tensor("tau", [1, 1], F32, kind="ExternalInput").ap()
    nu_in = nc.dram_tensor("nu", [1, 1], F32, kind="ExternalInput").ap()
    wg_in = nc.dram_tensor("wg", [16, 128, NFC * 512], I8, kind="ExternalInput").ap()
    wu_in = nc.dram_tensor("wu", [16, 128, NFC * 512], I8, kind="ExternalInput").ap()
    wd_in = nc.dram_tensor("wd", [8, 16, 128, 8 * 128], I8, kind="ExternalInput").ap()
    sg_in = nc.dram_tensor("sg", [128, 64], F32, kind="ExternalInput").ap()
    su_in = nc.dram_tensor("su", [128, 64], F32, kind="ExternalInput").ap()
    sd_in = nc.dram_tensor("sd", [128, 16], F32, kind="ExternalInput").ap()
    mask_in = nc.dram_tensor("mask", [128, 4 * 512], BF16, kind="ExternalInput").ap()
    eye_in = nc.dram_tensor("eye", [128, 128], BF16, kind="ExternalInput").ap()
    y_out = nc.dram_tensor("y", [128, NFC * TOK], F32, kind="ExternalOutput").ap()

    xres = nc.dram_tensor("xres", [128, NFC * TOK], F32).ap()
    ag_in = nc.dram_tensor("ag_in", [2 * 128, NFC * TOK], BF16)
    ag_out = nc.dram_tensor("ag_out", [NCORE * 2 * 128, NFC * TOK], BF16)
    a2a_in = nc.dram_tensor("a2a_in", [C, TOK], F32)
    a2a_out = nc.dram_tensor("a2a_out", [C, TOK], F32)
    dbg_dram = nc.dram_tensor("dbg_dram", [C, TOK], F32)

    with (
        nc.sbuf_tensor("AR1", [128, 16384], F32) as AR1,
        nc.sbuf_tensor("AR2", [128, 16384], F32) as AR2,
        nc.sbuf_tensor("AR3", [128, 16384], F32) as AR3,
        nc.sbuf_tensor("AR4", [128, 2816], F32) as AR4,
        nc.psum_tensor("PS", [128, 4096], F32) as PS,
        nc.semaphore("sT") as sT, nc.semaphore("sV") as sV,
        nc.semaphore("sS") as sS, nc.semaphore("sDq") as sDq,
        nc.semaphore("sDg") as sDg, nc.semaphore("sCC") as sCC,
        nc.Block() as block,
    ):
        sems = {"T": sT, "V": sV, "S": sS, "Dq": sDq, "Dg": sDg, "CC": sCC}

        def bank(i, width=512, off=0):
            return PS[:, i * 512 + off: i * 512 + off + width]

        def bank_row(i, width=512):
            return PS[0:1, i * 512: i * 512 + width]

        xa = AR1
        wgb = [AR1[:, 0:4096].bitcast(BF16), AR1[:, 4096:8192].bitcast(BF16)]
        wub = [AR1[:, 8192:12288].bitcast(BF16), AR1[:, 12288:16384].bitcast(BF16)]
        qh_fm = AR1[:, 0:4096].bitcast(BF16)
        ql_fm = AR1[:, 4096:8192].bitcast(BF16)
        kh_fm = AR1[:, 8192:12288].bitcast(BF16)
        kl_fm = AR1[:, 12288:16384].bitcast(BF16)
        yh = AR2[:, 0:8192].bitcast(BF16)
        yl = AR2[:, 8192:16384].bitcast(BF16)
        vh_tm = AR2[:, 0:4096].bitcast(BF16)
        vl_tm = AR2[:, 4096:8192].bitcast(BF16)
        ysb = [(AR2[:, 8192:10240].bitcast(BF16), AR2[:, 10240:12288].bitcast(BF16)),
               (AR2[:, 12288:14336].bitcast(BF16), AR2[:, 14336:16384].bitcast(BF16))]
        x2t = [AR3[:, 0:1024], AR3[:, 1024:2048]]
        bca = AR3[:, 2048:3072]
        bcb = AR3[:, 3072:4096]
        lst = [AR3[:, 4096:4608], AR3[:, 4608:5120]]
        lsy = [AR3[:, 5120:5632], AR3[:, 5632:6144]]
        fh = AR3[:, 0:4096].bitcast(BF16)
        fl = AR3[:, 4096:8192].bitcast(BF16)
        wdb = [AR3[:, 8192:8704].bitcast(BF16), AR3[:, 8704:9216].bitcast(BF16)]
        gb = [AR3[:, 9216:9728], AR3[:, 9728:10240]]
        ub = [AR3[:, 10240:10752], AR3[:, 10752:11264]]
        fb = [AR3[:, 11264:11776], AR3[:, 11776:12288]]
        dsb = [AR3[:, 12288:12800], AR3[:, 12800:13312]]
        w32 = AR3[:, 0:2048]
        wsh = AR3[:, 2048:5120].bitcast(BF16)
        wsl = AR3[:, 5120:8192].bitcast(BF16)
        psq = AR3[:, 8192:8576]
        nrmt = AR3[:, 8576:8608]
        p32 = [AR3[:, 8704 + 512 * i: 9216 + 512 * i] for i in range(4)]
        phb = [AR3[:, 10752 + 256 * i: 11008 + 256 * i].bitcast(BF16) for i in range(4)]
        plb = [AR3[:, 11776 + 256 * i: 12032 + 256 * i].bitcast(BF16) for i in range(4)]
        rc_sb = [AR3[:, 12800:13312], AR3[:, 13312:13824]]
        osb = [AR3[:, 13824:14336], AR3[:, 14336:14848]]
        ttt = [AR3[:, 14848 + 64 * i: 14912 + 64 * i].bitcast(BF16) for i in range(4)]
        masks = AR4[:, 0:1024].bitcast(BF16)
        eye = AR4[:, 1024:1088].bitcast(BF16)
        ones_bf = AR4[:, 1088:1089].bitcast(BF16)[:, 0:1]
        ones_f = AR4[:, 1090:1091]
        ones_row = AR4[0:1, 1092:1220]
        sg_sb = AR4[:, 1220:1284]
        su_sb = AR4[:, 1284:1348]
        sdh_sb = AR4[:, 1348:1364]
        sd_sb = AR4[:, 1364:1380]
        tau_bc = AR4[:, 1380:1381]
        nu_sb = AR4[0:1, 1382:1383]
        tau_sb = AR4[0:1, 1384:1385]
        ones128_bf = AR4[:, 1408:1472].bitcast(BF16)
        ones128_f = AR4[:, 1472:1600]
        nu_bc = AR4[:, 1600:1601]
        r_s1 = AR3[0:1, 6144:7168]
        r_s2 = AR3[0:1, 7168:8192]
        r_t0 = AR3[0:1, 8192:9216]
        r_a = AR3[0:1, 9216:10240]
        r_b = AR3[0:1, 10240:11264]
        r_m = r_s1

        nmm = [0]

        def mm(out, lhsT, rhs, start, stop, waits=(), inc=None):
            def f(e, out=out, lhsT=lhsT, rhs=rhs, start=start, stop=stop):
                return e.matmul(out, lhsT, rhs, start=start, stop=stop)
            nmm[0] += 1
            return p.add("tensor", f, waits=waits, inc=inc)

        def init():
            p.add("sync", lambda e: e.dma_start(out=xa[:, :], in_=x_in[:, :]), inc=Dq, inc_n=16)
            p.add("sync", lambda e: e.dma_start(out=masks[:, :], in_=mask_in[:, :]), inc=Dq, inc_n=16)
            p.add("sync", lambda e: e.dma_start(out=eye[:, :], in_=eye_in[:, :]), inc=Dq, inc_n=16)
            p.add("sync", lambda e: e.dma_start(out=sg_sb[:, :], in_=sg_in[:, :]), inc=Dq, inc_n=16)
            p.add("sync", lambda e: e.dma_start(out=su_sb[:, :], in_=su_in[:, :]), inc=Dq, inc_n=16)
            p.add("sync", lambda e: e.dma_start(out=sd_sb[:, :], in_=sd_in[:, :]), inc=Dq, inc_n=16)
            p.add("sync", lambda e: e.dma_start(out=nu_sb[:, :], in_=nu_in[:, :]), inc=Dq, inc_n=16)
            p.add("sync", lambda e: e.dma_start(out=tau_sb[:, :], in_=tau_in[:, :]), inc=Dq, inc_n=16)
            dq0 = p.c(Dq)
            p.add("vector", lambda e: e.memset(ones_bf[:, :], 1.0), inc=V_)
            p.add("vector", lambda e: e.memset(ones_f[:, :], 1.0), inc=V_)
            p.add("vector", lambda e: e.memset(ones_row[:, :], 1.0), inc=V_)
            p.add("vector", lambda e: e.tensor_scalar_mul(sdh_sb[:, :], sd_sb[:, :], 0.5),
                  waits=[(Dq, dq0)], inc=V_)
            p.add("vector", lambda e: e.memset(ones128_bf[:, :], 1.0), inc=V_)
            p.add("vector", lambda e: e.memset(ones128_f[:, :], 1.0), inc=V_)
            mm(bank(7, 1), ones_row[:, :], tau_sb[:, :], True, True,
               waits=[(V_, p.c(V_)), (Dq, dq0)], inc=T_)
            mm(bank(6, 1), ones_row[:, :], nu_sb[:, :], True, True, inc=T_)
            p.add("scalar", lambda e: e.copy(tau_bc[:, :], bank(7, 1)),
                  waits=[(T_, p.c(T_))], inc=S_)
            p.add("scalar", lambda e: e.copy(nu_bc[:, :], bank(6, 1)), inc=S_)
            p.add("gpsimd", lambda e: e.dma_start(out=xres[:, :], in_=xa[:, :]),
                  waits=[(Dq, dq0)], inc=Dg, inc_n=16)

        def layer_norm(load_first=False):
            if load_first:
                p.add("sync", lambda e: e.dma_start(out=xa[:, :], in_=xres[:, :]),
                      inc=Dq, inc_n=16)
                dq_x = p.c(Dq)
                p.ops["tensor"].append((None, ((Dq, dq_x),), None, 0))
                p.ops["scalar"].append((None, ((Dq, dq_x),), None, 0))
                p.ops["vector"].append((None, ((Dq, dq_x),), None, 0))
            t_sq = {}
            for fc in range(NFC):
                xr = xa[:, fc * 1024:(fc + 1) * 1024]
                b = fc % 2
                w = [(T_, t_sq[fc - 2])] if fc >= 2 else []
                t_sq[fc] = None
                s_done = p.add("scalar", (lambda e, o=x2t[b], i=xr: e.square(o, i)),
                               waits=w, inc=S_)
                for tt in range(2):
                    mm(bank(0 + tt), ones128_f[:, :], xr[:, tt * 512:(tt + 1) * 512],
                       fc == 0, fc == NFC - 1)
                    t_sq[fc] = mm(bank(2 + tt), ones128_f[:, :],
                                  x2t[b][:, tt * 512:(tt + 1) * 512],
                                  fc == 0, fc == NFC - 1,
                                  waits=[(S_, s_done)], inc=T_ if tt == 1 else None)
            t_stats = p.c(T_)
            p.add("vector", lambda e: e.tensor_copy(r_s1[:, 0:512], bank_row(0)), waits=[(T_, t_stats)], inc=V_)
            p.add("vector", lambda e: e.tensor_copy(r_s1[:, 512:1024], bank_row(1)), inc=V_)
            p.add("vector", lambda e: e.tensor_copy(r_s2[:, 0:512], bank_row(2)), inc=V_)
            p.add("vector", lambda e: e.tensor_copy(r_s2[:, 512:1024], bank_row(3)), inc=V_)
            p.add("vector", lambda e: e.tensor_scalar_mul(r_m[:, :], r_s1[:, :], 1.0 / C), inc=V_)
            p.add("vector", lambda e: e.tensor_tensor(r_t0[:, :], r_m[:, :], r_m[:, :], op=OP.mult), inc=V_)
            p.add("vector", lambda e: e.tensor_scalar(r_s2[:, :], r_s2[:, :], 1.0 / C, EPS_LN,
                                                      op0=OP.mult, op1=OP.add), inc=V_)
            v_pre = p.add("vector", lambda e: e.tensor_tensor(r_t0[:, :], r_s2[:, :], r_t0[:, :],
                                                              op=OP.subtract), inc=V_)
            s_sqrt = p.add("scalar", lambda e: e.sqrt(r_t0[:, :], r_t0[:, :]),
                           waits=[(V_, v_pre)], inc=S_)
            p.add("vector", lambda e: e.reciprocal(r_a[:, :], r_t0[:, :]),
                  waits=[(S_, s_sqrt)], inc=V_)
            p.add("vector", lambda e: e.tensor_tensor(r_b[:, :], r_m[:, :], r_a[:, :], op=OP.mult), inc=V_)
            v_rows = p.add("vector", lambda e: e.tensor_scalar_mul(r_b[:, :], r_b[:, :], -1.0), inc=V_)
            for tt in range(2):
                mm(bank(4 + 2 * tt), ones_row[:, :], r_a[:, tt * 512:(tt + 1) * 512], True, True,
                   waits=[(V_, v_rows)])
                t_bc = mm(bank(5 + 2 * tt), ones_row[:, :], r_b[:, tt * 512:(tt + 1) * 512],
                          True, True, inc=T_)
                p.add("scalar", (lambda e, tt=tt: e.copy(bca[:, tt * 512:(tt + 1) * 512], bank(4 + 2 * tt))),
                      waits=[(T_, t_bc)], inc=S_)
                p.add("scalar", (lambda e, tt=tt: e.copy(bcb[:, tt * 512:(tt + 1) * 512], bank(5 + 2 * tt))),
                      inc=S_)
            s_bc = p.c(S_)
            for fc in range(NFC):
                for tt in range(2):
                    xrt = xa[:, fc * 1024 + tt * 512: fc * 1024 + tt * 512 + 512]
                    st = tt
                    yh_t = yh[:, fc * 1024 + tt * 512: fc * 1024 + tt * 512 + 512]
                    yl_t = yl[:, fc * 1024 + tt * 512: fc * 1024 + tt * 512 + 512]
                    p.add("vector", (lambda e, o=lst[st], i=xrt, b2=bca[:, tt * 512:(tt + 1) * 512]:
                                     e.tensor_tensor(o, i, b2, op=OP.mult)),
                          waits=[(S_, s_bc)], inc=V_)
                    v_y = p.add("vector", (lambda e, o=lsy[st], i=lst[st], b2=bcb[:, tt * 512:(tt + 1) * 512]:
                                           e.tensor_tensor(o, i, b2, op=OP.add)), inc=V_)
                    s_h = p.add("scalar", (lambda e, o=yh_t, i=lsy[st]: e.copy(o, i)),
                                waits=[(V_, v_y)], inc=S_)
                    p.add("vector", (lambda e, o=yl_t, i=lsy[st], h=yh_t:
                                     e.tensor_tensor(o, i, h, op=OP.subtract)),
                          waits=[(S_, s_h)], inc=V_)

        def ffn():
            t_gate_last = {}
            t_up_last = {}
            t_down_last = {}
            s_evac = {}
            v_evac = {}
            s_fuse = 0
            v_fuse = 0
            dg_w = {}
            dg_wd = {}
            s_dsb = {}
            dg_dsb = {}
            y_ready = (V_, p.c(V_))

            def load_w(cidx):
                buf = cidx % 2
                w = []
                if cidx >= 2:
                    w = [(T_, t_gate_last.get(buf, 0))]
                p.add("gpsimd", (lambda e, b=buf, ci=cidx:
                                 e.dma_start(out=wgb[b][:, :], in_=wg_in[ci, :, :])),
                      waits=w, inc=Dg, inc_n=16)
                if cidx >= 2:
                    w = [(T_, t_up_last.get(buf, 0))]
                dg_w[cidx] = p.add("gpsimd", (lambda e, b=buf, ci=cidx:
                                              e.dma_start(out=wub[b][:, :], in_=wu_in[ci, :, :])),
                                   waits=w, inc=Dg, inc_n=16)

            load_w(0)
            load_w(1)
            for blk in range(8):
                for ic in range(2):
                    cidx = blk * 2 + ic
                    buf = cidx % 2
                    wgv = wgb[buf].rearrange("p (f i) -> p f i", f=NFC)
                    wuv = wub[buf].rearrange("p (f i) -> p f i", f=NFC)
                    for sub in range(4):
                        ig = cidx * 4 + sub
                        sl = ic * 4 + sub
                        for tt in range(2):
                            rot = (sub * 2 + tt) % 2
                            gbank, ubank = 0 + rot, 2 + rot
                            w = [(Dg, dg_w[cidx]), y_ready,
                                 (S_, s_evac.get(gbank, 0))]
                            for pi, ysl in enumerate((yh, yl)):
                                for fc in range(NFC):
                                    t_g = mm(bank(gbank),
                                             wgv[:, fc, sub * 128:(sub + 1) * 128],
                                             ysl[:, fc * 1024 + tt * 512:][:, 0:512],
                                             pi == 0 and fc == 0, pi == 1 and fc == NFC - 1,
                                             waits=w if (pi == 0 and fc == 0) else (),
                                             inc=T_ if (pi == 1 and fc == NFC - 1) else None)
                            t_gate_last[buf] = t_g
                            w = [(V_, v_evac.get(ubank, 0))]
                            for pi, ysl in enumerate((yh, yl)):
                                for fc in range(NFC):
                                    t_u = mm(bank(ubank),
                                             wuv[:, fc, sub * 128:(sub + 1) * 128],
                                             ysl[:, fc * 1024 + tt * 512:][:, 0:512],
                                             pi == 0 and fc == 0, pi == 1 and fc == NFC - 1,
                                             waits=w if (pi == 0 and fc == 0) else (),
                                             inc=T_ if (pi == 1 and fc == NFC - 1) else None)
                            t_up_last[buf] = t_u
                            st = rot
                            s_g = p.add("scalar",
                                        (lambda e, o=gb[st], b2=gbank, ig=ig:
                                         e.activation(o, bank(b2), AF.Silu, scale=sg_sb[:, ig:ig + 1])),
                                        waits=[(T_, t_g)], inc=S_)
                            s_evac[gbank] = s_g
                            v_u = p.add("vector",
                                        (lambda e, o=ub[st], b2=ubank, ig=ig:
                                         e.tensor_scalar_mul(o, bank(b2), su_sb[:, ig:ig + 1])),
                                        waits=[(T_, t_u)], inc=V_)
                            v_evac[ubank] = v_u
                            v_f = p.add("vector",
                                        (lambda e, o=fb[st], a=gb[st], b2=ub[st]:
                                         e.tensor_tensor(o, a, b2, op=OP.mult)),
                                        waits=[(S_, s_g)], inc=V_)
                            fh_t = fh[:, sl * 1024 + tt * 512:][:, 0:512]
                            fl_t = fl[:, sl * 1024 + tt * 512:][:, 0:512]
                            w_fz = [(V_, v_f)]
                            if blk > 0:
                                w_fz.append((T_, t_blk_down))
                            s_fuse = p.add("scalar",
                                           (lambda e, o=fh_t, i=fb[st]: e.copy(o, i)),
                                           waits=w_fz, inc=S_)
                            v_fuse = p.add("vector",
                                           (lambda e, o=fl_t, i=fb[st], h=fh_t:
                                            e.tensor_tensor(o, i, h, op=OP.subtract)),
                                           waits=[(S_, s_fuse)], inc=V_)
                    if cidx + 2 < 16:
                        load_w(cidx + 2)
                fhv = fh.rearrange("p (s t) -> p s t", s=8)
                flv = fl.rearrange("p (s t) -> p s t", s=8)
                for dmc in range(16):
                    wbuf = dmc % 2
                    w = [(T_, t_down_last.get(wbuf, 0))]
                    dg_wd[dmc] = p.add("gpsimd",
                                       (lambda e, b=wbuf, blk=blk, dmc=dmc:
                                        e.dma_start(out=wdb[b][:, :], in_=wd_in[blk, dmc, :, :])),
                                       waits=w, inc=Dg, inc_n=16)
                    wdv = wdb[wbuf].rearrange("p (s i) -> p s i", s=8)
                    for tt in range(2):
                        rot = 4 + (dmc * 2 + tt) % 2
                        w = [(Dg, dg_wd[dmc]), (S_, s_fuse), (V_, v_fuse),
                             (S_, s_dsb.get(rot, 0))]
                        first = True
                        for sub in range(8):
                            for pv, fv in ((0, fhv), (1, flv)):
                                t_d = mm(bank(rot),
                                         wdv[:, sub, :],
                                         fv[:, sub, tt * 512:(tt + 1) * 512],
                                         first, sub == 7 and pv == 1,
                                         waits=w if first else (),
                                         inc=T_ if (sub == 7 and pv == 1) else None)
                                first = False
                        t_down_last[wbuf] = t_d
                        t_blk_down = t_d
                        dst = (dmc * 2 + tt) % 2
                        s_d = p.add("scalar",
                                    (lambda e, o=dsb[dst], b2=rot, dmc=dmc:
                                     e.activation(o, bank(b2), AF.Copy, scale=sdh_sb[:, dmc:dmc + 1])),
                                    waits=[(T_, t_d), (Dg, dg_dsb.get(dst, 0))], inc=S_)
                        s_dsb[rot] = s_d
                        dg_dsb[dst] = p.add("gpsimd",
                                            (lambda e, i=dsb[dst], dmc=dmc, tt=tt:
                                             e.dma_start(out=xres[:, dmc * 1024 + tt * 512:][:, 0:512],
                                                         in_=i, accum_op=OP.add)),
                                            waits=[(S_, s_d)], inc=Dg, inc_n=16)
            return p.c(Dg)

        agi = ag_in.ap().rearrange("(s p) t -> s p t", p=128)
        ago = ag_out.ap().rearrange("(r s p) t -> r s p t", s=2, p=128)

        def phase2_ag():
            v_last, s_last = p.c(V_), p.c(S_)
            p.add("sync", lambda e: e.dma_start(out=agi[0], in_=yh[:, :]),
                  waits=[(V_, v_last), (S_, s_last)], inc=Dq, inc_n=16)
            p.add("sync", lambda e: e.dma_start(out=agi[1], in_=yl[:, :]), inc=Dq, inc_n=16)
            dq_ag = p.c(Dq)
            p.add("gpsimd", (lambda e: e.collective_compute(
                "AllGather", OP.bypass, replica_groups=[list(range(NCORE))],
                ins=[ag_in.ap().opt()], outs=[ag_out.ap().opt()])),
                waits=[(Dq, dq_ag)], inc=CC)

        def attn_head(hl, cc_need):
            prev_use = [(T_, p.c(T_))]
            for ai, w_arr in enumerate((wq_in, wk_in, wv_in)):
                p.add("sync", (lambda e, w=w_arr: e.dma_start(
                    out=w32[:, :], in_=w[:, :, hl * 128:(hl + 1) * 128])),
                    waits=prev_use + ([(V_, p.c(V_))] if ai else []), inc=Dq, inc_n=16)
                dq_w = p.c(Dq)
                wsh_sl = wsh.rearrange("p (f j) -> p f j", f=NFC)[:, :, ai * 128:(ai + 1) * 128]
                wsl_sl = wsl.rearrange("p (f j) -> p f j", f=NFC)[:, :, ai * 128:(ai + 1) * 128]
                s_w = p.add("scalar", (lambda e, o=wsh_sl: e.copy(o, w32.rearrange(
                    "p (f j) -> p f j", f=NFC))), waits=[(Dq, dq_w)], inc=S_)
                v_w = p.add("vector", (lambda e, o=wsl_sl, h=wsh_sl: e.tensor_tensor(
                    o, w32.rearrange("p (f j) -> p f j", f=NFC), h, op=OP.subtract)),
                    waits=[(S_, s_w)], inc=V_)
                prev_use = [(V_, v_w)]
            s_wsplit, v_wsplit = p.c(S_), p.c(V_)

            wsh3 = wsh.rearrange("p (f j) -> p f j", f=NFC)
            wsl3 = wsl.rearrange("p (f j) -> p f j", f=NFC)
            t_y_use = {}
            v_split_bank = {}
            v_rd_prev = 0
            t_tp_use = {}
            s_tp_evac = {}
            for sh in range(NCORE):
                for tq in range(4):
                    gq = sh * 4 + tq
                    yb = gq % 2
                    w = [(CC, cc_need)]
                    if gq >= 2:
                        w.append((T_, t_y_use[gq - 2]))
                    for s2 in range(2):
                        p.add("sync", (lambda e, sh=sh, tq=tq, s2=s2, yb=yb: e.dma_start(
                            out=ysb[yb][s2].rearrange("p (f t) -> p f t", f=NFC),
                            in_=ago[sh, s2].rearrange("p (f t) -> p f t", f=NFC)
                                [:, :, tq * 256:(tq + 1) * 256])),
                            waits=w if s2 == 0 else (), inc=Dq, inc_n=16)
                    dq_y = p.c(Dq)
                    for t8 in range(2):
                        gt = gq * 2 + t8
                        pr = gt % 2
                        w0 = [(Dq, dq_y), (S_, s_wsplit), (V_, v_wsplit),
                              (V_, v_split_bank.get(pr, 0))]
                        ops48 = [(yb_i, wv3, fc)
                                 for yb_i, wv3 in ((0, wsh3), (1, wsh3), (0, wsl3))
                                 for fc in range(NFC)]
                        for i48, (yb_i, wv3, fc) in enumerate(ops48):
                            last = i48 == len(ops48) - 1
                            t_pj = mm(bank(pr, 384),
                                      ysb[yb][yb_i].rearrange("p (f t) -> p f t", f=NFC)
                                      [:, fc, t8 * 128:(t8 + 1) * 128],
                                      wv3[:, fc, :],
                                      i48 == 0, last,
                                      waits=w0 if i48 == 0 else (),
                                      inc=T_ if last else None)
                        t_y_use[gq] = t_pj
                        s_sq = p.add("scalar", (lambda e, pr=pr: e.square(psq[:, :], bank(pr, 384))),
                                     waits=[(T_, t_pj), (V_, v_rd_prev)], inc=S_)
                        v_rd = p.add("vector", (lambda e: e.tensor_reduce(
                            nrmt[:, 0:3], psq.rearrange("p (a j) -> p a j", a=3),
                            axis=mybir.AxisListType.X, op=OP.add)),
                            waits=[(S_, s_sq)], inc=V_)
                        v_rd_prev = v_rd
                        s_sr = p.add("scalar", lambda e: e.sqrt(nrmt[:, 4:7], nrmt[:, 0:3]),
                                     waits=[(V_, v_rd)], inc=S_)
                        v_ep = p.add("vector", (lambda e: e.tensor_scalar(
                            nrmt[:, 8:11], nrmt[:, 4:7], EPS_NRM, None, op0=OP.add, op1=OP.bypass)),
                            waits=[(S_, s_sr)], inc=V_)
                        p.add("vector", lambda e: e.reciprocal(nrmt[:, 12:15], nrmt[:, 8:11]),
                              waits=[(V_, v_ep)], inc=V_)
                        v_tau = p.add("vector", (lambda e: e.tensor_scalar_mul(
                            nrmt[:, 16:17], nrmt[:, 12:13], tau_bc[:, :])), inc=V_)
                        sc_q, sc_k, sc_v = nrmt[:, 16:17], nrmt[:, 13:14], nrmt[:, 14:15]
                        if DEBUG_TAP == "vraw":
                            one_r = AR4[:, 1090:1091]
                            sc_q = sc_k = sc_v = one_r
                        if DEBUG_TAP == "scl":
                            p.add("vector", (lambda e, gt=gt: e.tensor_copy(
                                AR3[:, 13568 + gt * 20: 13568 + gt * 20 + 17], nrmt[:, 0:17])),
                                inc=V_)
                        p.add("vector", (lambda e, pr=pr, gt=gt: e.tensor_scalar(
                            vh_tm[:, gt * 128:(gt + 1) * 128], bank(pr, 128, 256),
                            sc_v, None, op0=OP.mult, op1=OP.bypass)),
                            waits=[(V_, v_tau)], inc=V_)
                        v_vl = p.add("vector", (lambda e, pr=pr, gt=gt: e.scalar_tensor_tensor(
                            vl_tm[:, gt * 128:(gt + 1) * 128], bank(pr, 128, 256),
                            sc_v, vh_tm[:, gt * 128:(gt + 1) * 128],
                            op0=OP.mult, op1=OP.subtract)), inc=V_)
                        v_last_split = v_vl
                        for ai, (dh, dl, scl) in enumerate(((qh_fm, ql_fm, sc_q),
                                                           (kh_fm, kl_fm, sc_k))):
                            th, tl = ttt[2 * ai], ttt[2 * ai + 1]
                            v_h = p.add("vector", (lambda e, pr=pr, ai=ai, th=th, scl=scl:
                                e.tensor_scalar(th[:, :], bank(pr, 128, ai * 128), scl, None,
                                                op0=OP.mult, op1=OP.bypass)),
                                waits=[(T_, t_tp_use.get(2 * ai, 0))], inc=V_)
                            v_l = p.add("vector", (lambda e, pr=pr, ai=ai, th=th, tl=tl, scl=scl:
                                e.scalar_tensor_tensor(tl[:, :], bank(pr, 128, ai * 128),
                                                       scl, th[:, :],
                                                       op0=OP.mult, op1=OP.subtract)),
                                waits=[(T_, t_tp_use.get(2 * ai + 1, 0))], inc=V_)
                            v_last_split = v_l
                            for j, (tsrc, dst, vdep) in enumerate(((th, dh, v_h), (tl, dl, v_l))):
                                tpb = 2 + (2 * ai + j)
                                t_tp = p.add("tensor", (lambda e, tpb=tpb, tsrc=tsrc:
                                    e.transpose(bank(tpb, 64).bitcast(BF16),
                                                tsrc[:, :], eye[:, :])),
                                    waits=[(V_, vdep),
                                           (S_, s_tp_evac.get(tpb, 0))], inc=T_)
                                t_tp_use[2 * ai + j] = t_tp
                                s_tp_evac[tpb] = p.add("scalar", (lambda e, tpb=tpb, dst=dst, gt=gt:
                                    e.copy(dst[:, gt * 128:(gt + 1) * 128],
                                           bank(tpb, 64).bitcast(BF16))),
                                    waits=[(T_, t_tp)], inc=S_)
                        v_split_bank[pr] = v_last_split
                        if SERIAL_B:
                            p.barrier()
            s_prep, v_prep, t_prep = p.c(S_), p.c(V_), p.c(T_)

            if DEBUG_TAP == "scl":
                v_dbg = p.c(V_)
                p.add("sync", lambda e: e.dma_start(out=y_out[:, 0:1280],
                      in_=AR3[:, 13568:14848]), waits=[(V_, v_dbg)], inc=Dq, inc_n=16)
                return
            if DEBUG_TAP == "v2" and hl == 0:
                stage3 = AR3[:, 13568:16384]
                for hh in range(4):
                    v_d = p.add("vector", (lambda e, hh=hh: e.tensor_tensor(
                        stage3[:, 0:2048], vh_tm[:, hh * 2048:(hh + 1) * 2048],
                        vl_tm[:, hh * 2048:(hh + 1) * 2048], op=OP.add)),
                        waits=[(S_, s_prep), (V_, v_prep), (Dq, p.c(Dq))], inc=V_)
                    p.add("sync", (lambda e, hh=hh: e.dma_start(
                        out=y_out[:, hh * 2048:(hh + 1) * 2048],
                        in_=stage3[:, 0:2048])), waits=[(V_, v_d)], inc=Dq, inc_n=16)
            if DEBUG_TAP == "qk2" and hl == 0:
                stage2 = AR3[:, 13568:16384]
                for ii, (ah, al, off) in enumerate(((qh_fm, ql_fm, 0), (kh_fm, kl_fm, 8192))):
                    for hh in range(4):
                        v_d = p.add("vector", (lambda e, ah=ah, al=al, hh=hh: e.tensor_tensor(
                            stage2[:, 0:2048], ah[:, hh * 2048:(hh + 1) * 2048],
                            al[:, hh * 2048:(hh + 1) * 2048], op=OP.add)),
                            waits=[(S_, s_prep), (V_, v_prep), (Dq, p.c(Dq))], inc=V_)
                        p.add("sync", (lambda e, off=off, hh=hh: e.dma_start(
                            out=y_out[:, off + hh * 2048: off + (hh + 1) * 2048],
                            in_=stage2[:, 0:2048])), waits=[(V_, v_d)], inc=Dq, inc_n=16)
            if DEBUG_TAP in ("qk", "v", "vraw"):
                stage = AR2[:, 8192:16384]
                if DEBUG_TAP == "qk":
                    a_h, a_l, b_h, b_l = qh_fm, ql_fm, kh_fm, kl_fm
                else:
                    a_h, a_l, b_h, b_l = vh_tm, vl_tm, vh_tm, vl_tm
                p.add("vector", lambda e: e.tensor_tensor(stage[:, 0:4096], a_h[:, 0:4096],
                      a_l[:, 0:4096], op=OP.add), waits=[(S_, s_prep), (V_, v_prep)], inc=V_)
                p.add("vector", lambda e: e.tensor_tensor(stage[:, 4096:8192], b_h[:, 0:4096],
                      b_l[:, 0:4096], op=OP.add), inc=V_)
                v_dbg = p.c(V_)
                p.add("sync", lambda e: e.dma_start(out=y_out[:, 0:8192], in_=stage[:, :]),
                      waits=[(V_, v_dbg)], inc=Dq, inc_n=16)
                p.add("vector", lambda e: e.tensor_tensor(stage[:, 0:4096], a_h[:, 4096:8192],
                      a_l[:, 4096:8192], op=OP.add), waits=[(Dq, p.c(Dq))], inc=V_)
                p.add("vector", lambda e: e.tensor_tensor(stage[:, 4096:8192], b_h[:, 4096:8192],
                      b_l[:, 4096:8192], op=OP.add), inc=V_)
                v_dbg = p.c(V_)
                p.add("sync", lambda e: e.dma_start(out=y_out[:, 8192:16384], in_=stage[:, :]),
                      waits=[(V_, v_dbg)], inc=Dq, inc_n=16)
                return

            s_exp_bank = {}
            v_pl_buf = {}
            t_p_use = {}
            v_dn_free = {}
            v_pv_free = {}
            dq_osb = {}
            for qt in range(16):
                nkt = 4 * qt + 4
                dnb = 3 + qt % 2
                pvb = 5 + qt % 2
                for kt in range(nkt):
                    bp = kt % 4
                    bsc = kt % 3
                    w0 = [(S_, s_exp_bank.get(bsc, 0))]
                    if s_exp_bank.get(bsc, 0) == 0:
                        w0 += [(V_, v_prep), (S_, s_prep)]
                    if kt == 0:
                        w0 += [(V_, v_pv_free.get(pvb, 0)), (V_, v_dn_free.get(dnb, 0))]
                    mm(bank(bsc), kh_fm[:, kt * 128:(kt + 1) * 128],
                       qh_fm[:, qt * 512:(qt + 1) * 512], True, False, waits=w0)
                    mm(bank(bsc), kh_fm[:, kt * 128:(kt + 1) * 128],
                       ql_fm[:, qt * 512:(qt + 1) * 512], False, False)
                    t_sc = mm(bank(bsc), kl_fm[:, kt * 128:(kt + 1) * 128],
                              qh_fm[:, qt * 512:(qt + 1) * 512], False, True, inc=T_)
                    s_exp = p.add("scalar", (lambda e, bp=bp, bsc=bsc: e.activation(
                        p32[bp][:, :], bank(bsc), AF.Exp)),
                        waits=[(T_, t_sc), (V_, v_pl_buf.get(bp, 0))], inc=S_)
                    s_exp_bank[bsc] = s_exp
                    diag = kt >= 4 * qt
                    guard_T = [(T_, t_p_use.get(bp, 0))]
                    if diag:
                        dj = kt - 4 * qt
                        v_mask = p.add("vector", (lambda e, bp=bp, dj=dj: e.tensor_tensor(
                            p32[bp][:, :], p32[bp][:, :],
                            masks[:, dj * 512:(dj + 1) * 512], op=OP.mult)),
                            waits=[(S_, s_exp)], inc=V_)
                        s_ph = p.add("scalar", (lambda e, bp=bp: e.copy(phb[bp][:, :], p32[bp][:, :])),
                                     waits=[(V_, v_mask)] + guard_T, inc=S_)
                        ph_dep = (S_, s_ph)
                    else:
                        v_ph = p.add("vector", (lambda e, bp=bp: e.tensor_scalar(
                            phb[bp][:, :], p32[bp][:, :], 1.0, None,
                            op0=OP.mult, op1=OP.bypass)),
                            waits=[(S_, s_exp)] + guard_T, inc=V_)
                        ph_dep = (V_, v_ph)
                    v_pl = p.add("vector", (lambda e, bp=bp: e.tensor_tensor(
                        plb[bp][:, :], p32[bp][:, :], phb[bp][:, :], op=OP.subtract)),
                        waits=[ph_dep], inc=V_)
                    v_pl_buf[bp] = v_pl
                    mm(bank(dnb), ones128_bf[:, :], phb[bp][:, :], kt == 0, False,
                       waits=[ph_dep])
                    t_dn = mm(bank(dnb), ones128_bf[:, :], plb[bp][:, :], False, kt == nkt - 1,
                              waits=[(V_, v_pl)], inc=T_ if kt == nkt - 1 else None)
                    mm(bank(pvb), vh_tm[:, kt * 128:(kt + 1) * 128], phb[bp][:, :], kt == 0, False)
                    mm(bank(pvb), vh_tm[:, kt * 128:(kt + 1) * 128], plb[bp][:, :], False, False)
                    t_pv = mm(bank(pvb), vl_tm[:, kt * 128:(kt + 1) * 128], phb[bp][:, :],
                              False, kt == nkt - 1, inc=T_)
                    t_p_use[bp] = t_pv
                qtp = qt % 2
                v_rc = p.add("vector", (lambda e, qtp=qtp, dnb=dnb: e.reciprocal(
                    rc_sb[qtp][:, :], bank(dnb))), waits=[(T_, t_dn)], inc=V_)
                v_dn_free[dnb] = v_rc
                v_rcn = p.add("vector", (lambda e, qtp=qtp: e.tensor_scalar(
                    rc_sb[qtp][:, :], rc_sb[qtp][:, :], nu_bc[:, :], None,
                    op0=OP.mult, op1=OP.bypass)), waits=[(V_, v_rc)], inc=V_)
                v_pv_evac = p.add("vector", (lambda e, qtp=qtp, pvb=pvb: e.scalar_tensor_tensor(
                    osb[qtp][:, :], bank(pvb), 1.0, rc_sb[qtp][:, :],
                    op0=OP.mult, op1=OP.mult)),
                    waits=[(V_, v_rcn), (T_, t_pv), (Dq, dq_osb.get(qtp, 0))], inc=V_)
                v_pv_free[pvb] = v_pv_evac
                p.add("sync", (lambda e, qtp=qtp, qt=qt, hl=hl: e.dma_start(
                    out=a2a_in.ap()[256 * (qt // 2) + 128 * hl: 256 * (qt // 2) + 128 * (hl + 1),
                                    (qt % 2) * 512:(qt % 2 + 1) * 512],
                    in_=osb[qtp][:, :])), waits=[(V_, v_pv_evac)], inc=Dq, inc_n=16)
                dq_osb[qtp] = p.c(Dq)

        def phase4_a2a():
            dq_all = p.c(Dq)
            if DEBUG_TAP == "dbg2":
                dbv = dbg_dram.ap().rearrange("(f p) t -> p f t", p=128)
                p.add("sync", (lambda e: e.dma_start(
                    out=y_out.rearrange("p (f t) -> p f t", f=NFC), in_=dbv)),
                    waits=[(Dq, dq_all)], inc=Dq, inc_n=16)
                dq_all = p.c(Dq)
            if DEBUG_TAP == "ag2":
                p.add("gpsimd", (lambda e: e.dma_start(
                    out=y_out.rearrange("p t -> p t"), in_=ago[2, 0])),
                    waits=[(Dq, dq_all)], inc=Dg, inc_n=16)
            if DEBUG_TAP == "a2ain3":
                aiv3 = a2a_in.ap().rearrange("(f p) t -> p f t", p=128)
                p.add("sync", (lambda e: e.dma_start(
                    out=y_out.rearrange("p (f t) -> p f t", f=NFC), in_=aiv3)),
                    waits=[(Dq, dq_all)], inc=Dq, inc_n=16)
                dq_all = p.c(Dq)
            if DEBUG_TAP == "p0":
                p.add("sync", (lambda e: e.dma_start(
                    out=AR3[:, 13568:14080], in_=a2a_in.ap()[0:128, 0:512])),
                    waits=[(Dq, dq_all)], inc=Dq, inc_n=16)
                dq_b = p.c(Dq)
                p.add("sync", (lambda e: e.dma_start(
                    out=y_out[:, 4608:5120], in_=AR3[:, 13568:14080])),
                    waits=[(Dq, dq_b)], inc=Dq, inc_n=16)
                dq_all = p.c(Dq)

            def _a2a(e):
                with allow_ici_all_to_all():
                    return e.collective_compute(
                        "AllToAll", OP.bypass, replica_groups=[list(range(NCORE))],
                        ins=[a2a_in.ap().opt()], outs=[a2a_out.ap().opt()])
            p.add("gpsimd", _a2a, waits=[(Dq, dq_all)], inc=CC)
            cc_n = p.c(CC)
            if DEBUG_TAP == "a2aout":
                p.add("gpsimd", (lambda e: e.dma_start(
                    out=y_out.rearrange("p (f t) -> p f t", f=NFC),
                    in_=a2a_out.ap().rearrange("(f p) t -> p f t", p=128))),
                    waits=[(CC, cc_n)], inc=Dg, inc_n=16)
                return
            a2v = a2a_out.ap().rearrange("(f p) t -> p f t", p=128)
            for fc in range(NFC):
                p.add("gpsimd", (lambda e, fc=fc: e.dma_start(
                    out=xres[:, fc * 1024:(fc + 1) * 1024], in_=a2v[:, fc, :],
                    accum_op=OP.add)),
                    waits=[(CC, cc_n)] if fc == 0 else (), inc=Dg, inc_n=16)

        init()
        p.barrier()
        layer_norm()
        p.barrier()
        if DEBUG_TAP == "ln1":
            for fc in range(NFC):
                v_add = p.add("vector", (lambda e, fc=fc:
                              e.tensor_tensor(xa[:, fc * 1024:(fc + 1) * 1024],
                                              yh[:, fc * 1024:(fc + 1) * 1024],
                                              yl[:, fc * 1024:(fc + 1) * 1024], op=OP.add)), inc=V_)
            p.add("gpsimd", lambda e: e.dma_start(out=xres[:, :], in_=xa[:, :]),
                  waits=[(V_, p.c(V_))], inc=Dg, inc_n=16)
            p.barrier()
        else:
            ffn()
            p.barrier()
            if PHASES >= 2:
                layer_norm(load_first=True)
                phase2_ag()
                p.barrier()
            if DEBUG_TAP == "ag":
                cc_need = p.c(CC)
                p.add("gpsimd", (lambda e: e.dma_start(
                    out=y_out.rearrange("p t -> p t"), in_=ago[2, 0])),
                    waits=[(CC, cc_need)], inc=Dg, inc_n=16)
            if PHASES >= 3 and DEBUG_TAP != "ag":
                cc_need = p.c(CC)
                attn_head(0, cc_need)
                p.barrier()
                if DEBUG_TAP not in ("qk", "v", "vraw", "scl"):
                    attn_head(1, cc_need)
                    p.barrier()
            if DEBUG_TAP == "a2ain2":
                aiv2 = a2a_in.ap().rearrange("(f p) t -> p f t", p=128)
                p.add("gpsimd", (lambda e: e.dma_start(
                    out=y_out.rearrange("p (f t) -> p f t", f=NFC), in_=aiv2)),
                    waits=[(CC, p.c(CC))], inc=Dg, inc_n=16)
            if DEBUG_TAP == "a2a":
                aiv = a2a_in.ap().rearrange("(f p) t -> p f t", p=128)
                p.add("sync", lambda e: e.dma_start(
                    out=y_out.rearrange("p (f t) -> p f t", f=NFC), in_=aiv),
                    inc=Dq, inc_n=16)
            if PHASES >= 4 and DEBUG_TAP not in ("a2a",):
                phase4_a2a()
                p.barrier()
            if PHASES >= 5:
                layer_norm(load_first=True)
                p.barrier()
                ffn()
                p.barrier()

        if DEBUG_TAP not in ("a2a", "qk", "v", "vraw", "scl", "ag", "a2aout", "a2ain2", "a2ain3", "ag2", "qk2", "dbg2", "v2", "p0"):
            p.add("sync", lambda e: e.dma_start(out=xa[:, :], in_=xres[:, :]), inc=Dq, inc_n=16)
            dq_ld = p.c(Dq)
            p.add("sync", lambda e: e.dma_start(out=y_out[:, :], in_=xa[:, :]),
                  waits=[(Dq, dq_ld)], inc=Dq, inc_n=16)
        dq_st = p.c(Dq)
        p.add("gpsimd", lambda e: e.nop(), waits=[(Dq, dq_st)])

        stats = p.emit(block, sems)
        print("emit stats:", {k: tuple(v) for k, v in stats.items()}, "matmuls:", nmm[0])
    return nc


def stage_inputs(inputs):
    x = np.asarray(inputs["L0_attn_x"])[0]
    wq = np.asarray(inputs["L0_attn_wq"])
    wk = np.asarray(inputs["L0_attn_wk"])
    wv = np.asarray(inputs["L0_attn_wv"])
    tau = np.asarray(inputs["L0_attn_tau"]).reshape(1, 1)
    nu = np.asarray(inputs["L0_attn_nu"]).reshape(1, 1)
    wg = np.asarray(inputs["L0_ffn_w_gate"])
    wu = np.asarray(inputs["L0_ffn_w_up"])
    wd = np.asarray(inputs["L0_ffn_w_down"])
    sg = np.asarray(inputs["L0_ffn_s_gate"])
    su = np.asarray(inputs["L0_ffn_s_up"])
    sd = np.asarray(inputs["L0_ffn_s_down"])

    bf = ml_dtypes.bfloat16
    wg_s = np.ascontiguousarray(
        wg.reshape(NFC, 128, 16, 512).transpose(2, 1, 0, 3).reshape(16, 128, NFC * 512))
    wu_s = np.ascontiguousarray(
        wu.reshape(NFC, 128, 16, 512).transpose(2, 1, 0, 3).reshape(16, 128, NFC * 512))
    wd_s = np.ascontiguousarray(
        wd.reshape(8, 8, 128, 16, 128).transpose(0, 3, 2, 1, 4).reshape(8, 16, 128, 8 * 128))
    sg_s = np.ascontiguousarray(sg.reshape(64, 128).T)
    su_s = np.ascontiguousarray(su.reshape(64, 128).T)
    sd_s = np.ascontiguousarray(sd.reshape(16, 128).T)
    mask = np.ascontiguousarray(
        np.triu(np.ones((512, 512), np.float32)).reshape(4, 128, 512)
        .transpose(1, 0, 2).reshape(128, 4 * 512)).astype(bf)
    eye = np.eye(128, dtype=np.float32).astype(bf)

    in_maps = []
    for c in range(NCORE):
        rows = slice(c * TOK, (c + 1) * TOK)
        x_slab = np.ascontiguousarray(
            x[rows].T.reshape(NFC, 128, TOK).transpose(1, 0, 2).reshape(128, NFC * TOK))
        hc = slice(c * H_PER * DH, (c + 1) * H_PER * DH)
        wq_s = np.ascontiguousarray(wq[hc, :].T.reshape(NFC, 128, H_PER * DH).transpose(1, 0, 2))
        wk_s = np.ascontiguousarray(wk[hc, :].T.reshape(NFC, 128, H_PER * DH).transpose(1, 0, 2))
        wv_s = np.ascontiguousarray(wv[hc, :].T.reshape(NFC, 128, H_PER * DH).transpose(1, 0, 2))
        in_maps.append({
            "x": x_slab, "wq": wq_s, "wk": wk_s, "wv": wv_s,
            "tau": tau, "nu": nu,
            "wg": wg_s, "wu": wu_s, "wd": wd_s,
            "sg": sg_s, "su": su_s, "sd": sd_s,
            "mask": mask, "eye": eye,
        })
    return in_maps


def unstage_output(results):
    outs = []
    for c in range(NCORE):
        slab = results[c]["y"]
        xt = slab.reshape(128, NFC, TOK).transpose(1, 0, 2).reshape(C, TOK)
        outs.append(np.ascontiguousarray(xt.T))
    return np.concatenate(outs, axis=0)[None]


def kernel(**inputs):
    in_maps = stage_inputs(inputs)
    nc = build_nc()
    res = run_bass_kernel_spmd(nc, in_maps, core_ids=list(range(NCORE)))
    return unstage_output(res.results)


# version 05
# speedup vs baseline: 1.3451x; 1.3148x over previous
import sys
if "/opt/trn_rl_repo" not in sys.path:
    sys.path.insert(0, "/opt/trn_rl_repo")
import numpy as np
import ml_dtypes
import concourse.bass as bass
import concourse.mybir as mybir
from concourse.bass_utils import run_bass_kernel_spmd
from concourse.aot_env import allow_ici_all_to_all

F32 = mybir.dt.float32
BF16 = mybir.dt.bfloat16
I8 = mybir.dt.int8
AF = mybir.ActivationFunctionType
OP = mybir.AluOpType

NCORE = 8
C = 2048
TOK = 1024
T = 8192
INTER = 8192
H_PER = 2
DH = 128
NFC = C // 128
EPS_LN = 1e-5
EPS_NRM = 1e-8

PHASES = 6


class Plan:

    ENGS = ("tensor", "vector", "scalar", "sync", "gpsimd")

    def __init__(self):
        self.ops = {e: [] for e in self.ENGS}
        self.cnt = {}
        self.sem_names = []

    def sem(self, name):
        if name not in self.cnt:
            self.cnt[name] = 0
            self.sem_names.append(name)
        return name

    def c(self, sem):
        return self.cnt.get(sem, 0)

    def add(self, eng, fn, waits=(), inc=None, inc_n=1):
        for s, v in waits:
            assert v <= self.cnt.get(s, 0), (s, v, self.cnt.get(s, 0))
        if inc is not None:
            self.sem(inc)
            self.cnt[inc] += inc_n
        self.ops[eng].append((fn, tuple(waits), inc, inc_n))
        return self.cnt.get(inc, 0) if inc is not None else None

    def barrier(self):
        snap = dict(self.cnt)
        eng_sem = {"tensor": "T", "vector": "V", "scalar": "S",
                   "sync": "Dq", "gpsimd": "Dg"}
        for e in self.ENGS:
            waits = tuple((s, v) for s, v in snap.items()
                          if v > 0 and s != eng_sem.get(e))
            self.ops[e].append((None, waits, None, 0))

    def emit(self, block, sems):
        last_wait = {e: {} for e in self.ENGS}
        stats = {e: [0, 0] for e in self.ENGS}

        def emit_for(eng_name):
            def go(eng):
                lw = last_wait[eng_name]
                for fn, waits, inc, inc_n in self.ops[eng_name]:
                    for s, v in waits:
                        if lw.get(s, -1) >= v:
                            continue
                        eng.wait_ge(sems[s], v)
                        lw[s] = v
                        stats[eng_name][1] += 1
                    if fn is None:
                        continue
                    instr = fn(eng)
                    stats[eng_name][0] += 1
                    if inc is not None:
                        instr.then_inc(sems[inc], inc_n)
            return go

        for e in self.ENGS:
            getattr(block, e)(emit_for(e))
        return stats


def build_nc():
    nc = bass.Bass("TRN2", target_bir_lowering=False, debug=False)
    p = Plan()
    T_, V_, S_, Dq, Dg, CC = (p.sem(s) for s in ("T", "V", "S", "Dq", "Dg", "CC"))

    x_in = nc.dram_tensor("x", [128, NFC * TOK], F32, kind="ExternalInput").ap()
    wq_in = nc.dram_tensor("wq", [128, NFC, H_PER * DH], F32, kind="ExternalInput").ap()
    wk_in = nc.dram_tensor("wk", [128, NFC, H_PER * DH], F32, kind="ExternalInput").ap()
    wv_in = nc.dram_tensor("wv", [128, NFC, H_PER * DH], F32, kind="ExternalInput").ap()
    tau_in = nc.dram_tensor("tau", [1, 1], F32, kind="ExternalInput").ap()
    nu_in = nc.dram_tensor("nu", [1, 1], F32, kind="ExternalInput").ap()
    wg_in = nc.dram_tensor("wg", [16, 128, NFC * 512], I8, kind="ExternalInput").ap()
    wu_in = nc.dram_tensor("wu", [16, 128, NFC * 512], I8, kind="ExternalInput").ap()
    wd_in = nc.dram_tensor("wd", [8, 16, 128, 8 * 128], I8, kind="ExternalInput").ap()
    sg_in = nc.dram_tensor("sg", [128, 64], F32, kind="ExternalInput").ap()
    su_in = nc.dram_tensor("su", [128, 64], F32, kind="ExternalInput").ap()
    sd_in = nc.dram_tensor("sd", [128, 16], F32, kind="ExternalInput").ap()
    mask_in = nc.dram_tensor("mask", [128, 4 * 512], BF16, kind="ExternalInput").ap()
    eye_in = nc.dram_tensor("eye", [128, 128], BF16, kind="ExternalInput").ap()
    y_out = nc.dram_tensor("y", [128, NFC * TOK], F32, kind="ExternalOutput").ap()

    xres = nc.dram_tensor("xres", [128, NFC * TOK], F32).ap()
    ag_in_a = nc.dram_tensor("ag_in_a", [2 * 128, NFC * 512], BF16)
    ag_in_b = nc.dram_tensor("ag_in_b", [2 * 128, NFC * 512], BF16)
    ag_out_a = nc.dram_tensor("ag_out_a", [NCORE * 2 * 128, NFC * 512], BF16)
    ag_out_b = nc.dram_tensor("ag_out_b", [NCORE * 2 * 128, NFC * 512], BF16)
    a2a_in = [nc.dram_tensor("a2a_in0", [C // 2, TOK], F32),
              nc.dram_tensor("a2a_in1", [C // 2, TOK], F32)]
    a2a_out = [nc.dram_tensor("a2a_out0", [C // 2, TOK], F32),
               nc.dram_tensor("a2a_out1", [C // 2, TOK], F32)]

    with (
        nc.sbuf_tensor("AR1", [128, 16384], F32) as AR1,
        nc.sbuf_tensor("AR2", [128, 16384], F32) as AR2,
        nc.sbuf_tensor("AR3", [128, 16384], F32) as AR3,
        nc.sbuf_tensor("AR4", [128, 2816], F32) as AR4,
        nc.psum_tensor("PS", [128, 4096], F32) as PS,
        nc.semaphore("sT") as sT, nc.semaphore("sV") as sV,
        nc.semaphore("sS") as sS, nc.semaphore("sDq") as sDq,
        nc.semaphore("sDg") as sDg, nc.semaphore("sCC") as sCC,
        nc.Block() as block,
    ):
        sems = {"T": sT, "V": sV, "S": sS, "Dq": sDq, "Dg": sDg, "CC": sCC}

        def bank(i, width=512, off=0):
            return PS[:, i * 512 + off: i * 512 + off + width]

        def bank_row(i, width=512):
            return PS[0:1, i * 512: i * 512 + width]

        xa = AR1
        wgb = [AR1[:, 0:4096].bitcast(BF16), AR1[:, 4096:8192].bitcast(BF16)]
        wub = [AR1[:, 8192:12288].bitcast(BF16), AR1[:, 12288:16384].bitcast(BF16)]
        qh_fm = AR1[:, 0:4096].bitcast(BF16)
        ql_fm = AR1[:, 4096:8192].bitcast(BF16)
        kh_fm = AR1[:, 8192:12288].bitcast(BF16)
        kl_fm = AR1[:, 12288:16384].bitcast(BF16)
        yh = AR2[:, 0:8192].bitcast(BF16)
        yl = AR2[:, 8192:16384].bitcast(BF16)
        vh_tm = AR2[:, 0:4096].bitcast(BF16)
        vl_tm = AR2[:, 4096:8192].bitcast(BF16)
        ysb = [(AR2[:, 8192:10240].bitcast(BF16), AR2[:, 10240:12288].bitcast(BF16)),
               (AR2[:, 12288:14336].bitcast(BF16), AR2[:, 14336:16384].bitcast(BF16))]
        x2t = [AR3[:, 0:1024], AR3[:, 1024:2048]]
        bca = AR3[:, 2048:3072]
        bcb = AR3[:, 3072:4096]
        lst = [AR3[:, 4096:4608], AR3[:, 4608:5120]]
        lsy = [AR3[:, 5120:5632], AR3[:, 5632:6144]]
        fh = AR3[:, 0:4096].bitcast(BF16)
        fl = AR3[:, 4096:8192].bitcast(BF16)
        wdb = [AR3[:, 8192:8704].bitcast(BF16), AR3[:, 8704:9216].bitcast(BF16)]
        gb = [AR3[:, 9216:9728], AR3[:, 9728:10240]]
        ub = [AR3[:, 10240:10752], AR3[:, 10752:11264]]
        fb = [AR3[:, 11264:11776], AR3[:, 11776:12288]]
        dsb = [AR3[:, 12288:12800], AR3[:, 12800:13312]]
        w32 = AR3[:, 0:2048]
        wsh = AR3[:, 2048:5120].bitcast(BF16)
        wsl = AR3[:, 5120:8192].bitcast(BF16)
        psq = AR3[:, 8192:8576]
        nrmt = AR3[:, 8576:8608]
        p32 = [AR3[:, 8704 + 512 * i: 9216 + 512 * i] for i in range(4)]
        phb = [AR3[:, 10752 + 256 * i: 11008 + 256 * i].bitcast(BF16) for i in range(4)]
        plb = [AR3[:, 11776 + 256 * i: 12032 + 256 * i].bitcast(BF16) for i in range(4)]
        rc_sb = [AR3[:, 12800:13312], AR3[:, 13312:13824]]
        osb = [AR3[:, 13824:14336], AR3[:, 14336:14848]]
        ttt = [AR3[:, 14848 + 64 * i: 14912 + 64 * i].bitcast(BF16) for i in range(4)]
        masks = AR4[:, 0:1024].bitcast(BF16)
        eye = AR4[:, 1024:1088].bitcast(BF16)
        ones_bf = AR4[:, 1088:1089].bitcast(BF16)[:, 0:1]
        ones_f = AR4[:, 1090:1091]
        ones_row = AR4[0:1, 1092:1220]
        sg_sb = AR4[:, 1220:1284]
        su_sb = AR4[:, 1284:1348]
        sdh_sb = AR4[:, 1348:1364]
        sd_sb = AR4[:, 1364:1380]
        tau_bc = AR4[:, 1380:1381]
        nu_sb = AR4[0:1, 1382:1383]
        tau_sb = AR4[0:1, 1384:1385]
        ones128_bf = AR4[:, 1408:1472].bitcast(BF16)
        ones128_f = AR4[:, 1472:1600]
        nu_bc = AR4[:, 1600:1601]
        r_s1 = AR3[0:1, 6144:7168]
        r_s2 = AR3[0:1, 7168:8192]
        r_t0 = AR3[0:1, 8192:9216]
        r_a = AR3[0:1, 9216:10240]
        r_b = AR3[0:1, 10240:11264]
        r_m = r_s1

        nmm = [0]

        def mm(out, lhsT, rhs, start, stop, waits=(), inc=None):
            def f(e, out=out, lhsT=lhsT, rhs=rhs, start=start, stop=stop):
                return e.matmul(out, lhsT, rhs, start=start, stop=stop)
            nmm[0] += 1
            return p.add("tensor", f, waits=waits, inc=inc)

        def init():
            p.add("sync", lambda e: e.dma_start(out=xa[:, :], in_=x_in[:, :]), inc=Dq, inc_n=16)
            p.add("sync", lambda e: e.dma_start(out=masks[:, :], in_=mask_in[:, :]), inc=Dq, inc_n=16)
            p.add("sync", lambda e: e.dma_start(out=eye[:, :], in_=eye_in[:, :]), inc=Dq, inc_n=16)
            p.add("sync", lambda e: e.dma_start(out=sg_sb[:, :], in_=sg_in[:, :]), inc=Dq, inc_n=16)
            p.add("sync", lambda e: e.dma_start(out=su_sb[:, :], in_=su_in[:, :]), inc=Dq, inc_n=16)
            p.add("sync", lambda e: e.dma_start(out=sd_sb[:, :], in_=sd_in[:, :]), inc=Dq, inc_n=16)
            p.add("sync", lambda e: e.dma_start(out=nu_sb[:, :], in_=nu_in[:, :]), inc=Dq, inc_n=16)
            p.add("sync", lambda e: e.dma_start(out=tau_sb[:, :], in_=tau_in[:, :]), inc=Dq, inc_n=16)
            dq0 = p.c(Dq)
            p.add("vector", lambda e: e.memset(ones_bf[:, :], 1.0), inc=V_)
            p.add("vector", lambda e: e.memset(ones_f[:, :], 1.0), inc=V_)
            p.add("vector", lambda e: e.memset(ones_row[:, :], 1.0), inc=V_)
            p.add("vector", lambda e: e.tensor_scalar_mul(sdh_sb[:, :], sd_sb[:, :], 0.5),
                  waits=[(Dq, dq0)], inc=V_)
            p.add("vector", lambda e: e.memset(ones128_bf[:, :], 1.0), inc=V_)
            p.add("vector", lambda e: e.memset(ones128_f[:, :], 1.0), inc=V_)
            mm(bank(7, 1), ones_row[:, :], tau_sb[:, :], True, True,
               waits=[(V_, p.c(V_)), (Dq, dq0)], inc=T_)
            mm(bank(6, 1), ones_row[:, :], nu_sb[:, :], True, True, inc=T_)
            p.add("scalar", lambda e: e.copy(tau_bc[:, :], bank(7, 1)),
                  waits=[(T_, p.c(T_))], inc=S_)
            p.add("scalar", lambda e: e.copy(nu_bc[:, :], bank(6, 1)), inc=S_)
            p.add("gpsimd", lambda e: e.dma_start(out=xres[:, :], in_=xa[:, :]),
                  waits=[(Dq, dq0)], inc=Dg, inc_n=16)

        def layer_norm(load_first=False):
            if load_first:
                p.add("sync", lambda e: e.dma_start(out=xa[:, :], in_=xres[:, :]),
                      inc=Dq, inc_n=16)
                dq_x = p.c(Dq)
                p.ops["tensor"].append((None, ((Dq, dq_x),), None, 0))
                p.ops["scalar"].append((None, ((Dq, dq_x),), None, 0))
                p.ops["vector"].append((None, ((Dq, dq_x),), None, 0))
            t_sq = {}
            for fc in range(NFC):
                xr = xa[:, fc * 1024:(fc + 1) * 1024]
                b = fc % 2
                w = [(T_, t_sq[fc - 2])] if fc >= 2 else []
                t_sq[fc] = None
                s_done = p.add("scalar", (lambda e, o=x2t[b], i=xr: e.square(o, i)),
                               waits=w, inc=S_)
                for tt in range(2):
                    mm(bank(0 + tt), ones128_f[:, :], xr[:, tt * 512:(tt + 1) * 512],
                       fc == 0, fc == NFC - 1)
                    t_sq[fc] = mm(bank(2 + tt), ones128_f[:, :],
                                  x2t[b][:, tt * 512:(tt + 1) * 512],
                                  fc == 0, fc == NFC - 1,
                                  waits=[(S_, s_done)], inc=T_ if tt == 1 else None)
            t_stats = p.c(T_)
            p.add("vector", lambda e: e.tensor_copy(r_s1[:, 0:512], bank_row(0)), waits=[(T_, t_stats)], inc=V_)
            p.add("vector", lambda e: e.tensor_copy(r_s1[:, 512:1024], bank_row(1)), inc=V_)
            p.add("vector", lambda e: e.tensor_copy(r_s2[:, 0:512], bank_row(2)), inc=V_)
            p.add("vector", lambda e: e.tensor_copy(r_s2[:, 512:1024], bank_row(3)), inc=V_)
            p.add("vector", lambda e: e.tensor_scalar_mul(r_m[:, :], r_s1[:, :], 1.0 / C), inc=V_)
            p.add("vector", lambda e: e.tensor_tensor(r_t0[:, :], r_m[:, :], r_m[:, :], op=OP.mult), inc=V_)
            p.add("vector", lambda e: e.tensor_scalar(r_s2[:, :], r_s2[:, :], 1.0 / C, EPS_LN,
                                                      op0=OP.mult, op1=OP.add), inc=V_)
            v_pre = p.add("vector", lambda e: e.tensor_tensor(r_t0[:, :], r_s2[:, :], r_t0[:, :],
                                                              op=OP.subtract), inc=V_)
            s_sqrt = p.add("scalar", lambda e: e.sqrt(r_t0[:, :], r_t0[:, :]),
                           waits=[(V_, v_pre)], inc=S_)
            p.add("vector", lambda e: e.reciprocal(r_a[:, :], r_t0[:, :]),
                  waits=[(S_, s_sqrt)], inc=V_)
            p.add("vector", lambda e: e.tensor_tensor(r_b[:, :], r_m[:, :], r_a[:, :], op=OP.mult), inc=V_)
            v_rows = p.add("vector", lambda e: e.tensor_scalar_mul(r_b[:, :], r_b[:, :], -1.0), inc=V_)
            for tt in range(2):
                mm(bank(4 + 2 * tt), ones_row[:, :], r_a[:, tt * 512:(tt + 1) * 512], True, True,
                   waits=[(V_, v_rows)])
                t_bc = mm(bank(5 + 2 * tt), ones_row[:, :], r_b[:, tt * 512:(tt + 1) * 512],
                          True, True, inc=T_)
                p.add("scalar", (lambda e, tt=tt: e.copy(bca[:, tt * 512:(tt + 1) * 512], bank(4 + 2 * tt))),
                      waits=[(T_, t_bc)], inc=S_)
                p.add("scalar", (lambda e, tt=tt: e.copy(bcb[:, tt * 512:(tt + 1) * 512], bank(5 + 2 * tt))),
                      inc=S_)
            s_bc = p.c(S_)
            for fc in range(NFC):
                for tt in range(2):
                    xrt = xa[:, fc * 1024 + tt * 512: fc * 1024 + tt * 512 + 512]
                    st = tt
                    yh_t = yh[:, fc * 1024 + tt * 512: fc * 1024 + tt * 512 + 512]
                    yl_t = yl[:, fc * 1024 + tt * 512: fc * 1024 + tt * 512 + 512]
                    p.add("vector", (lambda e, o=lst[st], i=xrt, b2=bca[:, tt * 512:(tt + 1) * 512]:
                                     e.tensor_tensor(o, i, b2, op=OP.mult)),
                          waits=[(S_, s_bc)], inc=V_)
                    v_y = p.add("vector", (lambda e, o=lsy[st], i=lst[st], b2=bcb[:, tt * 512:(tt + 1) * 512]:
                                           e.tensor_tensor(o, i, b2, op=OP.add)), inc=V_)
                    s_h = p.add("scalar", (lambda e, o=yh_t, i=lsy[st]: e.copy(o, i)),
                                waits=[(V_, v_y)], inc=S_)
                    p.add("vector", (lambda e, o=yl_t, i=lsy[st], h=yh_t:
                                     e.tensor_tensor(o, i, h, op=OP.subtract)),
                          waits=[(S_, s_h)], inc=V_)

        def ffn():
            t_gate_last = {}
            t_up_last = {}
            t_down_last = {}
            s_evac = {}
            v_evac = {}
            s_fuse = 0
            v_fuse = 0
            dg_w = {}
            dg_wd = {}
            s_dsb = {}
            dg_dsb = {}
            y_ready = (V_, p.c(V_))

            def load_w(cidx):
                buf = cidx % 2
                w = []
                if cidx >= 2:
                    w = [(T_, t_gate_last.get(buf, 0))]
                p.add("gpsimd", (lambda e, b=buf, ci=cidx:
                                 e.dma_start(out=wgb[b][:, :], in_=wg_in[ci, :, :])),
                      waits=w, inc=Dg, inc_n=16)
                if cidx >= 2:
                    w = [(T_, t_up_last.get(buf, 0))]
                dg_w[cidx] = p.add("gpsimd", (lambda e, b=buf, ci=cidx:
                                              e.dma_start(out=wub[b][:, :], in_=wu_in[ci, :, :])),
                                   waits=w, inc=Dg, inc_n=16)

            load_w(0)
            load_w(1)
            for blk in range(8):
                for ic in range(2):
                    cidx = blk * 2 + ic
                    buf = cidx % 2
                    wgv = wgb[buf].rearrange("p (f i) -> p f i", f=NFC)
                    wuv = wub[buf].rearrange("p (f i) -> p f i", f=NFC)
                    for sub in range(4):
                        ig = cidx * 4 + sub
                        sl = ic * 4 + sub
                        for tt in range(2):
                            rot = (sub * 2 + tt) % 2
                            gbank, ubank = 0 + rot, 2 + rot
                            w = [(Dg, dg_w[cidx]), y_ready,
                                 (S_, s_evac.get(gbank, 0))]
                            for pi, ysl in enumerate((yh, yl)):
                                for fc in range(NFC):
                                    t_g = mm(bank(gbank),
                                             wgv[:, fc, sub * 128:(sub + 1) * 128],
                                             ysl[:, fc * 1024 + tt * 512:][:, 0:512],
                                             pi == 0 and fc == 0, pi == 1 and fc == NFC - 1,
                                             waits=w if (pi == 0 and fc == 0) else (),
                                             inc=T_ if (pi == 1 and fc == NFC - 1) else None)
                            t_gate_last[buf] = t_g
                            w = [(V_, v_evac.get(ubank, 0))]
                            for pi, ysl in enumerate((yh, yl)):
                                for fc in range(NFC):
                                    t_u = mm(bank(ubank),
                                             wuv[:, fc, sub * 128:(sub + 1) * 128],
                                             ysl[:, fc * 1024 + tt * 512:][:, 0:512],
                                             pi == 0 and fc == 0, pi == 1 and fc == NFC - 1,
                                             waits=w if (pi == 0 and fc == 0) else (),
                                             inc=T_ if (pi == 1 and fc == NFC - 1) else None)
                            t_up_last[buf] = t_u
                            st = rot
                            s_g = p.add("scalar",
                                        (lambda e, o=gb[st], b2=gbank, ig=ig:
                                         e.activation(o, bank(b2), AF.Silu, scale=sg_sb[:, ig:ig + 1])),
                                        waits=[(T_, t_g)], inc=S_)
                            s_evac[gbank] = s_g
                            v_u = p.add("vector",
                                        (lambda e, o=ub[st], b2=ubank, ig=ig:
                                         e.tensor_scalar_mul(o, bank(b2), su_sb[:, ig:ig + 1])),
                                        waits=[(T_, t_u)], inc=V_)
                            v_evac[ubank] = v_u
                            v_f = p.add("vector",
                                        (lambda e, o=fb[st], a=gb[st], b2=ub[st]:
                                         e.tensor_tensor(o, a, b2, op=OP.mult)),
                                        waits=[(S_, s_g)], inc=V_)
                            fh_t = fh[:, sl * 1024 + tt * 512:][:, 0:512]
                            fl_t = fl[:, sl * 1024 + tt * 512:][:, 0:512]
                            w_fz = [(V_, v_f)]
                            if blk > 0:
                                w_fz.append((T_, t_blk_down))
                            s_fuse = p.add("scalar",
                                           (lambda e, o=fh_t, i=fb[st]: e.copy(o, i)),
                                           waits=w_fz, inc=S_)
                            v_fuse = p.add("vector",
                                           (lambda e, o=fl_t, i=fb[st], h=fh_t:
                                            e.tensor_tensor(o, i, h, op=OP.subtract)),
                                           waits=[(S_, s_fuse)], inc=V_)
                    if cidx + 2 < 16:
                        load_w(cidx + 2)
                fhv = fh.rearrange("p (s t) -> p s t", s=8)
                flv = fl.rearrange("p (s t) -> p s t", s=8)
                def load_wd(blk2, dmc2):
                    wbuf2 = dmc2 % 2
                    w2 = [(T_, t_down_last.get(wbuf2, 0))]
                    dg_wd[dmc2] = p.add("gpsimd",
                                        (lambda e, b=wbuf2, blk2=blk2, d2=dmc2:
                                         e.dma_start(out=wdb[b][:, :], in_=wd_in[blk2, d2, :, :])),
                                        waits=w2, inc=Dg, inc_n=16)
                if blk == 0:
                    load_wd(0, 0)
                for dmc in range(16):
                    wbuf = dmc % 2
                    wdv = wdb[wbuf].rearrange("p (s i) -> p s i", s=8)
                    for tt in range(2):
                        rot = 4 + (dmc * 2 + tt) % 2
                        w = [(Dg, dg_wd[dmc]), (S_, s_fuse), (V_, v_fuse),
                             (S_, s_dsb.get(rot, 0))]
                        first = True
                        for sub in range(8):
                            for pv, fv in ((0, fhv), (1, flv)):
                                t_d = mm(bank(rot),
                                         wdv[:, sub, :],
                                         fv[:, sub, tt * 512:(tt + 1) * 512],
                                         first, sub == 7 and pv == 1,
                                         waits=w if first else (),
                                         inc=T_ if (sub == 7 and pv == 1) else None)
                                first = False
                        t_down_last[wbuf] = t_d
                        t_blk_down = t_d
                        if tt == 1:
                            if dmc < 15:
                                load_wd(blk, dmc + 1)
                            elif blk < 7:
                                load_wd(blk + 1, 0)
                        dst = (dmc * 2 + tt) % 2
                        s_d = p.add("scalar",
                                    (lambda e, o=dsb[dst], b2=rot, dmc=dmc:
                                     e.activation(o, bank(b2), AF.Copy, scale=sdh_sb[:, dmc:dmc + 1])),
                                    waits=[(T_, t_d), (Dg, dg_dsb.get(dst, 0))], inc=S_)
                        s_dsb[rot] = s_d
                        dg_dsb[dst] = p.add("gpsimd",
                                            (lambda e, i=dsb[dst], dmc=dmc, tt=tt:
                                             e.dma_start(out=xres[:, dmc * 1024 + tt * 512:][:, 0:512],
                                                         in_=i, accum_op=OP.add)),
                                            waits=[(S_, s_d)], inc=Dg, inc_n=16)
            return p.c(Dg)

        agi_a = ag_in_a.ap().rearrange("(s p) (f t) -> s p f t", p=128, f=NFC)
        agi_b = ag_in_b.ap().rearrange("(s p) (f t) -> s p f t", p=128, f=NFC)
        ago_a = ag_out_a.ap().rearrange("(r s p) t -> r s p t", s=2, p=128)
        ago_b = ag_out_b.ap().rearrange("(r s p) t -> r s p t", s=2, p=128)

        def phase2_ag():
            v_last, s_last = p.c(V_), p.c(S_)
            yhv = yh.rearrange("p (f t) -> p f t", f=NFC)
            ylv = yl.rearrange("p (f t) -> p f t", f=NFC)
            p.add("sync", (lambda e: e.dma_start(out=agi_a[0], in_=yhv[:, :, 0:512])),
                  waits=[(V_, v_last), (S_, s_last)], inc=Dq, inc_n=16)
            p.add("sync", (lambda e: e.dma_start(out=agi_a[1], in_=ylv[:, :, 0:512])),
                  inc=Dq, inc_n=16)
            dq_ag = p.c(Dq)
            p.add("gpsimd", (lambda e: e.collective_compute(
                "AllGather", OP.bypass, replica_groups=[list(range(NCORE))],
                ins=[ag_in_a.ap().opt()], outs=[ag_out_a.ap().opt()])),
                waits=[(Dq, dq_ag)], inc=CC)
            p.add("sync", (lambda e: e.dma_start(out=agi_b[0], in_=yhv[:, :, 512:1024])),
                  inc=Dq, inc_n=16)
            p.add("sync", (lambda e: e.dma_start(out=agi_b[1], in_=ylv[:, :, 512:1024])),
                  inc=Dq, inc_n=16)
            dq_ag = p.c(Dq)
            p.add("gpsimd", (lambda e: e.collective_compute(
                "AllGather", OP.bypass, replica_groups=[list(range(NCORE))],
                ins=[ag_in_b.ap().opt()], outs=[ag_out_b.ap().opt()])),
                waits=[(Dq, dq_ag)], inc=CC)

        def attn_head(hl, cc_need):
            prev_use = [(T_, p.c(T_))]
            for ai, w_arr in enumerate((wq_in, wk_in, wv_in)):
                p.add("sync", (lambda e, w=w_arr: e.dma_start(
                    out=w32[:, :], in_=w[:, :, hl * 128:(hl + 1) * 128])),
                    waits=prev_use + ([(V_, p.c(V_))] if ai else []), inc=Dq, inc_n=16)
                dq_w = p.c(Dq)
                wsh_sl = wsh.rearrange("p (f j) -> p f j", f=NFC)[:, :, ai * 128:(ai + 1) * 128]
                wsl_sl = wsl.rearrange("p (f j) -> p f j", f=NFC)[:, :, ai * 128:(ai + 1) * 128]
                s_w = p.add("scalar", (lambda e, o=wsh_sl: e.copy(o, w32.rearrange(
                    "p (f j) -> p f j", f=NFC))), waits=[(Dq, dq_w)], inc=S_)
                v_w = p.add("vector", (lambda e, o=wsl_sl, h=wsh_sl: e.tensor_tensor(
                    o, w32.rearrange("p (f j) -> p f j", f=NFC), h, op=OP.subtract)),
                    waits=[(S_, s_w)], inc=V_)
                prev_use = [(V_, v_w)]
            s_wsplit, v_wsplit = p.c(S_), p.c(V_)

            wsh3 = wsh.rearrange("p (f j) -> p f j", f=NFC)
            wsl3 = wsl.rearrange("p (f j) -> p f j", f=NFC)
            t_y_use = {}
            v_split_bank = {}
            v_rd_prev = 0
            t_tp_use = {}
            s_tp_evac = {}
            for sh in range(NCORE):
                for tq in range(4):
                    gq = sh * 4 + tq
                    yb = gq % 2
                    w = [(CC, cc_need[0] if tq < 2 else cc_need[1])]
                    if gq >= 2:
                        w.append((T_, t_y_use[gq - 2]))
                    agh = ago_a if tq < 2 else ago_b
                    tq2 = tq % 2
                    for s2 in range(2):
                        p.add("sync", (lambda e, sh=sh, tq2=tq2, s2=s2, yb=yb, agh=agh: e.dma_start(
                            out=ysb[yb][s2].rearrange("p (f t) -> p f t", f=NFC),
                            in_=agh[sh, s2].rearrange("p (f t) -> p f t", f=NFC)
                                [:, :, tq2 * 256:(tq2 + 1) * 256])),
                            waits=w if s2 == 0 else (), inc=Dq, inc_n=16)
                    dq_y = p.c(Dq)
                    for t8 in range(2):
                        gt = gq * 2 + t8
                        pr = gt % 2
                        w0 = [(Dq, dq_y), (S_, s_wsplit), (V_, v_wsplit),
                              (V_, v_split_bank.get(pr, 0))]
                        ops48 = [(yb_i, wv3, fc)
                                 for yb_i, wv3 in ((0, wsh3), (1, wsh3), (0, wsl3))
                                 for fc in range(NFC)]
                        for i48, (yb_i, wv3, fc) in enumerate(ops48):
                            last = i48 == len(ops48) - 1
                            t_pj = mm(bank(pr, 384),
                                      ysb[yb][yb_i].rearrange("p (f t) -> p f t", f=NFC)
                                      [:, fc, t8 * 128:(t8 + 1) * 128],
                                      wv3[:, fc, :],
                                      i48 == 0, last,
                                      waits=w0 if i48 == 0 else (),
                                      inc=T_ if last else None)
                        t_y_use[gq] = t_pj
                        s_sq = p.add("scalar", (lambda e, pr=pr: e.square(psq[:, :], bank(pr, 384))),
                                     waits=[(T_, t_pj), (V_, v_rd_prev)], inc=S_)
                        v_rd = p.add("vector", (lambda e: e.tensor_reduce(
                            nrmt[:, 0:3], psq.rearrange("p (a j) -> p a j", a=3),
                            axis=mybir.AxisListType.X, op=OP.add)),
                            waits=[(S_, s_sq)], inc=V_)
                        v_rd_prev = v_rd
                        s_sr = p.add("scalar", lambda e: e.sqrt(nrmt[:, 4:7], nrmt[:, 0:3]),
                                     waits=[(V_, v_rd)], inc=S_)
                        v_ep = p.add("vector", (lambda e: e.tensor_scalar(
                            nrmt[:, 8:11], nrmt[:, 4:7], EPS_NRM, None, op0=OP.add, op1=OP.bypass)),
                            waits=[(S_, s_sr)], inc=V_)
                        p.add("vector", lambda e: e.reciprocal(nrmt[:, 12:15], nrmt[:, 8:11]),
                              waits=[(V_, v_ep)], inc=V_)
                        v_tau = p.add("vector", (lambda e: e.tensor_scalar_mul(
                            nrmt[:, 16:17], nrmt[:, 12:13], tau_bc[:, :])), inc=V_)
                        sc_q, sc_k, sc_v = nrmt[:, 16:17], nrmt[:, 13:14], nrmt[:, 14:15]
                        p.add("vector", (lambda e, pr=pr, gt=gt: e.tensor_scalar(
                            vh_tm[:, gt * 128:(gt + 1) * 128], bank(pr, 128, 256),
                            sc_v, None, op0=OP.mult, op1=OP.bypass)),
                            waits=[(V_, v_tau)], inc=V_)
                        v_vl = p.add("vector", (lambda e, pr=pr, gt=gt: e.scalar_tensor_tensor(
                            vl_tm[:, gt * 128:(gt + 1) * 128], bank(pr, 128, 256),
                            sc_v, vh_tm[:, gt * 128:(gt + 1) * 128],
                            op0=OP.mult, op1=OP.subtract)), inc=V_)
                        v_last_split = v_vl
                        for ai, (dh, dl, scl) in enumerate(((qh_fm, ql_fm, sc_q),
                                                           (kh_fm, kl_fm, sc_k))):
                            th, tl = ttt[2 * ai], ttt[2 * ai + 1]
                            v_h = p.add("vector", (lambda e, pr=pr, ai=ai, th=th, scl=scl:
                                e.tensor_scalar(th[:, :], bank(pr, 128, ai * 128), scl, None,
                                                op0=OP.mult, op1=OP.bypass)),
                                waits=[(T_, t_tp_use.get(2 * ai, 0))], inc=V_)
                            v_l = p.add("vector", (lambda e, pr=pr, ai=ai, th=th, tl=tl, scl=scl:
                                e.scalar_tensor_tensor(tl[:, :], bank(pr, 128, ai * 128),
                                                       scl, th[:, :],
                                                       op0=OP.mult, op1=OP.subtract)),
                                waits=[(T_, t_tp_use.get(2 * ai + 1, 0))], inc=V_)
                            v_last_split = v_l
                            for j, (tsrc, dst, vdep) in enumerate(((th, dh, v_h), (tl, dl, v_l))):
                                tpb = 2 + (2 * ai + j)
                                t_tp = p.add("tensor", (lambda e, tpb=tpb, tsrc=tsrc:
                                    e.transpose(bank(tpb, 64).bitcast(BF16),
                                                tsrc[:, :], eye[:, :])),
                                    waits=[(V_, vdep),
                                           (S_, s_tp_evac.get(tpb, 0))], inc=T_)
                                t_tp_use[2 * ai + j] = t_tp
                                s_tp_evac[tpb] = p.add("scalar", (lambda e, tpb=tpb, dst=dst, gt=gt:
                                    e.copy(dst[:, gt * 128:(gt + 1) * 128],
                                           bank(tpb, 64).bitcast(BF16))),
                                    waits=[(T_, t_tp)], inc=S_)
                        v_split_bank[pr] = v_last_split
            s_prep, v_prep, t_prep = p.c(S_), p.c(V_), p.c(T_)


            s_exp_bank = {}
            v_pl_buf = {}
            t_p_use = {}
            v_dn_free = {}
            v_pv_free = {}
            dq_osb = {}
            t_dn = t_pv = 0
            for qt in range(16):
                nkt = 4 * qt + 4
                dnb = 3 + qt % 2
                pvb = 5 + qt % 2
                pend = None
                for kt in range(nkt):
                    bp = kt % 4
                    bsc = kt % 3
                    w0 = [(S_, s_exp_bank.get(bsc, 0))]
                    if s_exp_bank.get(bsc, 0) == 0:
                        w0 += [(V_, v_prep), (S_, s_prep)]
                    if kt == 0:
                        w0 += [(V_, v_pv_free.get(pvb, 0)), (V_, v_dn_free.get(dnb, 0))]
                    mm(bank(bsc), kh_fm[:, kt * 128:(kt + 1) * 128],
                       qh_fm[:, qt * 512:(qt + 1) * 512], True, False, waits=w0)
                    mm(bank(bsc), kh_fm[:, kt * 128:(kt + 1) * 128],
                       ql_fm[:, qt * 512:(qt + 1) * 512], False, False)
                    t_sc = mm(bank(bsc), kl_fm[:, kt * 128:(kt + 1) * 128],
                              qh_fm[:, qt * 512:(qt + 1) * 512], False, True, inc=T_)
                    if pend is not None:
                        pend()
                        pend = None
                    s_exp = p.add("scalar", (lambda e, bp=bp, bsc=bsc: e.activation(
                        p32[bp][:, :], bank(bsc), AF.Exp)),
                        waits=[(T_, t_sc), (V_, v_pl_buf.get(bp, 0))], inc=S_)
                    s_exp_bank[bsc] = s_exp
                    diag = kt >= 4 * qt
                    guard_T = [(T_, t_p_use.get(bp, 0))]
                    if diag:
                        dj = kt - 4 * qt
                        v_mask = p.add("vector", (lambda e, bp=bp, dj=dj: e.tensor_tensor(
                            p32[bp][:, :], p32[bp][:, :],
                            masks[:, dj * 512:(dj + 1) * 512], op=OP.mult)),
                            waits=[(S_, s_exp)], inc=V_)
                        s_ph = p.add("scalar", (lambda e, bp=bp: e.copy(phb[bp][:, :], p32[bp][:, :])),
                                     waits=[(V_, v_mask)] + guard_T, inc=S_)
                        ph_dep = (S_, s_ph)
                    else:
                        v_ph = p.add("vector", (lambda e, bp=bp: e.tensor_scalar(
                            phb[bp][:, :], p32[bp][:, :], 1.0, None,
                            op0=OP.mult, op1=OP.bypass)),
                            waits=[(S_, s_exp)] + guard_T, inc=V_)
                        ph_dep = (V_, v_ph)
                    v_pl = p.add("vector", (lambda e, bp=bp: e.tensor_tensor(
                        plb[bp][:, :], p32[bp][:, :], phb[bp][:, :], op=OP.subtract)),
                        waits=[ph_dep], inc=V_)
                    v_pl_buf[bp] = v_pl

                    def emit_dnpv(kt=kt, bp=bp, ph_dep=ph_dep, v_pl=v_pl):
                        nonlocal t_dn, t_pv
                        mm(bank(dnb), ones128_bf[:, :], phb[bp][:, :], kt == 0, False,
                           waits=[ph_dep])
                        t_dn = mm(bank(dnb), ones128_bf[:, :], plb[bp][:, :], False,
                                  kt == nkt - 1, waits=[(V_, v_pl)],
                                  inc=T_ if kt == nkt - 1 else None)
                        mm(bank(pvb), vh_tm[:, kt * 128:(kt + 1) * 128], phb[bp][:, :],
                           kt == 0, False)
                        mm(bank(pvb), vh_tm[:, kt * 128:(kt + 1) * 128], plb[bp][:, :],
                           False, False)
                        t_pv = mm(bank(pvb), vl_tm[:, kt * 128:(kt + 1) * 128], phb[bp][:, :],
                                  False, kt == nkt - 1, inc=T_)
                        t_p_use[bp] = t_pv
                    pend = emit_dnpv
                pend()
                pend = None
                qtp = qt % 2
                v_rc = p.add("vector", (lambda e, qtp=qtp, dnb=dnb: e.reciprocal(
                    rc_sb[qtp][:, :], bank(dnb))), waits=[(T_, t_dn)], inc=V_)
                v_dn_free[dnb] = v_rc
                v_rcn = p.add("vector", (lambda e, qtp=qtp: e.tensor_scalar(
                    rc_sb[qtp][:, :], rc_sb[qtp][:, :], nu_bc[:, :], None,
                    op0=OP.mult, op1=OP.bypass)), waits=[(V_, v_rc)], inc=V_)
                v_pv_evac = p.add("vector", (lambda e, qtp=qtp, pvb=pvb: e.scalar_tensor_tensor(
                    osb[qtp][:, :], bank(pvb), 1.0, rc_sb[qtp][:, :],
                    op0=OP.mult, op1=OP.mult)),
                    waits=[(V_, v_rcn), (T_, t_pv), (Dq, dq_osb.get(qtp, 0))], inc=V_)
                v_pv_free[pvb] = v_pv_evac
                p.add("sync", (lambda e, qtp=qtp, qt=qt, hl=hl: e.dma_start(
                    out=a2a_in[hl].ap()[128 * (qt // 2): 128 * (qt // 2) + 128,
                                        (qt % 2) * 512:(qt % 2 + 1) * 512],
                    in_=osb[qtp][:, :])), waits=[(V_, v_pv_evac)], inc=Dq, inc_n=16)
                dq_osb[qtp] = p.c(Dq)

        def phase4_a2a(hl):
            dq_all = p.c(Dq)

            def _a2a(e, hl=hl):
                with allow_ici_all_to_all():
                    return e.collective_compute(
                        "AllToAll", OP.bypass, replica_groups=[list(range(NCORE))],
                        ins=[a2a_in[hl].ap().opt()], outs=[a2a_out[hl].ap().opt()])
            p.add("gpsimd", _a2a, waits=[(Dq, dq_all)], inc=CC)
            cc_n = p.c(CC)
            a2v = a2a_out[hl].ap().rearrange("(i p) t -> p i t", p=128)
            for i in range(NCORE):
                p.add("gpsimd", (lambda e, i=i, hl=hl: e.dma_start(
                    out=xres[:, (2 * i + hl) * 1024:(2 * i + hl + 1) * 1024],
                    in_=a2v[:, i, :], accum_op=OP.add)),
                    waits=[(CC, cc_n)] if i == 0 else (), inc=Dg, inc_n=16)

        init()
        p.barrier()
        layer_norm()
        p.barrier()
        ffn()
        p.barrier()
        if PHASES >= 2:
            layer_norm(load_first=True)
            phase2_ag()
            cc_need = (p.c(CC) - 1, p.c(CC))
            p.barrier()
        if PHASES >= 3:
            attn_head(0, cc_need)
            p.barrier()
            if PHASES >= 4:
                phase4_a2a(0)
            attn_head(1, cc_need)
            if PHASES >= 4:
                phase4_a2a(1)
            p.barrier()
        if PHASES >= 5:
            layer_norm(load_first=True)
            p.barrier()
            ffn()
            p.barrier()

        p.add("sync", lambda e: e.dma_start(out=xa[:, :], in_=xres[:, :]), inc=Dq, inc_n=16)
        dq_ld = p.c(Dq)
        p.add("sync", lambda e: e.dma_start(out=y_out[:, :], in_=xa[:, :]),
              waits=[(Dq, dq_ld)], inc=Dq, inc_n=16)
        dq_st = p.c(Dq)
        p.add("gpsimd", lambda e: e.nop(), waits=[(Dq, dq_st)])

        stats = p.emit(block, sems)
        print("emit stats:", {k: tuple(v) for k, v in stats.items()}, "matmuls:", nmm[0])
    return nc


def stage_inputs(inputs):
    x = np.asarray(inputs["L0_attn_x"])[0]
    wq = np.asarray(inputs["L0_attn_wq"])
    wk = np.asarray(inputs["L0_attn_wk"])
    wv = np.asarray(inputs["L0_attn_wv"])
    tau = np.asarray(inputs["L0_attn_tau"]).reshape(1, 1)
    nu = np.asarray(inputs["L0_attn_nu"]).reshape(1, 1)
    wg = np.asarray(inputs["L0_ffn_w_gate"])
    wu = np.asarray(inputs["L0_ffn_w_up"])
    wd = np.asarray(inputs["L0_ffn_w_down"])
    sg = np.asarray(inputs["L0_ffn_s_gate"])
    su = np.asarray(inputs["L0_ffn_s_up"])
    sd = np.asarray(inputs["L0_ffn_s_down"])

    bf = ml_dtypes.bfloat16
    wg_s = np.ascontiguousarray(
        wg.reshape(NFC, 128, 16, 512).transpose(2, 1, 0, 3).reshape(16, 128, NFC * 512))
    wu_s = np.ascontiguousarray(
        wu.reshape(NFC, 128, 16, 512).transpose(2, 1, 0, 3).reshape(16, 128, NFC * 512))
    wd_s = np.ascontiguousarray(
        wd.reshape(8, 8, 128, 16, 128).transpose(0, 3, 2, 1, 4).reshape(8, 16, 128, 8 * 128))
    sg_s = np.ascontiguousarray(sg.reshape(64, 128).T)
    su_s = np.ascontiguousarray(su.reshape(64, 128).T)
    sd_s = np.ascontiguousarray(sd.reshape(16, 128).T)
    mask = np.ascontiguousarray(
        np.triu(np.ones((512, 512), np.float32)).reshape(4, 128, 512)
        .transpose(1, 0, 2).reshape(128, 4 * 512)).astype(bf)
    eye = np.eye(128, dtype=np.float32).astype(bf)

    in_maps = []
    for c in range(NCORE):
        rows = slice(c * TOK, (c + 1) * TOK)
        x_slab = np.ascontiguousarray(
            x[rows].T.reshape(NFC, 128, TOK).transpose(1, 0, 2).reshape(128, NFC * TOK))
        hc = slice(c * H_PER * DH, (c + 1) * H_PER * DH)
        wq_s = np.ascontiguousarray(wq[hc, :].T.reshape(NFC, 128, H_PER * DH).transpose(1, 0, 2))
        wk_s = np.ascontiguousarray(wk[hc, :].T.reshape(NFC, 128, H_PER * DH).transpose(1, 0, 2))
        wv_s = np.ascontiguousarray(wv[hc, :].T.reshape(NFC, 128, H_PER * DH).transpose(1, 0, 2))
        in_maps.append({
            "x": x_slab, "wq": wq_s, "wk": wk_s, "wv": wv_s,
            "tau": tau, "nu": nu,
            "wg": wg_s, "wu": wu_s, "wd": wd_s,
            "sg": sg_s, "su": su_s, "sd": sd_s,
            "mask": mask, "eye": eye,
        })
    return in_maps


def unstage_output(results):
    outs = []
    for c in range(NCORE):
        slab = results[c]["y"]
        xt = slab.reshape(128, NFC, TOK).transpose(1, 0, 2).reshape(C, TOK)
        outs.append(np.ascontiguousarray(xt.T))
    return np.concatenate(outs, axis=0)[None]


def kernel(**inputs):
    in_maps = stage_inputs(inputs)
    nc = build_nc()
    res = run_bass_kernel_spmd(nc, in_maps, core_ids=list(range(NCORE)))
    return unstage_output(res.results)


# version 07
# speedup vs baseline: 1.3549x; 1.0073x over previous
import sys
if "/opt/trn_rl_repo" not in sys.path:
    sys.path.insert(0, "/opt/trn_rl_repo")
import numpy as np
import ml_dtypes
import concourse.bass as bass
import concourse.mybir as mybir
from concourse.bass_utils import run_bass_kernel_spmd
from concourse.aot_env import allow_ici_all_to_all

F32 = mybir.dt.float32
BF16 = mybir.dt.bfloat16
I8 = mybir.dt.int8
AF = mybir.ActivationFunctionType
OP = mybir.AluOpType

NCORE = 8
C = 2048
TOK = 1024
T = 8192
INTER = 8192
H_PER = 2
DH = 128
NFC = C // 128
EPS_LN = 1e-5
EPS_NRM = 1e-8

PHASES = 6


class Plan:

    ENGS = ("tensor", "vector", "scalar", "sync", "gpsimd")

    def __init__(self):
        self.ops = {e: [] for e in self.ENGS}
        self.cnt = {}
        self.sem_names = []

    def sem(self, name):
        if name not in self.cnt:
            self.cnt[name] = 0
            self.sem_names.append(name)
        return name

    def c(self, sem):
        return self.cnt.get(sem, 0)

    def add(self, eng, fn, waits=(), inc=None, inc_n=1):
        for s, v in waits:
            assert v <= self.cnt.get(s, 0), (s, v, self.cnt.get(s, 0))
        if inc is not None:
            self.sem(inc)
            self.cnt[inc] += inc_n
        self.ops[eng].append((fn, tuple(waits), inc, inc_n))
        return self.cnt.get(inc, 0) if inc is not None else None

    def barrier(self):
        snap = dict(self.cnt)
        eng_sem = {"tensor": "T", "vector": "V", "scalar": "S",
                   "sync": "Dq", "gpsimd": "Dg"}
        for e in self.ENGS:
            waits = tuple((s, v) for s, v in snap.items()
                          if v > 0 and s != eng_sem.get(e))
            self.ops[e].append((None, waits, None, 0))

    def emit(self, block, sems):
        last_wait = {e: {} for e in self.ENGS}
        stats = {e: [0, 0] for e in self.ENGS}

        def emit_for(eng_name):
            def go(eng):
                lw = last_wait[eng_name]
                for fn, waits, inc, inc_n in self.ops[eng_name]:
                    for s, v in waits:
                        if lw.get(s, -1) >= v:
                            continue
                        eng.wait_ge(sems[s], v)
                        lw[s] = v
                        stats[eng_name][1] += 1
                    if fn is None:
                        continue
                    instr = fn(eng)
                    stats[eng_name][0] += 1
                    if inc is not None:
                        instr.then_inc(sems[inc], inc_n)
            return go

        for e in self.ENGS:
            getattr(block, e)(emit_for(e))
        return stats


def build_nc():
    nc = bass.Bass("TRN2", target_bir_lowering=False, debug=False)
    p = Plan()
    T_, V_, S_, Dq, Dg, CC, Da = (p.sem(s) for s in ("T", "V", "S", "Dq", "Dg", "CC", "Da"))

    x_in = nc.dram_tensor("x", [128, NFC * TOK], F32, kind="ExternalInput").ap()
    wq_in = nc.dram_tensor("wq", [128, NFC, H_PER * DH], F32, kind="ExternalInput").ap()
    wk_in = nc.dram_tensor("wk", [128, NFC, H_PER * DH], F32, kind="ExternalInput").ap()
    wv_in = nc.dram_tensor("wv", [128, NFC, H_PER * DH], F32, kind="ExternalInput").ap()
    tau_in = nc.dram_tensor("tau", [1, 1], F32, kind="ExternalInput").ap()
    nu_in = nc.dram_tensor("nu", [1, 1], F32, kind="ExternalInput").ap()
    wg_in = nc.dram_tensor("wg", [16, 128, NFC * 512], I8, kind="ExternalInput").ap()
    wu_in = nc.dram_tensor("wu", [16, 128, NFC * 512], I8, kind="ExternalInput").ap()
    wd_in = nc.dram_tensor("wd", [8, 16, 128, 8 * 128], I8, kind="ExternalInput").ap()
    sg_in = nc.dram_tensor("sg", [128, 64], F32, kind="ExternalInput").ap()
    su_in = nc.dram_tensor("su", [128, 64], F32, kind="ExternalInput").ap()
    sd_in = nc.dram_tensor("sd", [128, 16], F32, kind="ExternalInput").ap()
    mask_in = nc.dram_tensor("mask", [128, 4 * 512], BF16, kind="ExternalInput").ap()
    eye_in = nc.dram_tensor("eye", [128, 128], BF16, kind="ExternalInput").ap()
    y_out = nc.dram_tensor("y", [128, NFC * TOK], F32, kind="ExternalOutput").ap()

    xres = nc.dram_tensor("xres", [128, NFC * TOK], F32).ap()
    ag_in_a = nc.dram_tensor("ag_in_a", [2 * 128, NFC * 512], BF16)
    ag_in_b = nc.dram_tensor("ag_in_b", [2 * 128, NFC * 512], BF16)
    ag_out_a = nc.dram_tensor("ag_out_a", [NCORE * 2 * 128, NFC * 512], BF16)
    ag_out_b = nc.dram_tensor("ag_out_b", [NCORE * 2 * 128, NFC * 512], BF16)
    a2a_in = [nc.dram_tensor("a2a_in0", [C // 2, TOK], F32),
              nc.dram_tensor("a2a_in1", [C // 2, TOK], F32)]
    a2a_out = [nc.dram_tensor("a2a_out0", [C // 2, TOK], F32),
               nc.dram_tensor("a2a_out1", [C // 2, TOK], F32)]

    with (
        nc.sbuf_tensor("AR1", [128, 16384], F32) as AR1,
        nc.sbuf_tensor("AR2", [128, 16384], F32) as AR2,
        nc.sbuf_tensor("AR3", [128, 16384], F32) as AR3,
        nc.sbuf_tensor("AR4", [128, 2816], F32) as AR4,
        nc.psum_tensor("PS", [128, 4096], F32) as PS,
        nc.semaphore("sT") as sT, nc.semaphore("sV") as sV,
        nc.semaphore("sS") as sS, nc.semaphore("sDq") as sDq,
        nc.semaphore("sDg") as sDg, nc.semaphore("sCC") as sCC,
        nc.semaphore("sDa") as sDa,
        nc.Block() as block,
    ):
        sems = {"T": sT, "V": sV, "S": sS, "Dq": sDq, "Dg": sDg, "CC": sCC, "Da": sDa}

        def bank(i, width=512, off=0):
            return PS[:, i * 512 + off: i * 512 + off + width]

        def bank_row(i, width=512):
            return PS[0:1, i * 512: i * 512 + width]

        xa = AR1
        wgb = [AR1[:, 0:4096].bitcast(BF16), AR1[:, 4096:8192].bitcast(BF16)]
        wub = [AR1[:, 8192:12288].bitcast(BF16), AR1[:, 12288:16384].bitcast(BF16)]
        qh_fm = AR1[:, 0:4096].bitcast(BF16)
        ql_fm = AR1[:, 4096:8192].bitcast(BF16)
        kh_fm = AR1[:, 8192:12288].bitcast(BF16)
        kl_fm = AR1[:, 12288:16384].bitcast(BF16)
        yh = AR2[:, 0:8192].bitcast(BF16)
        yl = AR2[:, 8192:16384].bitcast(BF16)
        vh_tm = AR2[:, 0:4096].bitcast(BF16)
        vl_tm = AR2[:, 4096:8192].bitcast(BF16)
        ysb = [(AR2[:, 8192:10240].bitcast(BF16), AR2[:, 10240:12288].bitcast(BF16)),
               (AR2[:, 12288:14336].bitcast(BF16), AR2[:, 14336:16384].bitcast(BF16))]
        x2t = [AR3[:, 0:1024], AR3[:, 1024:2048]]
        bca = AR3[:, 2048:3072]
        bcb = AR3[:, 3072:4096]
        lst = [AR3[:, 4096:4608], AR3[:, 4608:5120]]
        lsy = [AR3[:, 5120:5632], AR3[:, 5632:6144]]
        fh = AR3[:, 0:4096].bitcast(BF16)
        fl = AR3[:, 4096:8192].bitcast(BF16)
        wdb = [AR3[:, 8192:8704].bitcast(BF16), AR3[:, 8704:9216].bitcast(BF16)]
        gb = [AR3[:, 9216:9728], AR3[:, 9728:10240], AR3[:, 13312:13824]]
        ub = [AR3[:, 10240:10752], AR3[:, 10752:11264], AR3[:, 13824:14336]]
        fb = [AR3[:, 11264:11776], AR3[:, 11776:12288], AR3[:, 14336:14848]]
        dsb = [AR3[:, 12288:12800], AR3[:, 12800:13312]]
        w32 = AR3[:, 0:2048]
        wsh = AR3[:, 2048:5120].bitcast(BF16)
        wsl = AR3[:, 5120:8192].bitcast(BF16)
        psq = AR3[:, 8192:8576]
        nrmt = AR3[:, 8576:8608]
        p32 = [AR3[:, 8704 + 512 * i: 9216 + 512 * i] for i in range(4)]
        phb = [AR3[:, 10752 + 256 * i: 11008 + 256 * i].bitcast(BF16) for i in range(4)]
        plb = [AR3[:, 11776 + 256 * i: 12032 + 256 * i].bitcast(BF16) for i in range(4)]
        rc_sb = [AR3[:, 12800:13312], AR3[:, 13312:13824]]
        osb = [AR3[:, 13824:14336], AR3[:, 14336:14848]]
        ttt = [AR3[:, 14848 + 64 * i: 14912 + 64 * i].bitcast(BF16) for i in range(4)]
        masks = AR4[:, 0:1024].bitcast(BF16)
        eye = AR4[:, 1024:1088].bitcast(BF16)
        ones_bf = AR4[:, 1088:1089].bitcast(BF16)[:, 0:1]
        ones_f = AR4[:, 1090:1091]
        ones_row = AR4[0:1, 1092:1220]
        sg_sb = AR4[:, 1220:1284]
        su_sb = AR4[:, 1284:1348]
        sdh_sb = AR4[:, 1348:1364]
        sd_sb = AR4[:, 1364:1380]
        tau_bc = AR4[:, 1380:1381]
        nu_sb = AR4[0:1, 1382:1383]
        tau_sb = AR4[0:1, 1384:1385]
        ones128_bf = AR4[:, 1408:1472].bitcast(BF16)
        ones128_f = AR4[:, 1472:1600]
        nu_bc = AR4[:, 1600:1601]
        r_s1 = AR3[0:1, 6144:7168]
        r_s2 = AR3[0:1, 7168:8192]
        r_t0 = AR3[0:1, 8192:9216]
        r_a = AR3[0:1, 9216:10240]
        r_b = AR3[0:1, 10240:11264]
        r_m = r_s1

        nmm = [0]

        def mm(out, lhsT, rhs, start, stop, waits=(), inc=None):
            def f(e, out=out, lhsT=lhsT, rhs=rhs, start=start, stop=stop):
                return e.matmul(out, lhsT, rhs, start=start, stop=stop)
            nmm[0] += 1
            return p.add("tensor", f, waits=waits, inc=inc)

        def init():
            p.add("sync", lambda e: e.dma_start(out=xa[:, :], in_=x_in[:, :]), inc=Dq, inc_n=16)
            p.add("sync", lambda e: e.dma_start(out=masks[:, :], in_=mask_in[:, :]), inc=Dq, inc_n=16)
            p.add("sync", lambda e: e.dma_start(out=eye[:, :], in_=eye_in[:, :]), inc=Dq, inc_n=16)
            p.add("sync", lambda e: e.dma_start(out=sg_sb[:, :], in_=sg_in[:, :]), inc=Dq, inc_n=16)
            p.add("sync", lambda e: e.dma_start(out=su_sb[:, :], in_=su_in[:, :]), inc=Dq, inc_n=16)
            p.add("sync", lambda e: e.dma_start(out=sd_sb[:, :], in_=sd_in[:, :]), inc=Dq, inc_n=16)
            p.add("sync", lambda e: e.dma_start(out=nu_sb[:, :], in_=nu_in[:, :]), inc=Dq, inc_n=16)
            p.add("sync", lambda e: e.dma_start(out=tau_sb[:, :], in_=tau_in[:, :]), inc=Dq, inc_n=16)
            dq0 = p.c(Dq)
            p.add("vector", lambda e: e.memset(ones_bf[:, :], 1.0), inc=V_)
            p.add("vector", lambda e: e.memset(ones_f[:, :], 1.0), inc=V_)
            p.add("vector", lambda e: e.memset(ones_row[:, :], 1.0), inc=V_)
            p.add("vector", lambda e: e.tensor_scalar_mul(sdh_sb[:, :], sd_sb[:, :], 0.5),
                  waits=[(Dq, dq0)], inc=V_)
            p.add("vector", lambda e: e.memset(ones128_bf[:, :], 1.0), inc=V_)
            p.add("vector", lambda e: e.memset(ones128_f[:, :], 1.0), inc=V_)
            mm(bank(7, 1), ones_row[:, :], tau_sb[:, :], True, True,
               waits=[(V_, p.c(V_)), (Dq, dq0)], inc=T_)
            mm(bank(6, 1), ones_row[:, :], nu_sb[:, :], True, True, inc=T_)
            p.add("scalar", lambda e: e.copy(tau_bc[:, :], bank(7, 1)),
                  waits=[(T_, p.c(T_))], inc=S_)
            p.add("scalar", lambda e: e.copy(nu_bc[:, :], bank(6, 1)), inc=S_)
            p.add("gpsimd", lambda e: e.dma_start(out=xres[:, :], in_=xa[:, :]),
                  waits=[(Dq, dq0)], inc=Da, inc_n=16)

        def layer_norm(load_first=False):
            if load_first:
                p.add("sync", lambda e: e.dma_start(out=xa[:, :], in_=xres[:, :]),
                      inc=Dq, inc_n=16)
                dq_x = p.c(Dq)
                p.ops["tensor"].append((None, ((Dq, dq_x),), None, 0))
                p.ops["scalar"].append((None, ((Dq, dq_x),), None, 0))
                p.ops["vector"].append((None, ((Dq, dq_x),), None, 0))
            t_sq = {}
            for fc in range(NFC):
                xr = xa[:, fc * 1024:(fc + 1) * 1024]
                b = fc % 2
                w = [(T_, t_sq[fc - 2])] if fc >= 2 else []
                t_sq[fc] = None
                s_done = p.add("scalar", (lambda e, o=x2t[b], i=xr: e.square(o, i)),
                               waits=w, inc=S_)
                for tt in range(2):
                    mm(bank(0 + tt), ones128_f[:, :], xr[:, tt * 512:(tt + 1) * 512],
                       fc == 0, fc == NFC - 1)
                    t_sq[fc] = mm(bank(2 + tt), ones128_f[:, :],
                                  x2t[b][:, tt * 512:(tt + 1) * 512],
                                  fc == 0, fc == NFC - 1,
                                  waits=[(S_, s_done)], inc=T_ if tt == 1 else None)
            t_stats = p.c(T_)
            p.add("vector", lambda e: e.tensor_copy(r_s1[:, 0:512], bank_row(0)), waits=[(T_, t_stats)], inc=V_)
            p.add("vector", lambda e: e.tensor_copy(r_s1[:, 512:1024], bank_row(1)), inc=V_)
            p.add("vector", lambda e: e.tensor_copy(r_s2[:, 0:512], bank_row(2)), inc=V_)
            p.add("vector", lambda e: e.tensor_copy(r_s2[:, 512:1024], bank_row(3)), inc=V_)
            p.add("vector", lambda e: e.tensor_scalar_mul(r_m[:, :], r_s1[:, :], 1.0 / C), inc=V_)
            p.add("vector", lambda e: e.tensor_tensor(r_t0[:, :], r_m[:, :], r_m[:, :], op=OP.mult), inc=V_)
            p.add("vector", lambda e: e.tensor_scalar(r_s2[:, :], r_s2[:, :], 1.0 / C, EPS_LN,
                                                      op0=OP.mult, op1=OP.add), inc=V_)
            v_pre = p.add("vector", lambda e: e.tensor_tensor(r_t0[:, :], r_s2[:, :], r_t0[:, :],
                                                              op=OP.subtract), inc=V_)
            s_sqrt = p.add("scalar", lambda e: e.sqrt(r_t0[:, :], r_t0[:, :]),
                           waits=[(V_, v_pre)], inc=S_)
            p.add("vector", lambda e: e.reciprocal(r_a[:, :], r_t0[:, :]),
                  waits=[(S_, s_sqrt)], inc=V_)
            p.add("vector", lambda e: e.tensor_tensor(r_b[:, :], r_m[:, :], r_a[:, :], op=OP.mult), inc=V_)
            v_rows = p.add("vector", lambda e: e.tensor_scalar_mul(r_b[:, :], r_b[:, :], -1.0), inc=V_)
            for tt in range(2):
                mm(bank(4 + 2 * tt), ones_row[:, :], r_a[:, tt * 512:(tt + 1) * 512], True, True,
                   waits=[(V_, v_rows)])
                t_bc = mm(bank(5 + 2 * tt), ones_row[:, :], r_b[:, tt * 512:(tt + 1) * 512],
                          True, True, inc=T_)
                p.add("scalar", (lambda e, tt=tt: e.copy(bca[:, tt * 512:(tt + 1) * 512], bank(4 + 2 * tt))),
                      waits=[(T_, t_bc)], inc=S_)
                p.add("scalar", (lambda e, tt=tt: e.copy(bcb[:, tt * 512:(tt + 1) * 512], bank(5 + 2 * tt))),
                      inc=S_)
            s_bc = p.c(S_)
            for fc in range(NFC):
                for tt in range(2):
                    xrt = xa[:, fc * 1024 + tt * 512: fc * 1024 + tt * 512 + 512]
                    st = tt
                    yh_t = yh[:, fc * 1024 + tt * 512: fc * 1024 + tt * 512 + 512]
                    yl_t = yl[:, fc * 1024 + tt * 512: fc * 1024 + tt * 512 + 512]
                    p.add("vector", (lambda e, o=lst[st], i=xrt, b2=bca[:, tt * 512:(tt + 1) * 512]:
                                     e.tensor_tensor(o, i, b2, op=OP.mult)),
                          waits=[(S_, s_bc)], inc=V_)
                    v_y = p.add("vector", (lambda e, o=lsy[st], i=lst[st], b2=bcb[:, tt * 512:(tt + 1) * 512]:
                                           e.tensor_tensor(o, i, b2, op=OP.add)), inc=V_)
                    s_h = p.add("scalar", (lambda e, o=yh_t, i=lsy[st]: e.copy(o, i)),
                                waits=[(V_, v_y)], inc=S_)
                    p.add("vector", (lambda e, o=yl_t, i=lsy[st], h=yh_t:
                                     e.tensor_tensor(o, i, h, op=OP.subtract)),
                          waits=[(S_, s_h)], inc=V_)

        def ffn():
            t_gate_last = {}
            t_up_last = {}
            t_down_last = {}
            s_evac = {}
            v_evac = {}
            s_fuse = 0
            v_fuse = 0
            dg_w = {}
            dg_wd = {}
            s_dsb = {}
            dg_dsb = {}
            y_ready = (V_, p.c(V_))

            def load_w(cidx):
                buf = cidx % 2
                w = []
                if cidx >= 2:
                    w = [(T_, t_gate_last.get(buf, 0))]
                p.add("gpsimd", (lambda e, b=buf, ci=cidx:
                                 e.dma_start(out=wgb[b][:, :], in_=wg_in[ci, :, :])),
                      waits=w, inc=Dg, inc_n=16)
                if cidx >= 2:
                    w = [(T_, t_up_last.get(buf, 0))]
                dg_w[cidx] = p.add("gpsimd", (lambda e, b=buf, ci=cidx:
                                              e.dma_start(out=wub[b][:, :], in_=wu_in[ci, :, :])),
                                   waits=w, inc=Dg, inc_n=16)

            load_w(0)
            load_w(1)
            for blk in range(8):
                for ic in range(2):
                    cidx = blk * 2 + ic
                    buf = cidx % 2
                    wgv = wgb[buf].rearrange("p (f i) -> p f i", f=NFC)
                    wuv = wub[buf].rearrange("p (f i) -> p f i", f=NFC)
                    for sub in range(4):
                        ig = cidx * 4 + sub
                        sl = ic * 4 + sub
                        for tt in range(2):
                            rot = (sub * 2 + tt) % 3
                            gbank, ubank = (0, 1, 6)[rot], (2, 3, 7)[rot]
                            w = [(Dg, dg_w[cidx]), y_ready,
                                 (S_, s_evac.get(gbank, 0))]
                            for pi, ysl in enumerate((yh, yl)):
                                for fc in range(NFC):
                                    t_g = mm(bank(gbank),
                                             wgv[:, fc, sub * 128:(sub + 1) * 128],
                                             ysl[:, fc * 1024 + tt * 512:][:, 0:512],
                                             pi == 0 and fc == 0, pi == 1 and fc == NFC - 1,
                                             waits=w if (pi == 0 and fc == 0) else (),
                                             inc=T_ if (pi == 1 and fc == NFC - 1) else None)
                            t_gate_last[buf] = t_g
                            w = [(V_, v_evac.get(ubank, 0))]
                            for pi, ysl in enumerate((yh, yl)):
                                for fc in range(NFC):
                                    t_u = mm(bank(ubank),
                                             wuv[:, fc, sub * 128:(sub + 1) * 128],
                                             ysl[:, fc * 1024 + tt * 512:][:, 0:512],
                                             pi == 0 and fc == 0, pi == 1 and fc == NFC - 1,
                                             waits=w if (pi == 0 and fc == 0) else (),
                                             inc=T_ if (pi == 1 and fc == NFC - 1) else None)
                            t_up_last[buf] = t_u
                            st = rot
                            s_g = p.add("scalar",
                                        (lambda e, o=gb[st], b2=gbank, ig=ig:
                                         e.activation(o, bank(b2), AF.Silu, scale=sg_sb[:, ig:ig + 1])),
                                        waits=[(T_, t_g)], inc=S_)
                            s_evac[gbank] = s_g
                            v_u = p.add("vector",
                                        (lambda e, o=ub[st], b2=ubank, ig=ig:
                                         e.tensor_scalar_mul(o, bank(b2), su_sb[:, ig:ig + 1])),
                                        waits=[(T_, t_u)], inc=V_)
                            v_evac[ubank] = v_u
                            v_f = p.add("vector",
                                        (lambda e, o=fb[st], a=gb[st], b2=ub[st]:
                                         e.tensor_tensor(o, a, b2, op=OP.mult)),
                                        waits=[(S_, s_g)], inc=V_)
                            fh_t = fh[:, sl * 1024 + tt * 512:][:, 0:512]
                            fl_t = fl[:, sl * 1024 + tt * 512:][:, 0:512]
                            w_fz = [(V_, v_f)]
                            if blk > 0:
                                w_fz.append((T_, t_blk_down))
                            s_fuse = p.add("scalar",
                                           (lambda e, o=fh_t, i=fb[st]: e.copy(o, i)),
                                           waits=w_fz, inc=S_)
                            v_fuse = p.add("vector",
                                           (lambda e, o=fl_t, i=fb[st], h=fh_t:
                                            e.tensor_tensor(o, i, h, op=OP.subtract)),
                                           waits=[(S_, s_fuse)], inc=V_)
                    if cidx + 2 < 16:
                        load_w(cidx + 2)
                fhv = fh.rearrange("p (s t) -> p s t", s=8)
                flv = fl.rearrange("p (s t) -> p s t", s=8)
                def load_wd(blk2, dmc2):
                    wbuf2 = dmc2 % 2
                    w2 = [(T_, t_down_last.get(wbuf2, 0))]
                    dg_wd[dmc2] = p.add("gpsimd",
                                        (lambda e, b=wbuf2, blk2=blk2, d2=dmc2:
                                         e.dma_start(out=wdb[b][:, :], in_=wd_in[blk2, d2, :, :])),
                                        waits=w2, inc=Dg, inc_n=16)
                if blk == 0:
                    load_wd(0, 0)
                for dmc in range(16):
                    wbuf = dmc % 2
                    wdv = wdb[wbuf].rearrange("p (s i) -> p s i", s=8)
                    for tt in range(2):
                        rot = 4 + (dmc * 2 + tt) % 2
                        w = [(Dg, dg_wd[dmc]), (S_, s_fuse), (V_, v_fuse),
                             (S_, s_dsb.get(rot, 0))]
                        first = True
                        for sub in range(8):
                            for pv, fv in ((0, fhv), (1, flv)):
                                t_d = mm(bank(rot),
                                         wdv[:, sub, :],
                                         fv[:, sub, tt * 512:(tt + 1) * 512],
                                         first, sub == 7 and pv == 1,
                                         waits=w if first else (),
                                         inc=T_ if (sub == 7 and pv == 1) else None)
                                first = False
                        t_down_last[wbuf] = t_d
                        t_blk_down = t_d
                        if tt == 1:
                            if dmc < 15:
                                load_wd(blk, dmc + 1)
                            elif blk < 7:
                                load_wd(blk + 1, 0)
                        dst = (dmc * 2 + tt) % 2
                        s_d = p.add("scalar",
                                    (lambda e, o=dsb[dst], b2=rot, dmc=dmc:
                                     e.activation(o, bank(b2), AF.Copy, scale=sdh_sb[:, dmc:dmc + 1])),
                                    waits=[(T_, t_d), (Da, dg_dsb.get(dst, 0))], inc=S_)
                        s_dsb[rot] = s_d
                        dg_dsb[dst] = p.add("gpsimd",
                                            (lambda e, i=dsb[dst], dmc=dmc, tt=tt:
                                             e.dma_start(out=xres[:, dmc * 1024 + tt * 512:][:, 0:512],
                                                         in_=i, accum_op=OP.add)),
                                            waits=[(S_, s_d)], inc=Da, inc_n=16)
            return p.c(Dg)

        agi_a = ag_in_a.ap().rearrange("(s p) (f t) -> s p f t", p=128, f=NFC)
        agi_b = ag_in_b.ap().rearrange("(s p) (f t) -> s p f t", p=128, f=NFC)
        ago_a = ag_out_a.ap().rearrange("(r s p) t -> r s p t", s=2, p=128)
        ago_b = ag_out_b.ap().rearrange("(r s p) t -> r s p t", s=2, p=128)

        def phase2_ag():
            v_last, s_last = p.c(V_), p.c(S_)
            yhv = yh.rearrange("p (f t) -> p f t", f=NFC)
            ylv = yl.rearrange("p (f t) -> p f t", f=NFC)
            p.add("sync", (lambda e: e.dma_start(out=agi_a[0], in_=yhv[:, :, 0:512])),
                  waits=[(V_, v_last), (S_, s_last)], inc=Dq, inc_n=16)
            p.add("sync", (lambda e: e.dma_start(out=agi_a[1], in_=ylv[:, :, 0:512])),
                  inc=Dq, inc_n=16)
            dq_ag = p.c(Dq)
            p.add("gpsimd", (lambda e: e.collective_compute(
                "AllGather", OP.bypass, replica_groups=[list(range(NCORE))],
                ins=[ag_in_a.ap().opt()], outs=[ag_out_a.ap().opt()])),
                waits=[(Dq, dq_ag)], inc=CC)
            p.add("sync", (lambda e: e.dma_start(out=agi_b[0], in_=yhv[:, :, 512:1024])),
                  inc=Dq, inc_n=16)
            p.add("sync", (lambda e: e.dma_start(out=agi_b[1], in_=ylv[:, :, 512:1024])),
                  inc=Dq, inc_n=16)
            dq_ag = p.c(Dq)
            p.add("gpsimd", (lambda e: e.collective_compute(
                "AllGather", OP.bypass, replica_groups=[list(range(NCORE))],
                ins=[ag_in_b.ap().opt()], outs=[ag_out_b.ap().opt()])),
                waits=[(Dq, dq_ag)], inc=CC)

        def attn_head(hl, cc_need):
            prev_use = [(T_, p.c(T_))]
            for ai, w_arr in enumerate((wq_in, wk_in, wv_in)):
                p.add("sync", (lambda e, w=w_arr: e.dma_start(
                    out=w32[:, :], in_=w[:, :, hl * 128:(hl + 1) * 128])),
                    waits=prev_use + ([(V_, p.c(V_))] if ai else []), inc=Dq, inc_n=16)
                dq_w = p.c(Dq)
                wsh_sl = wsh.rearrange("p (f j) -> p f j", f=NFC)[:, :, ai * 128:(ai + 1) * 128]
                wsl_sl = wsl.rearrange("p (f j) -> p f j", f=NFC)[:, :, ai * 128:(ai + 1) * 128]
                s_w = p.add("scalar", (lambda e, o=wsh_sl: e.copy(o, w32.rearrange(
                    "p (f j) -> p f j", f=NFC))), waits=[(Dq, dq_w)], inc=S_)
                v_w = p.add("vector", (lambda e, o=wsl_sl, h=wsh_sl: e.tensor_tensor(
                    o, w32.rearrange("p (f j) -> p f j", f=NFC), h, op=OP.subtract)),
                    waits=[(S_, s_w)], inc=V_)
                prev_use = [(V_, v_w)]
            s_wsplit, v_wsplit = p.c(S_), p.c(V_)

            wsh3 = wsh.rearrange("p (f j) -> p f j", f=NFC)
            wsl3 = wsl.rearrange("p (f j) -> p f j", f=NFC)
            t_y_use = {}
            v_split_bank = {}
            v_rd_prev = 0
            t_tp_use = {}
            s_tp_evac = {}
            for sh in range(NCORE):
                for tq in range(4):
                    gq = sh * 4 + tq
                    yb = gq % 2
                    w = [(CC, cc_need[0] if tq < 2 else cc_need[1])]
                    if gq >= 2:
                        w.append((T_, t_y_use[gq - 2]))
                    agh = ago_a if tq < 2 else ago_b
                    tq2 = tq % 2
                    for s2 in range(2):
                        p.add("sync", (lambda e, sh=sh, tq2=tq2, s2=s2, yb=yb, agh=agh: e.dma_start(
                            out=ysb[yb][s2].rearrange("p (f t) -> p f t", f=NFC),
                            in_=agh[sh, s2].rearrange("p (f t) -> p f t", f=NFC)
                                [:, :, tq2 * 256:(tq2 + 1) * 256])),
                            waits=w if s2 == 0 else (), inc=Dq, inc_n=16)
                    dq_y = p.c(Dq)
                    for t8 in range(2):
                        gt = gq * 2 + t8
                        pr = gt % 2
                        w0 = [(Dq, dq_y), (S_, s_wsplit), (V_, v_wsplit),
                              (V_, v_split_bank.get(pr, 0))]
                        ops48 = [(yb_i, wv3, fc)
                                 for yb_i, wv3 in ((0, wsh3), (1, wsh3), (0, wsl3))
                                 for fc in range(NFC)]
                        for i48, (yb_i, wv3, fc) in enumerate(ops48):
                            last = i48 == len(ops48) - 1
                            t_pj = mm(bank(pr, 384),
                                      ysb[yb][yb_i].rearrange("p (f t) -> p f t", f=NFC)
                                      [:, fc, t8 * 128:(t8 + 1) * 128],
                                      wv3[:, fc, :],
                                      i48 == 0, last,
                                      waits=w0 if i48 == 0 else (),
                                      inc=T_ if last else None)
                        t_y_use[gq] = t_pj
                        s_sq = p.add("scalar", (lambda e, pr=pr: e.square(psq[:, :], bank(pr, 384))),
                                     waits=[(T_, t_pj), (V_, v_rd_prev)], inc=S_)
                        v_rd = p.add("vector", (lambda e: e.tensor_reduce(
                            nrmt[:, 0:3], psq.rearrange("p (a j) -> p a j", a=3),
                            axis=mybir.AxisListType.X, op=OP.add)),
                            waits=[(S_, s_sq)], inc=V_)
                        v_rd_prev = v_rd
                        s_sr = p.add("scalar", lambda e: e.sqrt(nrmt[:, 4:7], nrmt[:, 0:3]),
                                     waits=[(V_, v_rd)], inc=S_)
                        v_ep = p.add("vector", (lambda e: e.tensor_scalar(
                            nrmt[:, 8:11], nrmt[:, 4:7], EPS_NRM, None, op0=OP.add, op1=OP.bypass)),
                            waits=[(S_, s_sr)], inc=V_)
                        p.add("vector", lambda e: e.reciprocal(nrmt[:, 12:15], nrmt[:, 8:11]),
                              waits=[(V_, v_ep)], inc=V_)
                        v_tau = p.add("vector", (lambda e: e.tensor_scalar_mul(
                            nrmt[:, 16:17], nrmt[:, 12:13], tau_bc[:, :])), inc=V_)
                        sc_q, sc_k, sc_v = nrmt[:, 16:17], nrmt[:, 13:14], nrmt[:, 14:15]
                        p.add("vector", (lambda e, pr=pr, gt=gt: e.tensor_scalar(
                            vh_tm[:, gt * 128:(gt + 1) * 128], bank(pr, 128, 256),
                            sc_v, None, op0=OP.mult, op1=OP.bypass)),
                            waits=[(V_, v_tau)], inc=V_)
                        v_vl = p.add("vector", (lambda e, pr=pr, gt=gt: e.scalar_tensor_tensor(
                            vl_tm[:, gt * 128:(gt + 1) * 128], bank(pr, 128, 256),
                            sc_v, vh_tm[:, gt * 128:(gt + 1) * 128],
                            op0=OP.mult, op1=OP.subtract)), inc=V_)
                        v_last_split = v_vl
                        for ai, (dh, dl, scl) in enumerate(((qh_fm, ql_fm, sc_q),
                                                           (kh_fm, kl_fm, sc_k))):
                            th, tl = ttt[2 * ai], ttt[2 * ai + 1]
                            v_h = p.add("vector", (lambda e, pr=pr, ai=ai, th=th, scl=scl:
                                e.tensor_scalar(th[:, :], bank(pr, 128, ai * 128), scl, None,
                                                op0=OP.mult, op1=OP.bypass)),
                                waits=[(T_, t_tp_use.get(2 * ai, 0))], inc=V_)
                            v_l = p.add("vector", (lambda e, pr=pr, ai=ai, th=th, tl=tl, scl=scl:
                                e.scalar_tensor_tensor(tl[:, :], bank(pr, 128, ai * 128),
                                                       scl, th[:, :],
                                                       op0=OP.mult, op1=OP.subtract)),
                                waits=[(T_, t_tp_use.get(2 * ai + 1, 0))], inc=V_)
                            v_last_split = v_l
                            for j, (tsrc, dst, vdep) in enumerate(((th, dh, v_h), (tl, dl, v_l))):
                                tpb = 2 + (2 * ai + j)
                                t_tp = p.add("tensor", (lambda e, tpb=tpb, tsrc=tsrc:
                                    e.transpose(bank(tpb, 64).bitcast(BF16),
                                                tsrc[:, :], eye[:, :])),
                                    waits=[(V_, vdep),
                                           (S_, s_tp_evac.get(tpb, 0))], inc=T_)
                                t_tp_use[2 * ai + j] = t_tp
                                s_tp_evac[tpb] = p.add("scalar", (lambda e, tpb=tpb, dst=dst, gt=gt:
                                    e.copy(dst[:, gt * 128:(gt + 1) * 128],
                                           bank(tpb, 64).bitcast(BF16))),
                                    waits=[(T_, t_tp)], inc=S_)
                        v_split_bank[pr] = v_last_split
            s_prep, v_prep, t_prep = p.c(S_), p.c(V_), p.c(T_)


            s_exp_bank = {}
            v_pl_buf = {}
            t_p_use = {}
            v_dn_free = {}
            v_pv_free = {}
            dq_osb = {}
            t_dn = t_pv = 0
            for qt in range(16):
                nkt = 4 * qt + 4
                dnb = 3 + qt % 2
                pvb = 5 + qt % 2
                pend = None
                for kt in range(nkt):
                    bp = kt % 4
                    bsc = kt % 3
                    w0 = [(S_, s_exp_bank.get(bsc, 0))]
                    if s_exp_bank.get(bsc, 0) == 0:
                        w0 += [(V_, v_prep), (S_, s_prep)]
                    if kt == 0:
                        w0 += [(V_, v_pv_free.get(pvb, 0)), (V_, v_dn_free.get(dnb, 0))]
                    mm(bank(bsc), kh_fm[:, kt * 128:(kt + 1) * 128],
                       qh_fm[:, qt * 512:(qt + 1) * 512], True, False, waits=w0)
                    mm(bank(bsc), kh_fm[:, kt * 128:(kt + 1) * 128],
                       ql_fm[:, qt * 512:(qt + 1) * 512], False, False)
                    t_sc = mm(bank(bsc), kl_fm[:, kt * 128:(kt + 1) * 128],
                              qh_fm[:, qt * 512:(qt + 1) * 512], False, True, inc=T_)
                    if pend is not None:
                        pend()
                        pend = None
                    s_exp = p.add("scalar", (lambda e, bp=bp, bsc=bsc: e.activation(
                        p32[bp][:, :], bank(bsc), AF.Exp)),
                        waits=[(T_, t_sc), (V_, v_pl_buf.get(bp, 0))], inc=S_)
                    s_exp_bank[bsc] = s_exp
                    diag = kt >= 4 * qt
                    guard_T = [(T_, t_p_use.get(bp, 0))]
                    if diag:
                        dj = kt - 4 * qt
                        v_mask = p.add("vector", (lambda e, bp=bp, dj=dj: e.tensor_tensor(
                            p32[bp][:, :], p32[bp][:, :],
                            masks[:, dj * 512:(dj + 1) * 512], op=OP.mult)),
                            waits=[(S_, s_exp)], inc=V_)
                        s_ph = p.add("scalar", (lambda e, bp=bp: e.copy(phb[bp][:, :], p32[bp][:, :])),
                                     waits=[(V_, v_mask)] + guard_T, inc=S_)
                        ph_dep = (S_, s_ph)
                    else:
                        v_ph = p.add("vector", (lambda e, bp=bp: e.tensor_scalar(
                            phb[bp][:, :], p32[bp][:, :], 1.0, None,
                            op0=OP.mult, op1=OP.bypass)),
                            waits=[(S_, s_exp)] + guard_T, inc=V_)
                        ph_dep = (V_, v_ph)
                    v_pl = p.add("vector", (lambda e, bp=bp: e.tensor_tensor(
                        plb[bp][:, :], p32[bp][:, :], phb[bp][:, :], op=OP.subtract)),
                        waits=[ph_dep], inc=V_)
                    v_pl_buf[bp] = v_pl

                    def emit_dnpv(kt=kt, bp=bp, ph_dep=ph_dep, v_pl=v_pl):
                        nonlocal t_dn, t_pv
                        mm(bank(dnb), ones128_bf[:, :], phb[bp][:, :], kt == 0, False,
                           waits=[ph_dep])
                        t_dn = mm(bank(dnb), ones128_bf[:, :], plb[bp][:, :], False,
                                  kt == nkt - 1, waits=[(V_, v_pl)],
                                  inc=T_ if kt == nkt - 1 else None)
                        mm(bank(pvb), vh_tm[:, kt * 128:(kt + 1) * 128], phb[bp][:, :],
                           kt == 0, False)
                        mm(bank(pvb), vh_tm[:, kt * 128:(kt + 1) * 128], plb[bp][:, :],
                           False, False)
                        t_pv = mm(bank(pvb), vl_tm[:, kt * 128:(kt + 1) * 128], phb[bp][:, :],
                                  False, kt == nkt - 1, inc=T_)
                        t_p_use[bp] = t_pv
                    pend = emit_dnpv
                pend()
                pend = None
                qtp = qt % 2
                v_rc = p.add("vector", (lambda e, qtp=qtp, dnb=dnb: e.reciprocal(
                    rc_sb[qtp][:, :], bank(dnb))), waits=[(T_, t_dn)], inc=V_)
                v_dn_free[dnb] = v_rc
                v_rcn = p.add("vector", (lambda e, qtp=qtp: e.tensor_scalar(
                    rc_sb[qtp][:, :], rc_sb[qtp][:, :], nu_bc[:, :], None,
                    op0=OP.mult, op1=OP.bypass)), waits=[(V_, v_rc)], inc=V_)
                v_pv_evac = p.add("vector", (lambda e, qtp=qtp, pvb=pvb: e.scalar_tensor_tensor(
                    osb[qtp][:, :], bank(pvb), 1.0, rc_sb[qtp][:, :],
                    op0=OP.mult, op1=OP.mult)),
                    waits=[(V_, v_rcn), (T_, t_pv), (Dq, dq_osb.get(qtp, 0))], inc=V_)
                v_pv_free[pvb] = v_pv_evac
                p.add("sync", (lambda e, qtp=qtp, qt=qt, hl=hl: e.dma_start(
                    out=a2a_in[hl].ap()[128 * (qt // 2): 128 * (qt // 2) + 128,
                                        (qt % 2) * 512:(qt % 2 + 1) * 512],
                    in_=osb[qtp][:, :])), waits=[(V_, v_pv_evac)], inc=Dq, inc_n=16)
                dq_osb[qtp] = p.c(Dq)

        def phase4_a2a(hl):
            dq_all = p.c(Dq)

            def _a2a(e, hl=hl):
                with allow_ici_all_to_all():
                    return e.collective_compute(
                        "AllToAll", OP.bypass, replica_groups=[list(range(NCORE))],
                        ins=[a2a_in[hl].ap().opt()], outs=[a2a_out[hl].ap().opt()])
            p.add("gpsimd", _a2a, waits=[(Dq, dq_all)], inc=CC)
            cc_n = p.c(CC)
            a2v = a2a_out[hl].ap().rearrange("(i p) t -> p i t", p=128)
            for i in range(NCORE):
                p.add("gpsimd", (lambda e, i=i, hl=hl: e.dma_start(
                    out=xres[:, (2 * i + hl) * 1024:(2 * i + hl + 1) * 1024],
                    in_=a2v[:, i, :], accum_op=OP.add)),
                    waits=[(CC, cc_n)] if i == 0 else (), inc=Da, inc_n=16)

        init()
        p.barrier()
        layer_norm()
        p.barrier()
        ffn()
        p.barrier()
        if PHASES >= 2:
            layer_norm(load_first=True)
            phase2_ag()
            cc_need = (p.c(CC) - 1, p.c(CC))
            p.barrier()
        if PHASES >= 3:
            attn_head(0, cc_need)
            p.barrier()
            if PHASES >= 4:
                phase4_a2a(0)
            attn_head(1, cc_need)
            if PHASES >= 4:
                phase4_a2a(1)
            p.barrier()
        if PHASES >= 5:
            layer_norm(load_first=True)
            p.barrier()
            ffn()
            p.barrier()

        p.add("sync", lambda e: e.dma_start(out=xa[:, :], in_=xres[:, :]), inc=Dq, inc_n=16)
        dq_ld = p.c(Dq)
        p.add("sync", lambda e: e.dma_start(out=y_out[:, :], in_=xa[:, :]),
              waits=[(Dq, dq_ld)], inc=Dq, inc_n=16)
        dq_st = p.c(Dq)
        p.add("gpsimd", lambda e: e.nop(), waits=[(Dq, dq_st)])

        stats = p.emit(block, sems)
        print("emit stats:", {k: tuple(v) for k, v in stats.items()}, "matmuls:", nmm[0])
    return nc


def stage_inputs(inputs):
    x = np.asarray(inputs["L0_attn_x"])[0]
    wq = np.asarray(inputs["L0_attn_wq"])
    wk = np.asarray(inputs["L0_attn_wk"])
    wv = np.asarray(inputs["L0_attn_wv"])
    tau = np.asarray(inputs["L0_attn_tau"]).reshape(1, 1)
    nu = np.asarray(inputs["L0_attn_nu"]).reshape(1, 1)
    wg = np.asarray(inputs["L0_ffn_w_gate"])
    wu = np.asarray(inputs["L0_ffn_w_up"])
    wd = np.asarray(inputs["L0_ffn_w_down"])
    sg = np.asarray(inputs["L0_ffn_s_gate"])
    su = np.asarray(inputs["L0_ffn_s_up"])
    sd = np.asarray(inputs["L0_ffn_s_down"])

    bf = ml_dtypes.bfloat16
    wg_s = np.ascontiguousarray(
        wg.reshape(NFC, 128, 16, 512).transpose(2, 1, 0, 3).reshape(16, 128, NFC * 512))
    wu_s = np.ascontiguousarray(
        wu.reshape(NFC, 128, 16, 512).transpose(2, 1, 0, 3).reshape(16, 128, NFC * 512))
    wd_s = np.ascontiguousarray(
        wd.reshape(8, 8, 128, 16, 128).transpose(0, 3, 2, 1, 4).reshape(8, 16, 128, 8 * 128))
    sg_s = np.ascontiguousarray(sg.reshape(64, 128).T)
    su_s = np.ascontiguousarray(su.reshape(64, 128).T)
    sd_s = np.ascontiguousarray(sd.reshape(16, 128).T)
    mask = np.ascontiguousarray(
        np.triu(np.ones((512, 512), np.float32)).reshape(4, 128, 512)
        .transpose(1, 0, 2).reshape(128, 4 * 512)).astype(bf)
    eye = np.eye(128, dtype=np.float32).astype(bf)

    in_maps = []
    for c in range(NCORE):
        rows = slice(c * TOK, (c + 1) * TOK)
        x_slab = np.ascontiguousarray(
            x[rows].T.reshape(NFC, 128, TOK).transpose(1, 0, 2).reshape(128, NFC * TOK))
        hc = slice(c * H_PER * DH, (c + 1) * H_PER * DH)
        wq_s = np.ascontiguousarray(wq[hc, :].T.reshape(NFC, 128, H_PER * DH).transpose(1, 0, 2))
        wk_s = np.ascontiguousarray(wk[hc, :].T.reshape(NFC, 128, H_PER * DH).transpose(1, 0, 2))
        wv_s = np.ascontiguousarray(wv[hc, :].T.reshape(NFC, 128, H_PER * DH).transpose(1, 0, 2))
        in_maps.append({
            "x": x_slab, "wq": wq_s, "wk": wk_s, "wv": wv_s,
            "tau": tau, "nu": nu,
            "wg": wg_s, "wu": wu_s, "wd": wd_s,
            "sg": sg_s, "su": su_s, "sd": sd_s,
            "mask": mask, "eye": eye,
        })
    return in_maps


def unstage_output(results):
    outs = []
    for c in range(NCORE):
        slab = results[c]["y"]
        xt = slab.reshape(128, NFC, TOK).transpose(1, 0, 2).reshape(C, TOK)
        outs.append(np.ascontiguousarray(xt.T))
    return np.concatenate(outs, axis=0)[None]


def kernel(**inputs):
    in_maps = stage_inputs(inputs)
    nc = build_nc()
    res = run_bass_kernel_spmd(nc, in_maps, core_ids=list(range(NCORE)))
    return unstage_output(res.results)


# version 08
# speedup vs baseline: 1.3720x; 1.0126x over previous
import sys
if "/opt/trn_rl_repo" not in sys.path:
    sys.path.insert(0, "/opt/trn_rl_repo")
import numpy as np
import ml_dtypes
import concourse.bass as bass
import concourse.mybir as mybir
from concourse.bass_utils import run_bass_kernel_spmd
from concourse.aot_env import allow_ici_all_to_all

F32 = mybir.dt.float32
BF16 = mybir.dt.bfloat16
I8 = mybir.dt.int8
AF = mybir.ActivationFunctionType
OP = mybir.AluOpType

NCORE = 8
C = 2048
TOK = 1024
T = 8192
INTER = 8192
H_PER = 2
DH = 128
NFC = C // 128
EPS_LN = 1e-5
EPS_NRM = 1e-8

PHASES = 6


class Plan:

    ENGS = ("tensor", "vector", "scalar", "sync", "gpsimd")

    def __init__(self):
        self.ops = {e: [] for e in self.ENGS}
        self.cnt = {}
        self.sem_names = []

    def sem(self, name):
        if name not in self.cnt:
            self.cnt[name] = 0
            self.sem_names.append(name)
        return name

    def c(self, sem):
        return self.cnt.get(sem, 0)

    def add(self, eng, fn, waits=(), inc=None, inc_n=1):
        for s, v in waits:
            assert v <= self.cnt.get(s, 0), (s, v, self.cnt.get(s, 0))
        if inc is not None:
            self.sem(inc)
            self.cnt[inc] += inc_n
        self.ops[eng].append((fn, tuple(waits), inc, inc_n))
        return self.cnt.get(inc, 0) if inc is not None else None

    def barrier(self):
        snap = dict(self.cnt)
        eng_sem = {"tensor": "T", "vector": "V", "scalar": "S",
                   "sync": "Dq", "gpsimd": "Dg"}
        for e in self.ENGS:
            waits = tuple((s, v) for s, v in snap.items()
                          if v > 0 and s != eng_sem.get(e))
            self.ops[e].append((None, waits, None, 0))

    def emit(self, block, sems):
        last_wait = {e: {} for e in self.ENGS}
        stats = {e: [0, 0] for e in self.ENGS}

        def emit_for(eng_name):
            def go(eng):
                lw = last_wait[eng_name]
                for fn, waits, inc, inc_n in self.ops[eng_name]:
                    for s, v in waits:
                        if lw.get(s, -1) >= v:
                            continue
                        eng.wait_ge(sems[s], v)
                        lw[s] = v
                        stats[eng_name][1] += 1
                    if fn is None:
                        continue
                    instr = fn(eng)
                    stats[eng_name][0] += 1
                    if inc is not None:
                        instr.then_inc(sems[inc], inc_n)
            return go

        for e in self.ENGS:
            getattr(block, e)(emit_for(e))
        return stats


def build_nc():
    nc = bass.Bass("TRN2", target_bir_lowering=False, debug=False)
    p = Plan()
    T_, V_, S_, Dq, Dg, CC, Da = (p.sem(s) for s in ("T", "V", "S", "Dq", "Dg", "CC", "Da"))

    x_in = nc.dram_tensor("x", [128, NFC * TOK], F32, kind="ExternalInput").ap()
    wq_in = nc.dram_tensor("wq", [128, NFC, H_PER * DH], F32, kind="ExternalInput").ap()
    wk_in = nc.dram_tensor("wk", [128, NFC, H_PER * DH], F32, kind="ExternalInput").ap()
    wv_in = nc.dram_tensor("wv", [128, NFC, H_PER * DH], F32, kind="ExternalInput").ap()
    tau_in = nc.dram_tensor("tau", [1, 1], F32, kind="ExternalInput").ap()
    nu_in = nc.dram_tensor("nu", [1, 1], F32, kind="ExternalInput").ap()
    wg_in = nc.dram_tensor("wg", [16, 128, NFC * 512], I8, kind="ExternalInput").ap()
    wu_in = nc.dram_tensor("wu", [16, 128, NFC * 512], I8, kind="ExternalInput").ap()
    wd_in = nc.dram_tensor("wd", [8, 16, 128, 8 * 128], I8, kind="ExternalInput").ap()
    sg_in = nc.dram_tensor("sg", [128, 64], F32, kind="ExternalInput").ap()
    su_in = nc.dram_tensor("su", [128, 64], F32, kind="ExternalInput").ap()
    sd_in = nc.dram_tensor("sd", [128, 16], F32, kind="ExternalInput").ap()
    mask_in = nc.dram_tensor("mask", [128, 4 * 512], BF16, kind="ExternalInput").ap()
    eye_in = nc.dram_tensor("eye", [128, 128], BF16, kind="ExternalInput").ap()
    y_out = nc.dram_tensor("y", [128, NFC * TOK], F32, kind="ExternalOutput").ap()

    xres = nc.dram_tensor("xres", [128, NFC * TOK], F32).ap()
    ag_in_a = nc.dram_tensor("ag_in_a", [2 * 128, NFC * 512], BF16)
    ag_in_b = nc.dram_tensor("ag_in_b", [2 * 128, NFC * 512], BF16)
    ag_out_a = nc.dram_tensor("ag_out_a", [NCORE * 2 * 128, NFC * 512], BF16)
    ag_out_b = nc.dram_tensor("ag_out_b", [NCORE * 2 * 128, NFC * 512], BF16)
    a2a_in = [nc.dram_tensor("a2a_in0", [C // 2, TOK], F32),
              nc.dram_tensor("a2a_in1", [C // 2, TOK], F32)]
    a2a_out = [nc.dram_tensor("a2a_out0", [C // 2, TOK], F32),
               nc.dram_tensor("a2a_out1", [C // 2, TOK], F32)]

    with (
        nc.sbuf_tensor("AR1", [128, 16384], F32) as AR1,
        nc.sbuf_tensor("AR2", [128, 16384], F32) as AR2,
        nc.sbuf_tensor("AR3", [128, 16384], F32) as AR3,
        nc.sbuf_tensor("AR4", [128, 2816], F32) as AR4,
        nc.psum_tensor("PS", [128, 4096], F32) as PS,
        nc.semaphore("sT") as sT, nc.semaphore("sV") as sV,
        nc.semaphore("sS") as sS, nc.semaphore("sDq") as sDq,
        nc.semaphore("sDg") as sDg, nc.semaphore("sCC") as sCC,
        nc.semaphore("sDa") as sDa,
        nc.Block() as block,
    ):
        sems = {"T": sT, "V": sV, "S": sS, "Dq": sDq, "Dg": sDg, "CC": sCC, "Da": sDa}

        def bank(i, width=512, off=0):
            return PS[:, i * 512 + off: i * 512 + off + width]

        def bank_row(i, width=512):
            return PS[0:1, i * 512: i * 512 + width]

        xa = AR1
        wgb = [AR1[:, 0:4096].bitcast(BF16), AR1[:, 4096:8192].bitcast(BF16)]
        wub = [AR1[:, 8192:12288].bitcast(BF16), AR1[:, 12288:16384].bitcast(BF16)]
        qh_fm = AR1[:, 0:4096].bitcast(BF16)
        ql_fm = AR1[:, 4096:8192].bitcast(BF16)
        kh_fm = AR1[:, 8192:12288].bitcast(BF16)
        kl_fm = AR1[:, 12288:16384].bitcast(BF16)
        yh = AR2[:, 0:8192].bitcast(BF16)
        yl = AR2[:, 8192:16384].bitcast(BF16)
        vh_tm = AR2[:, 0:4096].bitcast(BF16)
        vl_tm = AR2[:, 4096:8192].bitcast(BF16)
        ysb = [(AR2[:, 8192:10240].bitcast(BF16), AR2[:, 10240:12288].bitcast(BF16)),
               (AR2[:, 12288:14336].bitcast(BF16), AR2[:, 14336:16384].bitcast(BF16))]
        x2t = [AR3[:, 0:1024], AR3[:, 1024:2048]]
        bca = AR3[:, 2048:3072]
        bcb = AR3[:, 3072:4096]
        lst = [AR3[:, 4096:4608], AR3[:, 4608:5120]]
        lsy = [AR3[:, 5120:5632], AR3[:, 5632:6144]]
        fh = AR3[:, 0:4096].bitcast(BF16)
        fl = AR3[:, 4096:8192].bitcast(BF16)
        wdb = [AR3[:, 8192:8704].bitcast(BF16), AR3[:, 8704:9216].bitcast(BF16)]
        gb = [AR3[:, 9216:9728], AR3[:, 9728:10240], AR3[:, 13312:13824]]
        ub = [AR3[:, 10240:10752], AR3[:, 10752:11264], AR3[:, 13824:14336]]
        fb = [AR3[:, 11264:11776], AR3[:, 11776:12288], AR3[:, 14336:14848]]
        dsb = [AR3[:, 12288:12800], AR3[:, 12800:13312]]
        w32 = AR3[:, 0:2048]
        wsh = AR3[:, 2048:5120].bitcast(BF16)
        wsl = AR3[:, 5120:8192].bitcast(BF16)
        psq = AR3[:, 8192:8576]
        nrmt = AR3[:, 8576:8608]
        p32 = [AR3[:, 8704 + 512 * i: 9216 + 512 * i] for i in range(4)]
        phb = [AR3[:, 10752 + 256 * i: 11008 + 256 * i].bitcast(BF16) for i in range(4)]
        plb = [AR3[:, 11776 + 256 * i: 12032 + 256 * i].bitcast(BF16) for i in range(4)]
        rc_sb = [AR3[:, 12800:13312], AR3[:, 13312:13824]]
        osb = [AR3[:, 13824:14336], AR3[:, 14336:14848]]
        ttt = [AR3[:, 14848 + 64 * i: 14912 + 64 * i].bitcast(BF16) for i in range(4)]
        masks = AR4[:, 0:1024].bitcast(BF16)
        eye = AR4[:, 1024:1088].bitcast(BF16)
        ones_bf = AR4[:, 1088:1089].bitcast(BF16)[:, 0:1]
        ones_f = AR4[:, 1090:1091]
        ones_row = AR4[0:1, 1092:1220]
        sg_sb = AR4[:, 1220:1284]
        su_sb = AR4[:, 1284:1348]
        sdh_sb = AR4[:, 1348:1364]
        sd_sb = AR4[:, 1364:1380]
        tau_bc = AR4[:, 1380:1381]
        nu_sb = AR4[0:1, 1382:1383]
        tau_sb = AR4[0:1, 1384:1385]
        ones128_bf = AR4[:, 1408:1472].bitcast(BF16)
        ones128_f = AR4[:, 1472:1600]
        nu_bc = AR4[:, 1600:1601]
        r_s1 = AR3[0:1, 6144:7168]
        r_s2 = AR3[0:1, 7168:8192]
        r_t0 = AR3[0:1, 8192:9216]
        r_a = AR3[0:1, 9216:10240]
        r_b = AR3[0:1, 10240:11264]
        r_m = r_s1

        nmm = [0]

        def mm(out, lhsT, rhs, start, stop, waits=(), inc=None):
            def f(e, out=out, lhsT=lhsT, rhs=rhs, start=start, stop=stop):
                return e.matmul(out, lhsT, rhs, start=start, stop=stop)
            nmm[0] += 1
            return p.add("tensor", f, waits=waits, inc=inc)

        def init():
            p.add("sync", lambda e: e.dma_start(out=xa[:, :], in_=x_in[:, :]), inc=Dq, inc_n=16)
            p.add("sync", lambda e: e.dma_start(out=masks[:, :], in_=mask_in[:, :]), inc=Dq, inc_n=16)
            p.add("sync", lambda e: e.dma_start(out=eye[:, :], in_=eye_in[:, :]), inc=Dq, inc_n=16)
            p.add("sync", lambda e: e.dma_start(out=sg_sb[:, :], in_=sg_in[:, :]), inc=Dq, inc_n=16)
            p.add("sync", lambda e: e.dma_start(out=su_sb[:, :], in_=su_in[:, :]), inc=Dq, inc_n=16)
            p.add("sync", lambda e: e.dma_start(out=sd_sb[:, :], in_=sd_in[:, :]), inc=Dq, inc_n=16)
            p.add("sync", lambda e: e.dma_start(out=nu_sb[:, :], in_=nu_in[:, :]), inc=Dq, inc_n=16)
            p.add("sync", lambda e: e.dma_start(out=tau_sb[:, :], in_=tau_in[:, :]), inc=Dq, inc_n=16)
            dq0 = p.c(Dq)
            p.add("vector", lambda e: e.memset(ones_bf[:, :], 1.0), inc=V_)
            p.add("vector", lambda e: e.memset(ones_f[:, :], 1.0), inc=V_)
            p.add("vector", lambda e: e.memset(ones_row[:, :], 1.0), inc=V_)
            p.add("vector", lambda e: e.tensor_scalar_mul(sdh_sb[:, :], sd_sb[:, :], 0.5),
                  waits=[(Dq, dq0)], inc=V_)
            p.add("vector", lambda e: e.memset(ones128_bf[:, :], 1.0), inc=V_)
            p.add("vector", lambda e: e.memset(ones128_f[:, :], 1.0), inc=V_)
            mm(bank(7, 1), ones_row[:, :], tau_sb[:, :], True, True,
               waits=[(V_, p.c(V_)), (Dq, dq0)], inc=T_)
            mm(bank(6, 1), ones_row[:, :], nu_sb[:, :], True, True, inc=T_)
            p.add("scalar", lambda e: e.copy(tau_bc[:, :], bank(7, 1)),
                  waits=[(T_, p.c(T_))], inc=S_)
            p.add("scalar", lambda e: e.copy(nu_bc[:, :], bank(6, 1)), inc=S_)
            p.add("gpsimd", lambda e: e.dma_start(out=xres[:, :], in_=xa[:, :]),
                  waits=[(Dq, dq0)], inc=Da, inc_n=16)

        def layer_norm(load_first=False):
            if load_first:
                p.add("sync", lambda e: e.dma_start(out=xa[:, :], in_=xres[:, :]),
                      inc=Dq, inc_n=16)
                dq_x = p.c(Dq)
                p.ops["tensor"].append((None, ((Dq, dq_x),), None, 0))
                p.ops["scalar"].append((None, ((Dq, dq_x),), None, 0))
                p.ops["vector"].append((None, ((Dq, dq_x),), None, 0))
            t_sq = {}
            for fc in range(NFC):
                xr = xa[:, fc * 1024:(fc + 1) * 1024]
                b = fc % 2
                w = [(T_, t_sq[fc - 2])] if fc >= 2 else []
                t_sq[fc] = None
                s_done = p.add("scalar", (lambda e, o=x2t[b], i=xr: e.square(o, i)),
                               waits=w, inc=S_)
                for tt in range(2):
                    mm(bank(0 + tt), ones128_f[:, :], xr[:, tt * 512:(tt + 1) * 512],
                       fc == 0, fc == NFC - 1)
                    t_sq[fc] = mm(bank(2 + tt), ones128_f[:, :],
                                  x2t[b][:, tt * 512:(tt + 1) * 512],
                                  fc == 0, fc == NFC - 1,
                                  waits=[(S_, s_done)], inc=T_ if tt == 1 else None)
            t_stats = p.c(T_)
            p.add("vector", lambda e: e.tensor_copy(r_s1[:, 0:512], bank_row(0)), waits=[(T_, t_stats)], inc=V_)
            p.add("vector", lambda e: e.tensor_copy(r_s1[:, 512:1024], bank_row(1)), inc=V_)
            p.add("vector", lambda e: e.tensor_copy(r_s2[:, 0:512], bank_row(2)), inc=V_)
            p.add("vector", lambda e: e.tensor_copy(r_s2[:, 512:1024], bank_row(3)), inc=V_)
            p.add("vector", lambda e: e.tensor_scalar_mul(r_m[:, :], r_s1[:, :], 1.0 / C), inc=V_)
            p.add("vector", lambda e: e.tensor_tensor(r_t0[:, :], r_m[:, :], r_m[:, :], op=OP.mult), inc=V_)
            p.add("vector", lambda e: e.tensor_scalar(r_s2[:, :], r_s2[:, :], 1.0 / C, EPS_LN,
                                                      op0=OP.mult, op1=OP.add), inc=V_)
            v_pre = p.add("vector", lambda e: e.tensor_tensor(r_t0[:, :], r_s2[:, :], r_t0[:, :],
                                                              op=OP.subtract), inc=V_)
            s_sqrt = p.add("scalar", lambda e: e.sqrt(r_t0[:, :], r_t0[:, :]),
                           waits=[(V_, v_pre)], inc=S_)
            p.add("vector", lambda e: e.reciprocal(r_a[:, :], r_t0[:, :]),
                  waits=[(S_, s_sqrt)], inc=V_)
            p.add("vector", lambda e: e.tensor_tensor(r_b[:, :], r_m[:, :], r_a[:, :], op=OP.mult), inc=V_)
            v_rows = p.add("vector", lambda e: e.tensor_scalar_mul(r_b[:, :], r_b[:, :], -1.0), inc=V_)
            for tt in range(2):
                mm(bank(4 + 2 * tt), ones_row[:, :], r_a[:, tt * 512:(tt + 1) * 512], True, True,
                   waits=[(V_, v_rows)])
                t_bc = mm(bank(5 + 2 * tt), ones_row[:, :], r_b[:, tt * 512:(tt + 1) * 512],
                          True, True, inc=T_)
                p.add("scalar", (lambda e, tt=tt: e.copy(bca[:, tt * 512:(tt + 1) * 512], bank(4 + 2 * tt))),
                      waits=[(T_, t_bc)], inc=S_)
                p.add("scalar", (lambda e, tt=tt: e.copy(bcb[:, tt * 512:(tt + 1) * 512], bank(5 + 2 * tt))),
                      inc=S_)
            s_bc = p.c(S_)
            for fc in range(NFC):
                for tt in range(2):
                    xrt = xa[:, fc * 1024 + tt * 512: fc * 1024 + tt * 512 + 512]
                    st = tt
                    yh_t = yh[:, fc * 1024 + tt * 512: fc * 1024 + tt * 512 + 512]
                    yl_t = yl[:, fc * 1024 + tt * 512: fc * 1024 + tt * 512 + 512]
                    p.add("vector", (lambda e, o=lst[st], i=xrt, b2=bca[:, tt * 512:(tt + 1) * 512]:
                                     e.tensor_tensor(o, i, b2, op=OP.mult)),
                          waits=[(S_, s_bc)], inc=V_)
                    v_y = p.add("vector", (lambda e, o=lsy[st], i=lst[st], b2=bcb[:, tt * 512:(tt + 1) * 512]:
                                           e.tensor_tensor(o, i, b2, op=OP.add)), inc=V_)
                    s_h = p.add("scalar", (lambda e, o=yh_t, i=lsy[st]: e.copy(o, i)),
                                waits=[(V_, v_y)], inc=S_)
                    p.add("vector", (lambda e, o=yl_t, i=lsy[st], h=yh_t:
                                     e.tensor_tensor(o, i, h, op=OP.subtract)),
                          waits=[(S_, s_h)], inc=V_)

        def ffn():
            t_gate_last = {}
            t_up_last = {}
            t_down_last = {}
            s_evac = {}
            v_evac = {}
            s_fuse = 0
            v_fuse = 0
            dg_w = {}
            dg_wd = {}
            s_dsb = {}
            dg_dsb = {}
            y_ready = (V_, p.c(V_))

            def load_w(cidx):
                buf = cidx % 2
                w = []
                if cidx >= 2:
                    w = [(T_, t_gate_last.get(buf, 0))]
                p.add("gpsimd", (lambda e, b=buf, ci=cidx:
                                 e.dma_start(out=wgb[b][:, :], in_=wg_in[ci, :, :])),
                      waits=w, inc=Dg, inc_n=16)
                if cidx >= 2:
                    w = [(T_, t_up_last.get(buf, 0))]
                dg_w[cidx] = p.add("gpsimd", (lambda e, b=buf, ci=cidx:
                                              e.dma_start(out=wub[b][:, :], in_=wu_in[ci, :, :])),
                                   waits=w, inc=Dg, inc_n=16)

            load_w(0)
            load_w(1)
            for blk in range(8):
                for ic in range(2):
                    cidx = blk * 2 + ic
                    buf = cidx % 2
                    wgv = wgb[buf].rearrange("p (f i) -> p f i", f=NFC)
                    wuv = wub[buf].rearrange("p (f i) -> p f i", f=NFC)
                    for sub in range(4):
                        ig = cidx * 4 + sub
                        sl = ic * 4 + sub
                        for tt in range(2):
                            rot = (sub * 2 + tt) % 3
                            gbank, ubank = (0, 1, 6)[rot], (2, 3, 7)[rot]
                            w = [(Dg, dg_w[cidx]), y_ready,
                                 (S_, s_evac.get(gbank, 0))]
                            for pi, ysl in enumerate((yh, yl)):
                                for fc in range(NFC):
                                    t_g = mm(bank(gbank),
                                             wgv[:, fc, sub * 128:(sub + 1) * 128],
                                             ysl[:, fc * 1024 + tt * 512:][:, 0:512],
                                             pi == 0 and fc == 0, pi == 1 and fc == NFC - 1,
                                             waits=w if (pi == 0 and fc == 0) else (),
                                             inc=T_ if (pi == 1 and fc == NFC - 1) else None)
                            t_gate_last[buf] = t_g
                            w = [(V_, v_evac.get(ubank, 0))]
                            for pi, ysl in enumerate((yh, yl)):
                                for fc in range(NFC):
                                    t_u = mm(bank(ubank),
                                             wuv[:, fc, sub * 128:(sub + 1) * 128],
                                             ysl[:, fc * 1024 + tt * 512:][:, 0:512],
                                             pi == 0 and fc == 0, pi == 1 and fc == NFC - 1,
                                             waits=w if (pi == 0 and fc == 0) else (),
                                             inc=T_ if (pi == 1 and fc == NFC - 1) else None)
                            t_up_last[buf] = t_u
                            st = rot
                            s_g = p.add("scalar",
                                        (lambda e, o=gb[st], b2=gbank, ig=ig:
                                         e.activation(o, bank(b2), AF.Silu, scale=sg_sb[:, ig:ig + 1])),
                                        waits=[(T_, t_g)], inc=S_)
                            s_evac[gbank] = s_g
                            v_u = p.add("vector",
                                        (lambda e, o=ub[st], b2=ubank, ig=ig:
                                         e.tensor_scalar_mul(o, bank(b2), su_sb[:, ig:ig + 1])),
                                        waits=[(T_, t_u)], inc=V_)
                            v_evac[ubank] = v_u
                            v_f = p.add("vector",
                                        (lambda e, o=fb[st], a=gb[st], b2=ub[st]:
                                         e.tensor_tensor(o, a, b2, op=OP.mult)),
                                        waits=[(S_, s_g)], inc=V_)
                            fh_t = fh[:, sl * 1024 + tt * 512:][:, 0:512]
                            fl_t = fl[:, sl * 1024 + tt * 512:][:, 0:512]
                            w_fz = [(V_, v_f)]
                            if blk > 0:
                                w_fz.append((T_, t_blk_down))
                            s_fuse = p.add("scalar",
                                           (lambda e, o=fh_t, i=fb[st]: e.copy(o, i)),
                                           waits=w_fz, inc=S_)
                            v_fuse = p.add("vector",
                                           (lambda e, o=fl_t, i=fb[st], h=fh_t:
                                            e.tensor_tensor(o, i, h, op=OP.subtract)),
                                           waits=[(S_, s_fuse)], inc=V_)
                    if cidx + 2 < 16:
                        load_w(cidx + 2)
                fhv = fh.rearrange("p (s t) -> p s t", s=8)
                flv = fl.rearrange("p (s t) -> p s t", s=8)
                def load_wd(blk2, dmc2):
                    wbuf2 = dmc2 % 2
                    w2 = [(T_, t_down_last.get(wbuf2, 0))]
                    dg_wd[dmc2] = p.add("gpsimd",
                                        (lambda e, b=wbuf2, blk2=blk2, d2=dmc2:
                                         e.dma_start(out=wdb[b][:, :], in_=wd_in[blk2, d2, :, :])),
                                        waits=w2, inc=Dg, inc_n=16)
                if blk == 0:
                    load_wd(0, 0)
                for dmc in range(16):
                    wbuf = dmc % 2
                    wdv = wdb[wbuf].rearrange("p (s i) -> p s i", s=8)
                    for tt in range(2):
                        rot = 4 + (dmc * 2 + tt) % 2
                        w = [(Dg, dg_wd[dmc]), (S_, s_fuse), (V_, v_fuse),
                             (S_, s_dsb.get(rot, 0))]
                        first = True
                        for sub in range(8):
                            for pv, fv in ((0, fhv), (1, flv)):
                                t_d = mm(bank(rot),
                                         wdv[:, sub, :],
                                         fv[:, sub, tt * 512:(tt + 1) * 512],
                                         first, sub == 7 and pv == 1,
                                         waits=w if first else (),
                                         inc=T_ if (sub == 7 and pv == 1) else None)
                                first = False
                        t_down_last[wbuf] = t_d
                        t_blk_down = t_d
                        if tt == 1:
                            if dmc < 15:
                                load_wd(blk, dmc + 1)
                            elif blk < 7:
                                load_wd(blk + 1, 0)
                        dst = (dmc * 2 + tt) % 2
                        s_d = p.add("scalar",
                                    (lambda e, o=dsb[dst], b2=rot, dmc=dmc:
                                     e.activation(o, bank(b2), AF.Copy, scale=sdh_sb[:, dmc:dmc + 1])),
                                    waits=[(T_, t_d), (Da, dg_dsb.get(dst, 0))], inc=S_)
                        s_dsb[rot] = s_d
                        dg_dsb[dst] = p.add("gpsimd",
                                            (lambda e, i=dsb[dst], dmc=dmc, tt=tt:
                                             e.dma_start(out=xres[:, dmc * 1024 + tt * 512:][:, 0:512],
                                                         in_=i, accum_op=OP.add)),
                                            waits=[(S_, s_d)], inc=Da, inc_n=16)
            return p.c(Dg)

        agi_a = ag_in_a.ap().rearrange("(s p) (f t) -> s p f t", p=128, f=NFC)
        agi_b = ag_in_b.ap().rearrange("(s p) (f t) -> s p f t", p=128, f=NFC)
        ago_a = ag_out_a.ap().rearrange("(r s p) t -> r s p t", s=2, p=128)
        ago_b = ag_out_b.ap().rearrange("(r s p) t -> r s p t", s=2, p=128)

        def phase2_ag():
            v_last, s_last = p.c(V_), p.c(S_)
            yhv = yh.rearrange("p (f t) -> p f t", f=NFC)
            ylv = yl.rearrange("p (f t) -> p f t", f=NFC)
            p.add("sync", (lambda e: e.dma_start(out=agi_a[0], in_=yhv[:, :, 0:512])),
                  waits=[(V_, v_last), (S_, s_last)], inc=Dq, inc_n=16)
            p.add("sync", (lambda e: e.dma_start(out=agi_a[1], in_=ylv[:, :, 0:512])),
                  inc=Dq, inc_n=16)
            dq_ag = p.c(Dq)
            p.add("gpsimd", (lambda e: e.collective_compute(
                "AllGather", OP.bypass, replica_groups=[list(range(NCORE))],
                ins=[ag_in_a.ap().opt()], outs=[ag_out_a.ap().opt()])),
                waits=[(Dq, dq_ag)], inc=CC)
            p.add("sync", (lambda e: e.dma_start(out=agi_b[0], in_=yhv[:, :, 512:1024])),
                  inc=Dq, inc_n=16)
            p.add("sync", (lambda e: e.dma_start(out=agi_b[1], in_=ylv[:, :, 512:1024])),
                  inc=Dq, inc_n=16)
            dq_ag = p.c(Dq)
            p.add("gpsimd", (lambda e: e.collective_compute(
                "AllGather", OP.bypass, replica_groups=[list(range(NCORE))],
                ins=[ag_in_b.ap().opt()], outs=[ag_out_b.ap().opt()])),
                waits=[(Dq, dq_ag)], inc=CC)

        def attn_head(hl, cc_need):
            prev_use = [(T_, p.c(T_))]
            for ai, w_arr in enumerate((wq_in, wk_in, wv_in)):
                p.add("sync", (lambda e, w=w_arr: e.dma_start(
                    out=w32[:, :], in_=w[:, :, hl * 128:(hl + 1) * 128])),
                    waits=prev_use + ([(V_, p.c(V_))] if ai else []), inc=Dq, inc_n=16)
                dq_w = p.c(Dq)
                wsh_sl = wsh.rearrange("p (f j) -> p f j", f=NFC)[:, :, ai * 128:(ai + 1) * 128]
                wsl_sl = wsl.rearrange("p (f j) -> p f j", f=NFC)[:, :, ai * 128:(ai + 1) * 128]
                s_w = p.add("scalar", (lambda e, o=wsh_sl: e.copy(o, w32.rearrange(
                    "p (f j) -> p f j", f=NFC))), waits=[(Dq, dq_w)], inc=S_)
                v_w = p.add("vector", (lambda e, o=wsl_sl, h=wsh_sl: e.tensor_tensor(
                    o, w32.rearrange("p (f j) -> p f j", f=NFC), h, op=OP.subtract)),
                    waits=[(S_, s_w)], inc=V_)
                prev_use = [(V_, v_w)]
            s_wsplit, v_wsplit = p.c(S_), p.c(V_)

            wsh3 = wsh.rearrange("p (f j) -> p f j", f=NFC)
            wsl3 = wsl.rearrange("p (f j) -> p f j", f=NFC)
            t_y_use = {}
            v_split_bank = {}
            v_rd_prev = 0
            t_tp_use = {}
            s_tp_evac = {}
            for sh in range(NCORE):
                for tq in range(4):
                    gq = sh * 4 + tq
                    yb = gq % 2
                    w = [(CC, cc_need[0] if tq < 2 else cc_need[1])]
                    if gq >= 2:
                        w.append((T_, t_y_use[gq - 2]))
                    agh = ago_a if tq < 2 else ago_b
                    tq2 = tq % 2
                    for s2 in range(2):
                        p.add("sync", (lambda e, sh=sh, tq2=tq2, s2=s2, yb=yb, agh=agh: e.dma_start(
                            out=ysb[yb][s2].rearrange("p (f t) -> p f t", f=NFC),
                            in_=agh[sh, s2].rearrange("p (f t) -> p f t", f=NFC)
                                [:, :, tq2 * 256:(tq2 + 1) * 256])),
                            waits=w if s2 == 0 else (), inc=Dq, inc_n=16)
                    dq_y = p.c(Dq)
                    for t8 in range(2):
                        gt = gq * 2 + t8
                        pr = gt % 2
                        w0 = [(Dq, dq_y), (S_, s_wsplit), (V_, v_wsplit),
                              (V_, v_split_bank.get(pr, 0))]
                        ops48 = [(yb_i, wv3, fc)
                                 for yb_i, wv3 in ((0, wsh3), (1, wsh3), (0, wsl3))
                                 for fc in range(NFC)]
                        for i48, (yb_i, wv3, fc) in enumerate(ops48):
                            last = i48 == len(ops48) - 1
                            t_pj = mm(bank(pr, 384),
                                      ysb[yb][yb_i].rearrange("p (f t) -> p f t", f=NFC)
                                      [:, fc, t8 * 128:(t8 + 1) * 128],
                                      wv3[:, fc, :],
                                      i48 == 0, last,
                                      waits=w0 if i48 == 0 else (),
                                      inc=T_ if last else None)
                        t_y_use[gq] = t_pj
                        s_sq = p.add("scalar", (lambda e, pr=pr: e.square(psq[:, :], bank(pr, 384))),
                                     waits=[(T_, t_pj), (V_, v_rd_prev)], inc=S_)
                        v_rd = p.add("vector", (lambda e: e.tensor_reduce(
                            nrmt[:, 0:3], psq.rearrange("p (a j) -> p a j", a=3),
                            axis=mybir.AxisListType.X, op=OP.add)),
                            waits=[(S_, s_sq)], inc=V_)
                        v_rd_prev = v_rd
                        s_sr = p.add("scalar", lambda e: e.sqrt(nrmt[:, 4:7], nrmt[:, 0:3]),
                                     waits=[(V_, v_rd)], inc=S_)
                        v_ep = p.add("vector", (lambda e: e.tensor_scalar(
                            nrmt[:, 8:11], nrmt[:, 4:7], EPS_NRM, None, op0=OP.add, op1=OP.bypass)),
                            waits=[(S_, s_sr)], inc=V_)
                        p.add("vector", lambda e: e.reciprocal(nrmt[:, 12:15], nrmt[:, 8:11]),
                              waits=[(V_, v_ep)], inc=V_)
                        v_tau = p.add("vector", (lambda e: e.tensor_scalar_mul(
                            nrmt[:, 16:17], nrmt[:, 12:13], tau_bc[:, :])), inc=V_)
                        sc_q, sc_k, sc_v = nrmt[:, 16:17], nrmt[:, 13:14], nrmt[:, 14:15]
                        p.add("vector", (lambda e, pr=pr, gt=gt: e.tensor_scalar(
                            vh_tm[:, gt * 128:(gt + 1) * 128], bank(pr, 128, 256),
                            sc_v, None, op0=OP.mult, op1=OP.bypass)),
                            waits=[(V_, v_tau)], inc=V_)
                        v_vl = p.add("vector", (lambda e, pr=pr, gt=gt: e.scalar_tensor_tensor(
                            vl_tm[:, gt * 128:(gt + 1) * 128], bank(pr, 128, 256),
                            sc_v, vh_tm[:, gt * 128:(gt + 1) * 128],
                            op0=OP.mult, op1=OP.subtract)), inc=V_)
                        v_last_split = v_vl
                        for ai, (dh, dl, scl) in enumerate(((qh_fm, ql_fm, sc_q),
                                                           (kh_fm, kl_fm, sc_k))):
                            th, tl = ttt[2 * ai], ttt[2 * ai + 1]
                            v_h = p.add("vector", (lambda e, pr=pr, ai=ai, th=th, scl=scl:
                                e.tensor_scalar(th[:, :], bank(pr, 128, ai * 128), scl, None,
                                                op0=OP.mult, op1=OP.bypass)),
                                waits=[(T_, t_tp_use.get(2 * ai, 0))], inc=V_)
                            v_l = p.add("vector", (lambda e, pr=pr, ai=ai, th=th, tl=tl, scl=scl:
                                e.scalar_tensor_tensor(tl[:, :], bank(pr, 128, ai * 128),
                                                       scl, th[:, :],
                                                       op0=OP.mult, op1=OP.subtract)),
                                waits=[(T_, t_tp_use.get(2 * ai + 1, 0))], inc=V_)
                            v_last_split = v_l
                            for j, (tsrc, dst, vdep) in enumerate(((th, dh, v_h), (tl, dl, v_l))):
                                tpb = 2 + (2 * ai + j)
                                t_tp = p.add("tensor", (lambda e, tpb=tpb, tsrc=tsrc:
                                    e.transpose(bank(tpb, 64).bitcast(BF16),
                                                tsrc[:, :], eye[:, :])),
                                    waits=[(V_, vdep),
                                           (S_, s_tp_evac.get(tpb, 0))], inc=T_)
                                t_tp_use[2 * ai + j] = t_tp
                                s_tp_evac[tpb] = p.add("scalar", (lambda e, tpb=tpb, dst=dst, gt=gt:
                                    e.copy(dst[:, gt * 128:(gt + 1) * 128],
                                           bank(tpb, 64).bitcast(BF16))),
                                    waits=[(T_, t_tp)], inc=S_)
                        v_split_bank[pr] = v_last_split
            s_prep, v_prep, t_prep = p.c(S_), p.c(V_), p.c(T_)


            s_exp_bank = {}
            v_pl_buf = {}
            t_p_use = {}
            v_dn_free = {}
            v_pv_free = {}
            dq_osb = {}
            t_dn = t_pv = 0
            for qt in range(16):
                nkt = 4 * qt + 4
                dnb = 3 + qt % 2
                pvb = 5 + qt % 2
                pend = []
                for kt in range(nkt):
                    bp = kt % 4
                    bsc = kt % 3
                    w0 = [(S_, s_exp_bank.get(bsc, 0))]
                    if s_exp_bank.get(bsc, 0) == 0:
                        w0 += [(V_, v_prep), (S_, s_prep)]
                    if kt == 0:
                        w0 += [(V_, v_pv_free.get(pvb, 0)), (V_, v_dn_free.get(dnb, 0))]
                    mm(bank(bsc), kh_fm[:, kt * 128:(kt + 1) * 128],
                       qh_fm[:, qt * 512:(qt + 1) * 512], True, False, waits=w0)
                    mm(bank(bsc), kh_fm[:, kt * 128:(kt + 1) * 128],
                       ql_fm[:, qt * 512:(qt + 1) * 512], False, False)
                    t_sc = mm(bank(bsc), kl_fm[:, kt * 128:(kt + 1) * 128],
                              qh_fm[:, qt * 512:(qt + 1) * 512], False, True, inc=T_)
                    if len(pend) >= 2:
                        pend.pop(0)()
                    s_exp = p.add("scalar", (lambda e, bp=bp, bsc=bsc: e.activation(
                        p32[bp][:, :], bank(bsc), AF.Exp)),
                        waits=[(T_, t_sc), (V_, v_pl_buf.get(bp, 0))], inc=S_)
                    s_exp_bank[bsc] = s_exp
                    diag = kt >= 4 * qt
                    guard_T = [(T_, t_p_use.get(bp, 0))]
                    if diag:
                        dj = kt - 4 * qt
                        v_mask = p.add("vector", (lambda e, bp=bp, dj=dj: e.tensor_tensor(
                            p32[bp][:, :], p32[bp][:, :],
                            masks[:, dj * 512:(dj + 1) * 512], op=OP.mult)),
                            waits=[(S_, s_exp)], inc=V_)
                        s_ph = p.add("scalar", (lambda e, bp=bp: e.copy(phb[bp][:, :], p32[bp][:, :])),
                                     waits=[(V_, v_mask)] + guard_T, inc=S_)
                        ph_dep = (S_, s_ph)
                    else:
                        v_ph = p.add("vector", (lambda e, bp=bp: e.tensor_scalar(
                            phb[bp][:, :], p32[bp][:, :], 1.0, None,
                            op0=OP.mult, op1=OP.bypass)),
                            waits=[(S_, s_exp)] + guard_T, inc=V_)
                        ph_dep = (V_, v_ph)
                    v_pl = p.add("vector", (lambda e, bp=bp: e.tensor_tensor(
                        plb[bp][:, :], p32[bp][:, :], phb[bp][:, :], op=OP.subtract)),
                        waits=[ph_dep], inc=V_)
                    v_pl_buf[bp] = v_pl

                    def emit_dnpv(kt=kt, bp=bp, ph_dep=ph_dep, v_pl=v_pl):
                        nonlocal t_dn, t_pv
                        mm(bank(dnb), ones128_bf[:, :], phb[bp][:, :], kt == 0, False,
                           waits=[ph_dep])
                        t_dn = mm(bank(dnb), ones128_bf[:, :], plb[bp][:, :], False,
                                  kt == nkt - 1, waits=[(V_, v_pl)],
                                  inc=T_ if kt == nkt - 1 else None)
                        mm(bank(pvb), vh_tm[:, kt * 128:(kt + 1) * 128], phb[bp][:, :],
                           kt == 0, False)
                        mm(bank(pvb), vh_tm[:, kt * 128:(kt + 1) * 128], plb[bp][:, :],
                           False, False)
                        t_pv = mm(bank(pvb), vl_tm[:, kt * 128:(kt + 1) * 128], phb[bp][:, :],
                                  False, kt == nkt - 1, inc=T_)
                        t_p_use[bp] = t_pv
                    pend.append(emit_dnpv)
                for f in pend:
                    f()
                pend = []
                qtp = qt % 2
                v_rc = p.add("vector", (lambda e, qtp=qtp, dnb=dnb: e.reciprocal(
                    rc_sb[qtp][:, :], bank(dnb))), waits=[(T_, t_dn)], inc=V_)
                v_dn_free[dnb] = v_rc
                v_rcn = p.add("vector", (lambda e, qtp=qtp: e.tensor_scalar(
                    rc_sb[qtp][:, :], rc_sb[qtp][:, :], nu_bc[:, :], None,
                    op0=OP.mult, op1=OP.bypass)), waits=[(V_, v_rc)], inc=V_)
                v_pv_evac = p.add("vector", (lambda e, qtp=qtp, pvb=pvb: e.scalar_tensor_tensor(
                    osb[qtp][:, :], bank(pvb), 1.0, rc_sb[qtp][:, :],
                    op0=OP.mult, op1=OP.mult)),
                    waits=[(V_, v_rcn), (T_, t_pv), (Dq, dq_osb.get(qtp, 0))], inc=V_)
                v_pv_free[pvb] = v_pv_evac
                p.add("sync", (lambda e, qtp=qtp, qt=qt, hl=hl: e.dma_start(
                    out=a2a_in[hl].ap()[128 * (qt // 2): 128 * (qt // 2) + 128,
                                        (qt % 2) * 512:(qt % 2 + 1) * 512],
                    in_=osb[qtp][:, :])), waits=[(V_, v_pv_evac)], inc=Dq, inc_n=16)
                dq_osb[qtp] = p.c(Dq)

        def phase4_a2a(hl):
            dq_all = p.c(Dq)

            def _a2a(e, hl=hl):
                with allow_ici_all_to_all():
                    return e.collective_compute(
                        "AllToAll", OP.bypass, replica_groups=[list(range(NCORE))],
                        ins=[a2a_in[hl].ap().opt()], outs=[a2a_out[hl].ap().opt()])
            p.add("gpsimd", _a2a, waits=[(Dq, dq_all)], inc=CC)
            cc_n = p.c(CC)
            a2v = a2a_out[hl].ap().rearrange("(i p) t -> p i t", p=128)
            for i in range(NCORE):
                p.add("gpsimd", (lambda e, i=i, hl=hl: e.dma_start(
                    out=xres[:, (2 * i + hl) * 1024:(2 * i + hl + 1) * 1024],
                    in_=a2v[:, i, :], accum_op=OP.add)),
                    waits=[(CC, cc_n)] if i == 0 else (), inc=Da, inc_n=16)

        init()
        p.barrier()
        layer_norm()
        p.barrier()
        ffn()
        p.barrier()
        if PHASES >= 2:
            layer_norm(load_first=True)
            phase2_ag()
            cc_need = (p.c(CC) - 1, p.c(CC))
            p.barrier()
        if PHASES >= 3:
            attn_head(0, cc_need)
            p.barrier()
            if PHASES >= 4:
                phase4_a2a(0)
            attn_head(1, cc_need)
            if PHASES >= 4:
                phase4_a2a(1)
            p.barrier()
        if PHASES >= 5:
            layer_norm(load_first=True)
            p.barrier()
            ffn()
            p.barrier()

        p.add("sync", lambda e: e.dma_start(out=xa[:, :], in_=xres[:, :]), inc=Dq, inc_n=16)
        dq_ld = p.c(Dq)
        p.add("sync", lambda e: e.dma_start(out=y_out[:, :], in_=xa[:, :]),
              waits=[(Dq, dq_ld)], inc=Dq, inc_n=16)
        dq_st = p.c(Dq)
        p.add("gpsimd", lambda e: e.nop(), waits=[(Dq, dq_st)])

        stats = p.emit(block, sems)
        print("emit stats:", {k: tuple(v) for k, v in stats.items()}, "matmuls:", nmm[0])
    return nc


def stage_inputs(inputs):
    x = np.asarray(inputs["L0_attn_x"])[0]
    wq = np.asarray(inputs["L0_attn_wq"])
    wk = np.asarray(inputs["L0_attn_wk"])
    wv = np.asarray(inputs["L0_attn_wv"])
    tau = np.asarray(inputs["L0_attn_tau"]).reshape(1, 1)
    nu = np.asarray(inputs["L0_attn_nu"]).reshape(1, 1)
    wg = np.asarray(inputs["L0_ffn_w_gate"])
    wu = np.asarray(inputs["L0_ffn_w_up"])
    wd = np.asarray(inputs["L0_ffn_w_down"])
    sg = np.asarray(inputs["L0_ffn_s_gate"])
    su = np.asarray(inputs["L0_ffn_s_up"])
    sd = np.asarray(inputs["L0_ffn_s_down"])

    bf = ml_dtypes.bfloat16
    wg_s = np.ascontiguousarray(
        wg.reshape(NFC, 128, 16, 512).transpose(2, 1, 0, 3).reshape(16, 128, NFC * 512))
    wu_s = np.ascontiguousarray(
        wu.reshape(NFC, 128, 16, 512).transpose(2, 1, 0, 3).reshape(16, 128, NFC * 512))
    wd_s = np.ascontiguousarray(
        wd.reshape(8, 8, 128, 16, 128).transpose(0, 3, 2, 1, 4).reshape(8, 16, 128, 8 * 128))
    sg_s = np.ascontiguousarray(sg.reshape(64, 128).T)
    su_s = np.ascontiguousarray(su.reshape(64, 128).T)
    sd_s = np.ascontiguousarray(sd.reshape(16, 128).T)
    mask = np.ascontiguousarray(
        np.triu(np.ones((512, 512), np.float32)).reshape(4, 128, 512)
        .transpose(1, 0, 2).reshape(128, 4 * 512)).astype(bf)
    eye = np.eye(128, dtype=np.float32).astype(bf)

    in_maps = []
    for c in range(NCORE):
        rows = slice(c * TOK, (c + 1) * TOK)
        x_slab = np.ascontiguousarray(
            x[rows].T.reshape(NFC, 128, TOK).transpose(1, 0, 2).reshape(128, NFC * TOK))
        hc = slice(c * H_PER * DH, (c + 1) * H_PER * DH)
        wq_s = np.ascontiguousarray(wq[hc, :].T.reshape(NFC, 128, H_PER * DH).transpose(1, 0, 2))
        wk_s = np.ascontiguousarray(wk[hc, :].T.reshape(NFC, 128, H_PER * DH).transpose(1, 0, 2))
        wv_s = np.ascontiguousarray(wv[hc, :].T.reshape(NFC, 128, H_PER * DH).transpose(1, 0, 2))
        in_maps.append({
            "x": x_slab, "wq": wq_s, "wk": wk_s, "wv": wv_s,
            "tau": tau, "nu": nu,
            "wg": wg_s, "wu": wu_s, "wd": wd_s,
            "sg": sg_s, "su": su_s, "sd": sd_s,
            "mask": mask, "eye": eye,
        })
    return in_maps


def unstage_output(results):
    outs = []
    for c in range(NCORE):
        slab = results[c]["y"]
        xt = slab.reshape(128, NFC, TOK).transpose(1, 0, 2).reshape(C, TOK)
        outs.append(np.ascontiguousarray(xt.T))
    return np.concatenate(outs, axis=0)[None]


def kernel(**inputs):
    in_maps = stage_inputs(inputs)
    nc = build_nc()
    res = run_bass_kernel_spmd(nc, in_maps, core_ids=list(range(NCORE)))
    return unstage_output(res.results)


# version 10
# speedup vs baseline: 1.3778x; 1.0042x over previous
import sys
if "/opt/trn_rl_repo" not in sys.path:
    sys.path.insert(0, "/opt/trn_rl_repo")
import numpy as np
import ml_dtypes
import concourse.bass as bass
import concourse.mybir as mybir
from concourse.bass_utils import run_bass_kernel_spmd
from concourse.aot_env import allow_ici_all_to_all

F32 = mybir.dt.float32
BF16 = mybir.dt.bfloat16
I8 = mybir.dt.int8
AF = mybir.ActivationFunctionType
OP = mybir.AluOpType

NCORE = 8
C = 2048
TOK = 1024
T = 8192
INTER = 8192
H_PER = 2
DH = 128
NFC = C // 128
EPS_LN = 1e-5
EPS_NRM = 1e-8

PHASES = 6


class Plan:

    ENGS = ("tensor", "vector", "scalar", "sync", "gpsimd")

    def __init__(self):
        self.ops = {e: [] for e in self.ENGS}
        self.cnt = {}
        self.sem_names = []

    def sem(self, name):
        if name not in self.cnt:
            self.cnt[name] = 0
            self.sem_names.append(name)
        return name

    def c(self, sem):
        return self.cnt.get(sem, 0)

    def add(self, eng, fn, waits=(), inc=None, inc_n=1):
        for s, v in waits:
            assert v <= self.cnt.get(s, 0), (s, v, self.cnt.get(s, 0))
        if inc is not None:
            self.sem(inc)
            self.cnt[inc] += inc_n
        self.ops[eng].append((fn, tuple(waits), inc, inc_n))
        return self.cnt.get(inc, 0) if inc is not None else None

    def barrier(self):
        snap = dict(self.cnt)
        eng_sem = {"tensor": "T", "vector": "V", "scalar": "S",
                   "sync": "Dq", "gpsimd": "Dg"}
        for e in self.ENGS:
            waits = tuple((s, v) for s, v in snap.items()
                          if v > 0 and s != eng_sem.get(e))
            self.ops[e].append((None, waits, None, 0))

    def emit(self, block, sems):
        last_wait = {e: {} for e in self.ENGS}
        stats = {e: [0, 0] for e in self.ENGS}

        def emit_for(eng_name):
            def go(eng):
                lw = last_wait[eng_name]
                for fn, waits, inc, inc_n in self.ops[eng_name]:
                    for s, v in waits:
                        if lw.get(s, -1) >= v:
                            continue
                        eng.wait_ge(sems[s], v)
                        lw[s] = v
                        stats[eng_name][1] += 1
                    if fn is None:
                        continue
                    instr = fn(eng)
                    stats[eng_name][0] += 1
                    if inc is not None:
                        instr.then_inc(sems[inc], inc_n)
            return go

        for e in self.ENGS:
            getattr(block, e)(emit_for(e))
        return stats


def build_nc():
    nc = bass.Bass("TRN2", target_bir_lowering=False, debug=False)
    p = Plan()
    T_, V_, S_, Dq, Dg, CC, Da = (p.sem(s) for s in ("T", "V", "S", "Dq", "Dg", "CC", "Da"))

    x_in = nc.dram_tensor("x", [128, NFC * TOK], F32, kind="ExternalInput").ap()
    wq_in = nc.dram_tensor("wq", [128, NFC, H_PER * DH], F32, kind="ExternalInput").ap()
    wk_in = nc.dram_tensor("wk", [128, NFC, H_PER * DH], F32, kind="ExternalInput").ap()
    wv_in = nc.dram_tensor("wv", [128, NFC, H_PER * DH], F32, kind="ExternalInput").ap()
    tau_in = nc.dram_tensor("tau", [1, 1], F32, kind="ExternalInput").ap()
    nu_in = nc.dram_tensor("nu", [1, 1], F32, kind="ExternalInput").ap()
    wg_in = nc.dram_tensor("wg", [16, 128, NFC * 512], I8, kind="ExternalInput").ap()
    wu_in = nc.dram_tensor("wu", [16, 128, NFC * 512], I8, kind="ExternalInput").ap()
    wd_in = nc.dram_tensor("wd", [8, 16, 128, 8 * 128], I8, kind="ExternalInput").ap()
    sg_in = nc.dram_tensor("sg", [128, 64], F32, kind="ExternalInput").ap()
    su_in = nc.dram_tensor("su", [128, 64], F32, kind="ExternalInput").ap()
    sd_in = nc.dram_tensor("sd", [128, 16], F32, kind="ExternalInput").ap()
    mask_in = nc.dram_tensor("mask", [128, 4 * 512], BF16, kind="ExternalInput").ap()
    eye_in = nc.dram_tensor("eye", [128, 128], BF16, kind="ExternalInput").ap()
    y_out = nc.dram_tensor("y", [128, NFC * TOK], F32, kind="ExternalOutput").ap()

    xres = nc.dram_tensor("xres", [128, NFC * TOK], F32).ap()
    ag_in_a = nc.dram_tensor("ag_in_a", [2 * 128, NFC * 512], BF16)
    ag_in_b = nc.dram_tensor("ag_in_b", [2 * 128, NFC * 512], BF16)
    ag_out_a = nc.dram_tensor("ag_out_a", [NCORE * 2 * 128, NFC * 512], BF16)
    ag_out_b = nc.dram_tensor("ag_out_b", [NCORE * 2 * 128, NFC * 512], BF16)
    a2a_in = [nc.dram_tensor("a2a_in0", [C // 2, TOK], F32),
              nc.dram_tensor("a2a_in1", [C // 2, TOK], F32)]
    a2a_out = [nc.dram_tensor("a2a_out0", [C // 2, TOK], F32),
               nc.dram_tensor("a2a_out1", [C // 2, TOK], F32)]

    with (
        nc.sbuf_tensor("AR1", [128, 16384], F32) as AR1,
        nc.sbuf_tensor("AR2", [128, 16384], F32) as AR2,
        nc.sbuf_tensor("AR3", [128, 16384], F32) as AR3,
        nc.sbuf_tensor("AR4", [128, 2816], F32) as AR4,
        nc.psum_tensor("PS", [128, 4096], F32) as PS,
        nc.semaphore("sT") as sT, nc.semaphore("sV") as sV,
        nc.semaphore("sS") as sS, nc.semaphore("sDq") as sDq,
        nc.semaphore("sDg") as sDg, nc.semaphore("sCC") as sCC,
        nc.semaphore("sDa") as sDa,
        nc.Block() as block,
    ):
        sems = {"T": sT, "V": sV, "S": sS, "Dq": sDq, "Dg": sDg, "CC": sCC, "Da": sDa}

        def bank(i, width=512, off=0):
            return PS[:, i * 512 + off: i * 512 + off + width]

        def bank_row(i, width=512):
            return PS[0:1, i * 512: i * 512 + width]

        xa = AR1
        wgb = [AR1[:, 0:4096].bitcast(BF16), AR1[:, 4096:8192].bitcast(BF16)]
        wub = [AR1[:, 8192:12288].bitcast(BF16), AR1[:, 12288:16384].bitcast(BF16)]
        qh_fm = AR1[:, 0:4096].bitcast(BF16)
        ql_fm = AR1[:, 4096:8192].bitcast(BF16)
        kh_fm = AR1[:, 8192:12288].bitcast(BF16)
        kl_fm = AR1[:, 12288:16384].bitcast(BF16)
        yh = AR2[:, 0:8192].bitcast(BF16)
        yl = AR2[:, 8192:16384].bitcast(BF16)
        vh_tm = AR2[:, 0:4096].bitcast(BF16)
        vl_tm = AR2[:, 4096:8192].bitcast(BF16)
        ysb = [(AR2[:, 8192:10240].bitcast(BF16), AR2[:, 10240:12288].bitcast(BF16)),
               (AR2[:, 12288:14336].bitcast(BF16), AR2[:, 14336:16384].bitcast(BF16))]
        x2t = [AR3[:, 0:1024], AR3[:, 1024:2048]]
        bca = AR3[:, 2048:3072]
        bcb = AR3[:, 3072:4096]
        lst = [AR3[:, 4096:4608], AR3[:, 4608:5120]]
        lsy = [AR3[:, 5120:5632], AR3[:, 5632:6144]]
        fh = AR3[:, 0:4096].bitcast(BF16)
        fl = AR3[:, 4096:8192].bitcast(BF16)
        wdb = [AR3[:, 8192:8704].bitcast(BF16), AR3[:, 8704:9216].bitcast(BF16)]
        gb = [AR3[:, 9216:9728], AR3[:, 9728:10240], AR3[:, 13312:13824]]
        ub = [AR3[:, 10240:10752], AR3[:, 10752:11264], AR3[:, 13824:14336]]
        fb = [AR3[:, 11264:11776], AR3[:, 11776:12288], AR3[:, 14336:14848]]
        dsb = [AR3[:, 12288:12800], AR3[:, 12800:13312]]
        w32 = AR3[:, 0:2048]
        wsh = AR3[:, 2048:5120].bitcast(BF16)
        wsl = AR3[:, 5120:8192].bitcast(BF16)
        psq = AR3[:, 8192:8576]
        nrmt = AR3[:, 8576:8608]
        p32 = [AR3[:, 8704 + 512 * i: 9216 + 512 * i] for i in range(4)]
        phb = [AR3[:, 10752 + 256 * i: 11008 + 256 * i].bitcast(BF16) for i in range(4)]
        plb = [AR3[:, 11776 + 256 * i: 12032 + 256 * i].bitcast(BF16) for i in range(4)]
        rc_sb = [AR3[:, 12800:13312], AR3[:, 13312:13824]]
        osb = [AR3[:, 13824:14336], AR3[:, 14336:14848]]
        ttt = [AR3[:, 14848 + 64 * i: 14912 + 64 * i].bitcast(BF16) for i in range(4)]
        masks = AR4[:, 0:1024].bitcast(BF16)
        eye = AR4[:, 1024:1088].bitcast(BF16)
        ones_bf = AR4[:, 1088:1089].bitcast(BF16)[:, 0:1]
        ones_f = AR4[:, 1090:1091]
        ones_row = AR4[0:1, 1092:1220]
        sg_sb = AR4[:, 1220:1284]
        su_sb = AR4[:, 1284:1348]
        sdh_sb = AR4[:, 1348:1364]
        sd_sb = AR4[:, 1364:1380]
        tau_bc = AR4[:, 1380:1381]
        nu_sb = AR4[0:1, 1382:1383]
        tau_sb = AR4[0:1, 1384:1385]
        ones128_bf = AR4[:, 1408:1472].bitcast(BF16)
        ones128_f = AR4[:, 1472:1600]
        nu_bc = AR4[:, 1600:1601]
        r_s1 = AR3[0:1, 6144:7168]
        r_s2 = AR3[0:1, 7168:8192]
        r_t0 = AR3[0:1, 8192:9216]
        r_a = AR3[0:1, 9216:10240]
        r_b = AR3[0:1, 10240:11264]
        r_m = r_s1

        nmm = [0]

        def mm(out, lhsT, rhs, start, stop, waits=(), inc=None):
            def f(e, out=out, lhsT=lhsT, rhs=rhs, start=start, stop=stop):
                return e.matmul(out, lhsT, rhs, start=start, stop=stop)
            nmm[0] += 1
            return p.add("tensor", f, waits=waits, inc=inc)

        def init():
            p.add("sync", lambda e: e.dma_start(out=xa[:, :], in_=x_in[:, :]), inc=Dq, inc_n=16)
            p.add("sync", lambda e: e.dma_start(out=masks[:, :], in_=mask_in[:, :]), inc=Dq, inc_n=16)
            p.add("sync", lambda e: e.dma_start(out=eye[:, :], in_=eye_in[:, :]), inc=Dq, inc_n=16)
            p.add("sync", lambda e: e.dma_start(out=sg_sb[:, :], in_=sg_in[:, :]), inc=Dq, inc_n=16)
            p.add("sync", lambda e: e.dma_start(out=su_sb[:, :], in_=su_in[:, :]), inc=Dq, inc_n=16)
            p.add("sync", lambda e: e.dma_start(out=sd_sb[:, :], in_=sd_in[:, :]), inc=Dq, inc_n=16)
            p.add("sync", lambda e: e.dma_start(out=nu_sb[:, :], in_=nu_in[:, :]), inc=Dq, inc_n=16)
            p.add("sync", lambda e: e.dma_start(out=tau_sb[:, :], in_=tau_in[:, :]), inc=Dq, inc_n=16)
            dq0 = p.c(Dq)
            p.add("vector", lambda e: e.memset(ones_bf[:, :], 1.0), inc=V_)
            p.add("vector", lambda e: e.memset(ones_f[:, :], 1.0), inc=V_)
            p.add("vector", lambda e: e.memset(ones_row[:, :], 1.0), inc=V_)
            p.add("vector", lambda e: e.tensor_scalar_mul(sdh_sb[:, :], sd_sb[:, :], 0.5),
                  waits=[(Dq, dq0)], inc=V_)
            p.add("vector", lambda e: e.memset(ones128_bf[:, :], 1.0), inc=V_)
            p.add("vector", lambda e: e.memset(ones128_f[:, :], 1.0), inc=V_)
            mm(bank(7, 1), ones_row[:, :], tau_sb[:, :], True, True,
               waits=[(V_, p.c(V_)), (Dq, dq0)], inc=T_)
            mm(bank(6, 1), ones_row[:, :], nu_sb[:, :], True, True, inc=T_)
            p.add("scalar", lambda e: e.copy(tau_bc[:, :], bank(7, 1)),
                  waits=[(T_, p.c(T_))], inc=S_)
            p.add("scalar", lambda e: e.copy(nu_bc[:, :], bank(6, 1)), inc=S_)
            p.add("gpsimd", lambda e: e.dma_start(out=xres[:, :], in_=xa[:, :]),
                  waits=[(Dq, dq0)], inc=Da, inc_n=16)

        def layer_norm(load_first=False):
            if load_first:
                p.add("sync", lambda e: e.dma_start(out=xa[:, :], in_=xres[:, :]),
                      inc=Dq, inc_n=16)
                dq_x = p.c(Dq)
                p.ops["tensor"].append((None, ((Dq, dq_x),), None, 0))
                p.ops["scalar"].append((None, ((Dq, dq_x),), None, 0))
                p.ops["vector"].append((None, ((Dq, dq_x),), None, 0))
            t_sq = {}
            for fc in range(NFC):
                xr = xa[:, fc * 1024:(fc + 1) * 1024]
                b = fc % 2
                w = [(T_, t_sq[fc - 2])] if fc >= 2 else []
                t_sq[fc] = None
                s_done = p.add("scalar", (lambda e, o=x2t[b], i=xr: e.square(o, i)),
                               waits=w, inc=S_)
                for tt in range(2):
                    mm(bank(0 + tt), ones128_f[:, :], xr[:, tt * 512:(tt + 1) * 512],
                       fc == 0, fc == NFC - 1)
                    t_sq[fc] = mm(bank(2 + tt), ones128_f[:, :],
                                  x2t[b][:, tt * 512:(tt + 1) * 512],
                                  fc == 0, fc == NFC - 1,
                                  waits=[(S_, s_done)], inc=T_ if tt == 1 else None)
            t_stats = p.c(T_)
            p.add("vector", lambda e: e.tensor_copy(r_s1[:, 0:512], bank_row(0)), waits=[(T_, t_stats)], inc=V_)
            p.add("vector", lambda e: e.tensor_copy(r_s1[:, 512:1024], bank_row(1)), inc=V_)
            p.add("vector", lambda e: e.tensor_copy(r_s2[:, 0:512], bank_row(2)), inc=V_)
            p.add("vector", lambda e: e.tensor_copy(r_s2[:, 512:1024], bank_row(3)), inc=V_)
            p.add("vector", lambda e: e.tensor_scalar_mul(r_m[:, :], r_s1[:, :], 1.0 / C), inc=V_)
            p.add("vector", lambda e: e.tensor_tensor(r_t0[:, :], r_m[:, :], r_m[:, :], op=OP.mult), inc=V_)
            p.add("vector", lambda e: e.tensor_scalar(r_s2[:, :], r_s2[:, :], 1.0 / C, EPS_LN,
                                                      op0=OP.mult, op1=OP.add), inc=V_)
            v_pre = p.add("vector", lambda e: e.tensor_tensor(r_t0[:, :], r_s2[:, :], r_t0[:, :],
                                                              op=OP.subtract), inc=V_)
            s_sqrt = p.add("scalar", lambda e: e.sqrt(r_t0[:, :], r_t0[:, :]),
                           waits=[(V_, v_pre)], inc=S_)
            p.add("vector", lambda e: e.reciprocal(r_a[:, :], r_t0[:, :]),
                  waits=[(S_, s_sqrt)], inc=V_)
            p.add("vector", lambda e: e.tensor_tensor(r_b[:, :], r_m[:, :], r_a[:, :], op=OP.mult), inc=V_)
            v_rows = p.add("vector", lambda e: e.tensor_scalar_mul(r_b[:, :], r_b[:, :], -1.0), inc=V_)
            for tt in range(2):
                mm(bank(4 + 2 * tt), ones_row[:, :], r_a[:, tt * 512:(tt + 1) * 512], True, True,
                   waits=[(V_, v_rows)])
                t_bc = mm(bank(5 + 2 * tt), ones_row[:, :], r_b[:, tt * 512:(tt + 1) * 512],
                          True, True, inc=T_)
                p.add("scalar", (lambda e, tt=tt: e.copy(bca[:, tt * 512:(tt + 1) * 512], bank(4 + 2 * tt))),
                      waits=[(T_, t_bc)], inc=S_)
                p.add("scalar", (lambda e, tt=tt: e.copy(bcb[:, tt * 512:(tt + 1) * 512], bank(5 + 2 * tt))),
                      inc=S_)
            s_bc = p.c(S_)
            for fc in range(NFC):
                for tt in range(2):
                    xrt = xa[:, fc * 1024 + tt * 512: fc * 1024 + tt * 512 + 512]
                    st = tt
                    yh_t = yh[:, fc * 1024 + tt * 512: fc * 1024 + tt * 512 + 512]
                    yl_t = yl[:, fc * 1024 + tt * 512: fc * 1024 + tt * 512 + 512]
                    p.add("vector", (lambda e, o=lst[st], i=xrt, b2=bca[:, tt * 512:(tt + 1) * 512]:
                                     e.tensor_tensor(o, i, b2, op=OP.mult)),
                          waits=[(S_, s_bc)], inc=V_)
                    v_y = p.add("vector", (lambda e, o=lsy[st], i=lst[st], b2=bcb[:, tt * 512:(tt + 1) * 512]:
                                           e.tensor_tensor(o, i, b2, op=OP.add)), inc=V_)
                    s_h = p.add("scalar", (lambda e, o=yh_t, i=lsy[st]: e.copy(o, i)),
                                waits=[(V_, v_y)], inc=S_)
                    p.add("vector", (lambda e, o=yl_t, i=lsy[st], h=yh_t:
                                     e.tensor_tensor(o, i, h, op=OP.subtract)),
                          waits=[(S_, s_h)], inc=V_)

        def ffn():
            t_gate_last = {}
            t_up_last = {}
            t_down_last = {}
            s_evac = {}
            v_evac = {}
            s_fuse = 0
            v_fuse = 0
            dg_w = {}
            dg_wd = {}
            s_dsb = {}
            dg_dsb = {}
            y_ready = (V_, p.c(V_))

            def load_w(cidx):
                buf = cidx % 2
                w = []
                if cidx >= 2:
                    w = [(T_, t_gate_last.get(buf, 0))]
                p.add("gpsimd", (lambda e, b=buf, ci=cidx:
                                 e.dma_start(out=wgb[b][:, :], in_=wg_in[ci, :, :])),
                      waits=w, inc=Dg, inc_n=16)
                if cidx >= 2:
                    w = [(T_, t_up_last.get(buf, 0))]
                dg_w[cidx] = p.add("gpsimd", (lambda e, b=buf, ci=cidx:
                                              e.dma_start(out=wub[b][:, :], in_=wu_in[ci, :, :])),
                                   waits=w, inc=Dg, inc_n=16)

            load_w(0)
            load_w(1)
            for blk in range(8):
                for ic in range(2):
                    cidx = blk * 2 + ic
                    buf = cidx % 2
                    wgv = wgb[buf].rearrange("p (f i) -> p f i", f=NFC)
                    wuv = wub[buf].rearrange("p (f i) -> p f i", f=NFC)
                    for sub in range(4):
                        ig = cidx * 4 + sub
                        sl = ic * 4 + sub
                        for tt in range(2):
                            rot = (sub * 2 + tt) % 3
                            gbank, ubank = (0, 1, 6)[rot], (2, 3, 7)[rot]
                            wg_w = [(Dg, dg_w[cidx]), y_ready,
                                    (S_, s_evac.get(gbank, 0))]
                            wu_w = [(V_, v_evac.get(ubank, 0))]
                            for pi, ysl in enumerate((yh, yl)):
                                for fc in range(NFC):
                                    first = pi == 0 and fc == 0
                                    last = pi == 1 and fc == NFC - 1
                                    mv = ysl[:, fc * 1024 + tt * 512:][:, 0:512]
                                    t_g = mm(bank(gbank),
                                             wgv[:, fc, sub * 128:(sub + 1) * 128],
                                             mv, first, last,
                                             waits=wg_w if first else (),
                                             inc=T_ if last else None)
                                    t_u = mm(bank(ubank),
                                             wuv[:, fc, sub * 128:(sub + 1) * 128],
                                             mv, first, last,
                                             waits=wu_w if first else (),
                                             inc=T_ if last else None)
                            t_gate_last[buf] = t_g
                            t_up_last[buf] = t_u
                            st = rot
                            s_g = p.add("scalar",
                                        (lambda e, o=gb[st], b2=gbank, ig=ig:
                                         e.activation(o, bank(b2), AF.Silu, scale=sg_sb[:, ig:ig + 1])),
                                        waits=[(T_, t_g)], inc=S_)
                            s_evac[gbank] = s_g
                            v_u = p.add("vector",
                                        (lambda e, o=ub[st], b2=ubank, ig=ig:
                                         e.tensor_scalar_mul(o, bank(b2), su_sb[:, ig:ig + 1])),
                                        waits=[(T_, t_u)], inc=V_)
                            v_evac[ubank] = v_u
                            v_f = p.add("vector",
                                        (lambda e, o=fb[st], a=gb[st], b2=ub[st]:
                                         e.tensor_tensor(o, a, b2, op=OP.mult)),
                                        waits=[(S_, s_g)], inc=V_)
                            fh_t = fh[:, sl * 1024 + tt * 512:][:, 0:512]
                            fl_t = fl[:, sl * 1024 + tt * 512:][:, 0:512]
                            w_fz = [(V_, v_f)]
                            if blk > 0:
                                w_fz.append((T_, t_blk_down))
                            s_fuse = p.add("scalar",
                                           (lambda e, o=fh_t, i=fb[st]: e.copy(o, i)),
                                           waits=w_fz, inc=S_)
                            v_fuse = p.add("vector",
                                           (lambda e, o=fl_t, i=fb[st], h=fh_t:
                                            e.tensor_tensor(o, i, h, op=OP.subtract)),
                                           waits=[(S_, s_fuse)], inc=V_)
                    if cidx + 2 < 16:
                        load_w(cidx + 2)
                fhv = fh.rearrange("p (s t) -> p s t", s=8)
                flv = fl.rearrange("p (s t) -> p s t", s=8)
                def load_wd(blk2, dmc2):
                    wbuf2 = dmc2 % 2
                    w2 = [(T_, t_down_last.get(wbuf2, 0))]
                    dg_wd[dmc2] = p.add("gpsimd",
                                        (lambda e, b=wbuf2, blk2=blk2, d2=dmc2:
                                         e.dma_start(out=wdb[b][:, :], in_=wd_in[blk2, d2, :, :])),
                                        waits=w2, inc=Dg, inc_n=16)
                if blk == 0:
                    load_wd(0, 0)
                for dmc in range(16):
                    wbuf = dmc % 2
                    wdv = wdb[wbuf].rearrange("p (s i) -> p s i", s=8)
                    for tt in range(2):
                        rot = 4 + (dmc * 2 + tt) % 2
                        w = [(Dg, dg_wd[dmc]), (S_, s_fuse), (V_, v_fuse),
                             (S_, s_dsb.get(rot, 0))]
                        first = True
                        for sub in range(8):
                            for pv, fv in ((0, fhv), (1, flv)):
                                t_d = mm(bank(rot),
                                         wdv[:, sub, :],
                                         fv[:, sub, tt * 512:(tt + 1) * 512],
                                         first, sub == 7 and pv == 1,
                                         waits=w if first else (),
                                         inc=T_ if (sub == 7 and pv == 1) else None)
                                first = False
                        t_down_last[wbuf] = t_d
                        t_blk_down = t_d
                        if tt == 1:
                            if dmc < 15:
                                load_wd(blk, dmc + 1)
                            elif blk < 7:
                                load_wd(blk + 1, 0)
                        dst = (dmc * 2 + tt) % 2
                        s_d = p.add("scalar",
                                    (lambda e, o=dsb[dst], b2=rot, dmc=dmc:
                                     e.activation(o, bank(b2), AF.Copy, scale=sdh_sb[:, dmc:dmc + 1])),
                                    waits=[(T_, t_d), (Da, dg_dsb.get(dst, 0))], inc=S_)
                        s_dsb[rot] = s_d
                        dg_dsb[dst] = p.add("gpsimd",
                                            (lambda e, i=dsb[dst], dmc=dmc, tt=tt:
                                             e.dma_start(out=xres[:, dmc * 1024 + tt * 512:][:, 0:512],
                                                         in_=i, accum_op=OP.add)),
                                            waits=[(S_, s_d)], inc=Da, inc_n=16)
            return p.c(Dg)

        agi_a = ag_in_a.ap().rearrange("(s p) (f t) -> s p f t", p=128, f=NFC)
        agi_b = ag_in_b.ap().rearrange("(s p) (f t) -> s p f t", p=128, f=NFC)
        ago_a = ag_out_a.ap().rearrange("(r s p) t -> r s p t", s=2, p=128)
        ago_b = ag_out_b.ap().rearrange("(r s p) t -> r s p t", s=2, p=128)

        def phase2_ag():
            v_last, s_last = p.c(V_), p.c(S_)
            yhv = yh.rearrange("p (f t) -> p f t", f=NFC)
            ylv = yl.rearrange("p (f t) -> p f t", f=NFC)
            p.add("sync", (lambda e: e.dma_start(out=agi_a[0], in_=yhv[:, :, 0:512])),
                  waits=[(V_, v_last), (S_, s_last)], inc=Dq, inc_n=16)
            p.add("sync", (lambda e: e.dma_start(out=agi_a[1], in_=ylv[:, :, 0:512])),
                  inc=Dq, inc_n=16)
            dq_ag = p.c(Dq)
            p.add("gpsimd", (lambda e: e.collective_compute(
                "AllGather", OP.bypass, replica_groups=[list(range(NCORE))],
                ins=[ag_in_a.ap().opt()], outs=[ag_out_a.ap().opt()])),
                waits=[(Dq, dq_ag)], inc=CC)
            p.add("sync", (lambda e: e.dma_start(out=agi_b[0], in_=yhv[:, :, 512:1024])),
                  inc=Dq, inc_n=16)
            p.add("sync", (lambda e: e.dma_start(out=agi_b[1], in_=ylv[:, :, 512:1024])),
                  inc=Dq, inc_n=16)
            dq_ag = p.c(Dq)
            p.add("gpsimd", (lambda e: e.collective_compute(
                "AllGather", OP.bypass, replica_groups=[list(range(NCORE))],
                ins=[ag_in_b.ap().opt()], outs=[ag_out_b.ap().opt()])),
                waits=[(Dq, dq_ag)], inc=CC)

        def attn_head(hl, cc_need):
            prev_use = [(T_, p.c(T_))]
            for ai, w_arr in enumerate((wq_in, wk_in, wv_in)):
                p.add("sync", (lambda e, w=w_arr: e.dma_start(
                    out=w32[:, :], in_=w[:, :, hl * 128:(hl + 1) * 128])),
                    waits=prev_use + ([(V_, p.c(V_))] if ai else []), inc=Dq, inc_n=16)
                dq_w = p.c(Dq)
                wsh_sl = wsh.rearrange("p (f j) -> p f j", f=NFC)[:, :, ai * 128:(ai + 1) * 128]
                wsl_sl = wsl.rearrange("p (f j) -> p f j", f=NFC)[:, :, ai * 128:(ai + 1) * 128]
                s_w = p.add("scalar", (lambda e, o=wsh_sl: e.copy(o, w32.rearrange(
                    "p (f j) -> p f j", f=NFC))), waits=[(Dq, dq_w)], inc=S_)
                v_w = p.add("vector", (lambda e, o=wsl_sl, h=wsh_sl: e.tensor_tensor(
                    o, w32.rearrange("p (f j) -> p f j", f=NFC), h, op=OP.subtract)),
                    waits=[(S_, s_w)], inc=V_)
                prev_use = [(V_, v_w)]
            s_wsplit, v_wsplit = p.c(S_), p.c(V_)

            wsh3 = wsh.rearrange("p (f j) -> p f j", f=NFC)
            wsl3 = wsl.rearrange("p (f j) -> p f j", f=NFC)
            t_y_use = {}
            v_split_bank = {}
            v_rd_prev = 0
            t_tp_use = {}
            s_tp_evac = {}
            for sh in range(NCORE):
                for tq in range(4):
                    gq = sh * 4 + tq
                    yb = gq % 2
                    w = [(CC, cc_need[0] if tq < 2 else cc_need[1])]
                    if gq >= 2:
                        w.append((T_, t_y_use[gq - 2]))
                    agh = ago_a if tq < 2 else ago_b
                    tq2 = tq % 2
                    for s2 in range(2):
                        p.add("sync", (lambda e, sh=sh, tq2=tq2, s2=s2, yb=yb, agh=agh: e.dma_start(
                            out=ysb[yb][s2].rearrange("p (f t) -> p f t", f=NFC),
                            in_=agh[sh, s2].rearrange("p (f t) -> p f t", f=NFC)
                                [:, :, tq2 * 256:(tq2 + 1) * 256])),
                            waits=w if s2 == 0 else (), inc=Dq, inc_n=16)
                    dq_y = p.c(Dq)
                    for t8 in range(2):
                        gt = gq * 2 + t8
                        pr = gt % 2
                        w0 = [(Dq, dq_y), (S_, s_wsplit), (V_, v_wsplit),
                              (V_, v_split_bank.get(pr, 0))]
                        ops48 = [(yb_i, wv3, fc)
                                 for yb_i, wv3 in ((0, wsh3), (1, wsh3), (0, wsl3))
                                 for fc in range(NFC)]
                        for i48, (yb_i, wv3, fc) in enumerate(ops48):
                            last = i48 == len(ops48) - 1
                            t_pj = mm(bank(pr, 384),
                                      ysb[yb][yb_i].rearrange("p (f t) -> p f t", f=NFC)
                                      [:, fc, t8 * 128:(t8 + 1) * 128],
                                      wv3[:, fc, :],
                                      i48 == 0, last,
                                      waits=w0 if i48 == 0 else (),
                                      inc=T_ if last else None)
                        t_y_use[gq] = t_pj
                        s_sq = p.add("scalar", (lambda e, pr=pr: e.square(psq[:, :], bank(pr, 384))),
                                     waits=[(T_, t_pj), (V_, v_rd_prev)], inc=S_)
                        v_rd = p.add("vector", (lambda e: e.tensor_reduce(
                            nrmt[:, 0:3], psq.rearrange("p (a j) -> p a j", a=3),
                            axis=mybir.AxisListType.X, op=OP.add)),
                            waits=[(S_, s_sq)], inc=V_)
                        v_rd_prev = v_rd
                        s_sr = p.add("scalar", lambda e: e.sqrt(nrmt[:, 4:7], nrmt[:, 0:3]),
                                     waits=[(V_, v_rd)], inc=S_)
                        v_ep = p.add("vector", (lambda e: e.tensor_scalar(
                            nrmt[:, 8:11], nrmt[:, 4:7], EPS_NRM, None, op0=OP.add, op1=OP.bypass)),
                            waits=[(S_, s_sr)], inc=V_)
                        p.add("vector", lambda e: e.reciprocal(nrmt[:, 12:15], nrmt[:, 8:11]),
                              waits=[(V_, v_ep)], inc=V_)
                        v_tau = p.add("vector", (lambda e: e.tensor_scalar_mul(
                            nrmt[:, 16:17], nrmt[:, 12:13], tau_bc[:, :])), inc=V_)
                        sc_q, sc_k, sc_v = nrmt[:, 16:17], nrmt[:, 13:14], nrmt[:, 14:15]
                        p.add("vector", (lambda e, pr=pr, gt=gt: e.tensor_scalar(
                            vh_tm[:, gt * 128:(gt + 1) * 128], bank(pr, 128, 256),
                            sc_v, None, op0=OP.mult, op1=OP.bypass)),
                            waits=[(V_, v_tau)], inc=V_)
                        v_vl = p.add("vector", (lambda e, pr=pr, gt=gt: e.scalar_tensor_tensor(
                            vl_tm[:, gt * 128:(gt + 1) * 128], bank(pr, 128, 256),
                            sc_v, vh_tm[:, gt * 128:(gt + 1) * 128],
                            op0=OP.mult, op1=OP.subtract)), inc=V_)
                        v_last_split = v_vl
                        for ai, (dh, dl, scl) in enumerate(((qh_fm, ql_fm, sc_q),
                                                           (kh_fm, kl_fm, sc_k))):
                            th, tl = ttt[2 * ai], ttt[2 * ai + 1]
                            v_h = p.add("vector", (lambda e, pr=pr, ai=ai, th=th, scl=scl:
                                e.tensor_scalar(th[:, :], bank(pr, 128, ai * 128), scl, None,
                                                op0=OP.mult, op1=OP.bypass)),
                                waits=[(T_, t_tp_use.get(2 * ai, 0))], inc=V_)
                            v_l = p.add("vector", (lambda e, pr=pr, ai=ai, th=th, tl=tl, scl=scl:
                                e.scalar_tensor_tensor(tl[:, :], bank(pr, 128, ai * 128),
                                                       scl, th[:, :],
                                                       op0=OP.mult, op1=OP.subtract)),
                                waits=[(T_, t_tp_use.get(2 * ai + 1, 0))], inc=V_)
                            v_last_split = v_l
                            for j, (tsrc, dst, vdep) in enumerate(((th, dh, v_h), (tl, dl, v_l))):
                                tpb = 2 + (2 * ai + j)
                                t_tp = p.add("tensor", (lambda e, tpb=tpb, tsrc=tsrc:
                                    e.transpose(bank(tpb, 64).bitcast(BF16),
                                                tsrc[:, :], eye[:, :])),
                                    waits=[(V_, vdep),
                                           (S_, s_tp_evac.get(tpb, 0))], inc=T_)
                                t_tp_use[2 * ai + j] = t_tp
                                s_tp_evac[tpb] = p.add("scalar", (lambda e, tpb=tpb, dst=dst, gt=gt:
                                    e.copy(dst[:, gt * 128:(gt + 1) * 128],
                                           bank(tpb, 64).bitcast(BF16))),
                                    waits=[(T_, t_tp)], inc=S_)
                        v_split_bank[pr] = v_last_split
            s_prep, v_prep, t_prep = p.c(S_), p.c(V_), p.c(T_)


            s_exp_bank = {}
            v_pl_buf = {}
            t_p_use = {}
            v_dn_free = {}
            v_pv_free = {}
            dq_osb = {}
            t_dn = t_pv = 0
            for qt in range(16):
                nkt = 4 * qt + 4
                dnb = 3 + qt % 2
                pvb = 5 + qt % 2
                pend = []
                for kt in range(nkt):
                    bp = kt % 4
                    bsc = (0, 1, 2, 7)[kt % 4]
                    w0 = [(S_, s_exp_bank.get(bsc, 0))]
                    if s_exp_bank.get(bsc, 0) == 0:
                        w0 += [(V_, v_prep), (S_, s_prep)]
                    if kt == 0:
                        w0 += [(V_, v_pv_free.get(pvb, 0)), (V_, v_dn_free.get(dnb, 0))]
                    mm(bank(bsc), kh_fm[:, kt * 128:(kt + 1) * 128],
                       qh_fm[:, qt * 512:(qt + 1) * 512], True, False, waits=w0)
                    mm(bank(bsc), kh_fm[:, kt * 128:(kt + 1) * 128],
                       ql_fm[:, qt * 512:(qt + 1) * 512], False, False)
                    t_sc = mm(bank(bsc), kl_fm[:, kt * 128:(kt + 1) * 128],
                              qh_fm[:, qt * 512:(qt + 1) * 512], False, True, inc=T_)
                    if len(pend) >= 3:
                        pend.pop(0)()
                    s_exp = p.add("scalar", (lambda e, bp=bp, bsc=bsc: e.activation(
                        p32[bp][:, :], bank(bsc), AF.Exp)),
                        waits=[(T_, t_sc), (V_, v_pl_buf.get(bp, 0))], inc=S_)
                    s_exp_bank[bsc] = s_exp
                    diag = kt >= 4 * qt
                    guard_T = [(T_, t_p_use.get(bp, 0))]
                    if diag:
                        dj = kt - 4 * qt
                        v_mask = p.add("vector", (lambda e, bp=bp, dj=dj: e.tensor_tensor(
                            p32[bp][:, :], p32[bp][:, :],
                            masks[:, dj * 512:(dj + 1) * 512], op=OP.mult)),
                            waits=[(S_, s_exp)], inc=V_)
                        s_ph = p.add("scalar", (lambda e, bp=bp: e.copy(phb[bp][:, :], p32[bp][:, :])),
                                     waits=[(V_, v_mask)] + guard_T, inc=S_)
                        ph_dep = (S_, s_ph)
                    else:
                        v_ph = p.add("vector", (lambda e, bp=bp: e.tensor_scalar(
                            phb[bp][:, :], p32[bp][:, :], 1.0, None,
                            op0=OP.mult, op1=OP.bypass)),
                            waits=[(S_, s_exp)] + guard_T, inc=V_)
                        ph_dep = (V_, v_ph)
                    v_pl = p.add("vector", (lambda e, bp=bp: e.tensor_tensor(
                        plb[bp][:, :], p32[bp][:, :], phb[bp][:, :], op=OP.subtract)),
                        waits=[ph_dep], inc=V_)
                    v_pl_buf[bp] = v_pl

                    def emit_dnpv(kt=kt, bp=bp, ph_dep=ph_dep, v_pl=v_pl):
                        nonlocal t_dn, t_pv
                        mm(bank(dnb), ones128_bf[:, :], phb[bp][:, :], kt == 0, False,
                           waits=[ph_dep])
                        t_dn = mm(bank(dnb), ones128_bf[:, :], plb[bp][:, :], False,
                                  kt == nkt - 1, waits=[(V_, v_pl)],
                                  inc=T_ if kt == nkt - 1 else None)
                        mm(bank(pvb), vh_tm[:, kt * 128:(kt + 1) * 128], phb[bp][:, :],
                           kt == 0, False)
                        mm(bank(pvb), vh_tm[:, kt * 128:(kt + 1) * 128], plb[bp][:, :],
                           False, False)
                        t_pv = mm(bank(pvb), vl_tm[:, kt * 128:(kt + 1) * 128], phb[bp][:, :],
                                  False, kt == nkt - 1, inc=T_)
                        t_p_use[bp] = t_pv
                    pend.append(emit_dnpv)
                for f in pend:
                    f()
                pend = []
                qtp = qt % 2
                v_rc = p.add("vector", (lambda e, qtp=qtp, dnb=dnb: e.reciprocal(
                    rc_sb[qtp][:, :], bank(dnb))), waits=[(T_, t_dn)], inc=V_)
                v_dn_free[dnb] = v_rc
                v_rcn = p.add("vector", (lambda e, qtp=qtp: e.tensor_scalar(
                    rc_sb[qtp][:, :], rc_sb[qtp][:, :], nu_bc[:, :], None,
                    op0=OP.mult, op1=OP.bypass)), waits=[(V_, v_rc)], inc=V_)
                v_pv_evac = p.add("vector", (lambda e, qtp=qtp, pvb=pvb: e.scalar_tensor_tensor(
                    osb[qtp][:, :], bank(pvb), 1.0, rc_sb[qtp][:, :],
                    op0=OP.mult, op1=OP.mult)),
                    waits=[(V_, v_rcn), (T_, t_pv), (Dq, dq_osb.get(qtp, 0))], inc=V_)
                v_pv_free[pvb] = v_pv_evac
                p.add("sync", (lambda e, qtp=qtp, qt=qt, hl=hl: e.dma_start(
                    out=a2a_in[hl].ap()[128 * (qt // 2): 128 * (qt // 2) + 128,
                                        (qt % 2) * 512:(qt % 2 + 1) * 512],
                    in_=osb[qtp][:, :])), waits=[(V_, v_pv_evac)], inc=Dq, inc_n=16)
                dq_osb[qtp] = p.c(Dq)

        def phase4_a2a(hl):
            dq_all = p.c(Dq)

            def _a2a(e, hl=hl):
                with allow_ici_all_to_all():
                    return e.collective_compute(
                        "AllToAll", OP.bypass, replica_groups=[list(range(NCORE))],
                        ins=[a2a_in[hl].ap().opt()], outs=[a2a_out[hl].ap().opt()])
            p.add("gpsimd", _a2a, waits=[(Dq, dq_all)], inc=CC)
            cc_n = p.c(CC)
            a2v = a2a_out[hl].ap().rearrange("(i p) t -> p i t", p=128)
            for i in range(NCORE):
                p.add("gpsimd", (lambda e, i=i, hl=hl: e.dma_start(
                    out=xres[:, (2 * i + hl) * 1024:(2 * i + hl + 1) * 1024],
                    in_=a2v[:, i, :], accum_op=OP.add)),
                    waits=[(CC, cc_n)] if i == 0 else (), inc=Da, inc_n=16)

        init()
        p.barrier()
        layer_norm()
        p.barrier()
        ffn()
        p.barrier()
        if PHASES >= 2:
            layer_norm(load_first=True)
            phase2_ag()
            cc_need = (p.c(CC) - 1, p.c(CC))
            p.barrier()
        if PHASES >= 3:
            attn_head(0, cc_need)
            p.barrier()
            if PHASES >= 4:
                phase4_a2a(0)
            attn_head(1, cc_need)
            if PHASES >= 4:
                phase4_a2a(1)
            p.barrier()
        if PHASES >= 5:
            layer_norm(load_first=True)
            p.barrier()
            ffn()
            p.barrier()

        p.add("sync", lambda e: e.dma_start(out=xa[:, :], in_=xres[:, :]), inc=Dq, inc_n=16)
        dq_ld = p.c(Dq)
        p.add("sync", lambda e: e.dma_start(out=y_out[:, :], in_=xa[:, :]),
              waits=[(Dq, dq_ld)], inc=Dq, inc_n=16)
        dq_st = p.c(Dq)
        p.add("gpsimd", lambda e: e.nop(), waits=[(Dq, dq_st)])

        stats = p.emit(block, sems)
        print("emit stats:", {k: tuple(v) for k, v in stats.items()}, "matmuls:", nmm[0])
    return nc


def stage_inputs(inputs):
    x = np.asarray(inputs["L0_attn_x"])[0]
    wq = np.asarray(inputs["L0_attn_wq"])
    wk = np.asarray(inputs["L0_attn_wk"])
    wv = np.asarray(inputs["L0_attn_wv"])
    tau = np.asarray(inputs["L0_attn_tau"]).reshape(1, 1)
    nu = np.asarray(inputs["L0_attn_nu"]).reshape(1, 1)
    wg = np.asarray(inputs["L0_ffn_w_gate"])
    wu = np.asarray(inputs["L0_ffn_w_up"])
    wd = np.asarray(inputs["L0_ffn_w_down"])
    sg = np.asarray(inputs["L0_ffn_s_gate"])
    su = np.asarray(inputs["L0_ffn_s_up"])
    sd = np.asarray(inputs["L0_ffn_s_down"])

    bf = ml_dtypes.bfloat16
    wg_s = np.ascontiguousarray(
        wg.reshape(NFC, 128, 16, 512).transpose(2, 1, 0, 3).reshape(16, 128, NFC * 512))
    wu_s = np.ascontiguousarray(
        wu.reshape(NFC, 128, 16, 512).transpose(2, 1, 0, 3).reshape(16, 128, NFC * 512))
    wd_s = np.ascontiguousarray(
        wd.reshape(8, 8, 128, 16, 128).transpose(0, 3, 2, 1, 4).reshape(8, 16, 128, 8 * 128))
    sg_s = np.ascontiguousarray(sg.reshape(64, 128).T)
    su_s = np.ascontiguousarray(su.reshape(64, 128).T)
    sd_s = np.ascontiguousarray(sd.reshape(16, 128).T)
    mask = np.ascontiguousarray(
        np.triu(np.ones((512, 512), np.float32)).reshape(4, 128, 512)
        .transpose(1, 0, 2).reshape(128, 4 * 512)).astype(bf)
    eye = np.eye(128, dtype=np.float32).astype(bf)

    in_maps = []
    for c in range(NCORE):
        rows = slice(c * TOK, (c + 1) * TOK)
        x_slab = np.ascontiguousarray(
            x[rows].T.reshape(NFC, 128, TOK).transpose(1, 0, 2).reshape(128, NFC * TOK))
        hc = slice(c * H_PER * DH, (c + 1) * H_PER * DH)
        wq_s = np.ascontiguousarray(wq[hc, :].T.reshape(NFC, 128, H_PER * DH).transpose(1, 0, 2))
        wk_s = np.ascontiguousarray(wk[hc, :].T.reshape(NFC, 128, H_PER * DH).transpose(1, 0, 2))
        wv_s = np.ascontiguousarray(wv[hc, :].T.reshape(NFC, 128, H_PER * DH).transpose(1, 0, 2))
        in_maps.append({
            "x": x_slab, "wq": wq_s, "wk": wk_s, "wv": wv_s,
            "tau": tau, "nu": nu,
            "wg": wg_s, "wu": wu_s, "wd": wd_s,
            "sg": sg_s, "su": su_s, "sd": sd_s,
            "mask": mask, "eye": eye,
        })
    return in_maps


def unstage_output(results):
    outs = []
    for c in range(NCORE):
        slab = results[c]["y"]
        xt = slab.reshape(128, NFC, TOK).transpose(1, 0, 2).reshape(C, TOK)
        outs.append(np.ascontiguousarray(xt.T))
    return np.concatenate(outs, axis=0)[None]


def kernel(**inputs):
    in_maps = stage_inputs(inputs)
    nc = build_nc()
    res = run_bass_kernel_spmd(nc, in_maps, core_ids=list(range(NCORE)))
    return unstage_output(res.results)
